# version 09
# baseline (speedup 1.0000x reference)
import numpy as np
import ml_dtypes
from contextlib import ExitStack

import concourse.bass as bass
import concourse.mybir as mybir
from concourse.bass_utils import run_bass_kernel_spmd

F32 = mybir.dt.float32
BF16 = mybir.dt.bfloat16
AF = mybir.ActivationFunctionType
ALU = mybir.AluOpType
AX = mybir.AxisListType

T = 2048
D = 1024
NT = T // 128
ND = D // 128
NHEAD = 16
DH = 64
CHUNKS = tuple((i * 512, 512) for i in range(4))
CHUNKS4 = CHUNKS
LN_EPS = 1e-5
PERF_EPS = 1e-6
LNSQRT_M = float(np.log(np.sqrt(128.0)))

BF = ml_dtypes.bfloat16

_POOL_SPECS = [
    ("b_attn", 24), ("b_proj", 8),
    ("sb1_0", 32), ("sb1_1", 32),
    ("rb1_0", 32), ("rb1_1", 32), ("rb1_2", 32), ("rb1_3", 32),
    ("bq", 8), ("bk", 8), ("bo", 8),
    ("g1", 8), ("b1", 8), ("c1", 32), ("c2", 8), ("g2", 8), ("b2", 8),
    ("sb2_0", 8), ("sb2_1", 8), ("rb2_0", 8), ("rb2_1", 8), ("rb2_2", 8), ("rb2_3", 8),
]
POOL = {}
_c = 0
for _n, _k in _POOL_SPECS:
    POOL[_n] = _c
    _c += _k
NPOOL = _c

_ROW_SPECS = [("bv", 8), ("bav", 8), ("gate_b", 1), ("sb2_0", 8), ("sb2_1", 8),
              ("rb2_0", 8), ("rb2_1", 8), ("rb2_2", 8), ("rb2_3", 8)]
ROWS = {}
_c = 0
for _n, _k in _ROW_SPECS:
    ROWS[_n] = _c * 128
    _c += _k
NROW = _c * 128


class Edge:
    def __init__(self, ctx, nc, name):
        self.sem = ctx.enter_context(nc.semaphore(name))
        self.n = 0


class PB:

    ENG = {"t": "tensor", "v": "vector", "s": "scalar", "g": "gpsimd", "y": "sync"}

    def __init__(self, prog):
        self.prog = prog
        self.qs = {k: [] for k in self.ENG}

    def op(self, eng, fn, wait=(), inc=None, inc_k=1, drain=False):
        wv = []
        for w in wait:
            if isinstance(w, tuple):
                e, v = w
            else:
                e, v = w, w.n
            if v > 0:
                wv.append((e.sem, v))
        iv = None
        if inc is not None:
            inc.n += inc_k
            iv = (inc.sem, inc_k)

        def run(E, wv=wv, iv=iv, fn=fn, drain=drain):
            for s, v in wv:
                E.wait_ge(s, v)
            if drain:
                E.drain()
            inst = fn(E)
            if iv is not None:
                inst.then_inc(*iv)

        self.qs[eng].append(run)

    def emit(self):
        nc = self.prog.nc
        with nc.Block() as b:
            for k, attr in self.ENG.items():
                if self.qs[k]:
                    getattr(b, attr)(lambda E, fns=tuple(self.qs[k]): [f(E) for f in fns])


class Prog:
    def __init__(self, nc, ctx):
        self.nc = nc
        self.ctx = ctx
        self.ctx_root = ctx
        self._edges = {}
        self.d = None
        self.PSF = None
        self.PSB = None

    def edge(self, name):
        if name not in self._edges:
            self._edges[name] = Edge(self.ctx_root, self.nc, "E_" + name)
        return self._edges[name]

    def sb(self, name, shape, dtype):
        return self.ctx.enter_context(self.nc.sbuf_tensor(name, shape, dtype))

    def psum(self, name, shape, dtype=F32):
        return self.ctx.enter_context(self.nc.psum_tensor(name, shape, dtype))

    def alloc_std_psum(self, tag, nf=6, nb=2):
        self.PSF = [self.psum("psf_%s%d" % (tag, i), [128, 512], F32)
                    for i in range(nf)]
        self.PSB = [self.psum("psb_%s%d" % (tag, i), [128, 1024], BF16)
                    for i in range(nb)]

    def alloc_wide_psum(self, tag):
        self.PSW = [self.psum("psw_%s%d" % (tag, i), [128, 1024], F32)
                    for i in range(4)]
        self.psh2 = [self.PSW[0][:, 0:512], self.PSW[0][:, 512:1024],
                     self.PSW[1][:, 0:512], self.PSW[1][:, 512:1024]]
        self.psy2 = [self.PSW[2][:, 0:512], self.PSW[2][:, 512:1024],
                     self.PSW[3][:, 0:512], self.PSW[3][:, 512:1024]]
        self.PSF = [self.PSW[0][:, 0:512], self.PSW[0][:, 512:1024],
                    self.PSW[1][:, 0:512], self.PSW[1][:, 512:1024],
                    self.PSW[2][:, 0:512], self.PSW[2][:, 512:1024]]


class NS:
    pass


def sub_prog(p, ctx):
    pp = Prog.__new__(Prog)
    pp.nc = p.nc
    pp.ctx = ctx
    pp.ctx_root = p.ctx_root
    pp._edges = p._edges
    pp.PSF = None
    pp.PSB = None
    pp.d = p.d
    return pp


def emit_load_consts(p, pb, d):
    eL = p.edge("cst")
    for fn in [
        lambda E: E.dma_start(d.pool[:, :], d.d_pool[:, :]),
        lambda E: E.dma_start(d.identb[:, :], d.d_identb[:, :]),
        lambda E: E.dma_start(d.identf[:, :], d.d_identf[:, :]),
        lambda E: E.dma_start(d.e2[:, :], d.d_e2[:, :]),
        lambda E: E.dma_start(d.bcsel8[:, :], d.d_bcsel8[:, :]),
    ]:
        pb.op("g", fn, inc=eL, inc_k=16)
    pb.op("g", lambda E: E.memset(d.ones1[:, :], 1.0))
    pb.op("g", lambda E: E.memset(d.onescol[:, :], 1.0))
    pb.op("g", lambda E: E.memset(d.onescol_f[:, :], 1.0))
    pb.op("g", lambda E: E.memset(d.cst[:, 0:1], -LNSQRT_M))
    pb.op("g", lambda E: E.memset(d.cst[:, 1:2], PERF_EPS))
    pb.op("g", lambda E: E.memset(d.cst[:, 2:3], LN_EPS))


def emit_ln(p, pb, src, dst, gain=None, bias=None, name="ln", src_f32=False,
            prewait=()):
    d = p.d
    ones = d.onescol_f if src_f32 else d.onescol
    e_row = p.edge("ln_row")
    e_st = p.edge("ln_st")
    e_sq = p.edge("ln_sq")
    e_qt = p.edge("ln_qt")
    e_rs = p.edge("ln_rs")
    e_cs = p.edge("ln_cs")
    e_bc = p.edge("ln_bc")
    e_ap = p.edge("ln_ap")
    ps_rs = [p.PSF[0], p.PSF[3]]
    ps_As = [p.PSF[1], p.PSF[4]]
    ps_Bs = [p.PSF[2], p.PSF[5]]
    row_b = e_row.n
    ap_b = e_ap.n
    for ch in range(4):
        cs = ch * 512
        ps_r = ps_rs[ch % 2]
        ps_A = ps_As[ch % 2]
        ps_B = ps_Bs[ch % 2]
        for dt in range(ND):
            pb.op("t", lambda E, dt=dt, cs=cs, ps_r=ps_r: E.matmul(
                ps_r[0:1, 0:512], ones[:, :], src(dt)[:, cs:cs + 512],
                start=(dt == 0), stop=(dt == ND - 1)),
                wait=(list(prewait) if (ch == 0 and dt == 0) else []) +
                     ([(e_row, row_b + ch - 1)] if (dt == 0 and ch >= 2) else []),
                inc=e_st if dt == ND - 1 else None)
        for dt in range(ND):
            pb.op("s", lambda E, dt=dt, cs=cs: E.activation(
                d.ln_sq[:, (dt % 2) * 512:(dt % 2) * 512 + 512], src(dt)[:, cs:cs + 512],
                AF.Square),
                wait=(list(prewait) if (ch == 0 and dt == 0) else []) +
                     ([(e_qt, e_qt.n - 1)] if dt >= 2 else []), inc=e_sq)
            pb.op("t", lambda E, dt=dt, ps_r=ps_r: E.matmul(
                ps_r[32:33, 0:512], d.onescol[:, :],
                d.ln_sq[:, (dt % 2) * 512:(dt % 2) * 512 + 512],
                start=(dt == 0), stop=(dt == ND - 1)),
                wait=[(e_sq, e_sq.n)], inc=e_qt)
        def rowmath1(E, ps_r=ps_r):
            E.tensor_scalar(d.ln_r1[0:1, 0:512], ps_r[0:1, 0:512], 1.0 / D, None,
                            op0=ALU.mult)
            E.drain()
            return E.tensor_scalar(d.ln_r2[0:1, 0:512], ps_r[32:33, 0:512],
                                   1.0 / D, None, op0=ALU.mult)
        pb.op("v", rowmath1, wait=[(e_st, e_st.n), (e_qt, e_qt.n), (e_rs, e_rs.n)],
              drain=True, inc=e_row)

        def rowmath(E):
            E.tensor_tensor(d.ln_tmp[0:1, 0:512], d.ln_r1[0:1, 0:512],
                            d.ln_r1[0:1, 0:512], op=ALU.mult)
            E.drain()
            return E.tensor_tensor(d.ln_r2[0:1, 0:512], d.ln_r2[0:1, 0:512],
                                   d.ln_tmp[0:1, 0:512], op=ALU.subtract)
        e_rw2 = p.edge("ln_rw2")
        pb.op("v", rowmath, drain=True, inc=e_rw2)
        def rstd(E):
            E.activation(d.ln_r2[0:1, 0:512], d.ln_r2[0:1, 0:512], AF.Ln,
                         bias=d.cst[0:1, 2:3])
            E.drain()
            return E.activation(d.ln_rowAb[0:1, 0:512], d.ln_r2[0:1, 0:512], AF.Exp,
                                scale=-0.5)
        pb.op("s", rstd, wait=[(e_rw2, e_rw2.n), (e_bc, e_bc.n)], drain=True,
              inc=e_rs)
        pb.op("v", lambda E: E.scalar_tensor_tensor(
            d.ln_rowBb[0:1, 0:512], d.ln_r1[0:1, 0:512], -1.0,
            d.ln_rowAb[0:1, 0:512], op0=ALU.mult, op1=ALU.mult),
            wait=[(e_rs, e_rs.n)], drain=True, inc=e_cs)
        pb.op("t", lambda E, ps_A=ps_A: E.matmul(
            ps_A[:, 0:512], d.ones1[:, :], d.ln_rowAb[0:1, 0:512]),
            wait=[(e_cs, e_cs.n)] +
                 ([(e_ap, ap_b + ch - 1)] if ch >= 2 else []))
        pb.op("t", lambda E, ps_B=ps_B: E.matmul(
            ps_B[:, 0:512], d.ones1[:, :], d.ln_rowBb[0:1, 0:512]),
            inc=e_bc)
        for dt in range(ND):
            if gain is None:
                def apply(E, dt=dt, cs=cs, ps_A=ps_A, ps_B=ps_B):
                    E.drain()
                    E.tensor_tensor(d.ln_tmp[:, 0:512], src(dt)[:, cs:cs + 512],
                                    ps_A[:, 0:512], op=ALU.mult)
                    E.drain()
                    return E.tensor_tensor(dst(dt, cs // 512), d.ln_tmp[:, 0:512],
                                           ps_B[:, 0:512], op=ALU.add)
            else:
                def apply(E, dt=dt, cs=cs, gain=gain, bias=bias, ps_A=ps_A, ps_B=ps_B):
                    E.drain()
                    E.tensor_tensor(d.ln_tmp[:, 0:512], src(dt)[:, cs:cs + 512],
                                    ps_A[:, 0:512], op=ALU.mult)
                    E.drain()
                    E.tensor_tensor(d.ln_tmp[:, 0:512], d.ln_tmp[:, 0:512],
                                    ps_B[:, 0:512], op=ALU.add)
                    E.drain()
                    return E.tensor_scalar(dst(dt, cs // 512), d.ln_tmp[:, 0:512],
                                           d.pool[:, gain + dt:gain + dt + 1],
                                           d.pool[:, bias + dt:bias + dt + 1],
                                           op0=ALU.mult, op1=ALU.add)
            pb.op("v", apply, wait=[(e_bc, e_bc.n)],
                  inc=e_ap if dt == ND - 1 else None)


def emit_p1(p, pbs, stages="ABCDEF"):
    d = p.d

    pb = PB(p)
    eg = [p.edge("p1a_g0"), p.edge("p1a_g1")]
    et = p.edge("p1a_t")
    ev = p.edge("p1a_v")
    ev_b = ev.n
    trb = [p.PSB[0], p.PSB[1]]
    for t in range(NT):
        pb.op("g", lambda E, t=t: E.dma_start(
            d.xld[:, t % 2, :], d.d_x[t * 128:(t + 1) * 128, :]),
            wait=[(ev, ev_b + t - 1)] if t >= 2 else (), inc=eg[t % 2], inc_k=16)
        gt = eg[t % 2].n
        for dt in range(ND):
            pb.op("t", lambda E, t=t, dt=dt: E.transpose(
                trb[t % 2][:, dt * 128:(dt + 1) * 128],
                d.xld[:, t % 2, dt * 128:(dt + 1) * 128], d.identb[:, :]),
                wait=[(eg[t % 2], gt), (ev, ev_b + t - 1)] if dt == 0 else (),
                inc=et if dt == ND - 1 else None)
        pb.op("v", lambda E, t=t: E.tensor_copy(
            d.arA[:, :, t * 128:(t + 1) * 128],
            trb[t % 2].rearrange("p (d q) -> p d q", q=128)),
            wait=[(et, et.n)], inc=ev)
    pbs.append(pb)

    if "B" not in stages:
        return
    pb = PB(p)
    ew = p.edge("qkv_w")
    em = p.edge("qkv_m")
    ee = p.edge("qkv_e")
    ps_mm = [p.PSF[0], p.PSF[1]]
    erow = p.edge("qkv_row")
    ee_b = ee.n
    pb.op("g", lambda E: E.dma_start(
        d.rows_bv[:, :], d.d_rows[0:1, ROWS["bav"]:ROWS["bav"] + 1024]),
        inc=erow, inc_k=16)
    er_t = erow.n
    nmm = 0
    for part, dst_ar in ((0, d.qC), (1, d.kD), (2, None)):
        for kc in range(ND):
            pb.op("g", lambda E, kc=kc, part=part: E.dma_start(
                d.wW[:, kc, :], d.d_W_attn[kc * 128:(kc + 1) * 128,
                                           part * 1024:(part + 1) * 1024]),
                wait=[(em, em.n)] if kc == 0 else (), inc=ew, inc_k=16)
        wt = ew.n
        if part < 2:
            for m in range(ND):
                for ch in range(4):
                    pi = nmm % 2
                    for kc in range(ND):
                        w = [(ew, wt), (ee, ee_b + nmm - 1)] if kc == 0 else []
                        pb.op("t", lambda E, m=m, ch=ch, kc=kc, pi=pi: E.matmul(
                            ps_mm[pi][:, :], d.wW[:, kc, m * 128:(m + 1) * 128],
                            d.arA[:, kc, ch * 512:(ch + 1) * 512],
                            start=(kc == 0), stop=(kc == ND - 1)),
                            wait=w, inc=em if kc == ND - 1 else None)
                    bcol = POOL["b_attn"] + part * 8 + m
                    pb.op("s", lambda E, pi=pi, bcol=bcol, m=m, ch=ch, dst_ar=dst_ar:
                          E.activation(dst_ar[:, m, ch * 512:(ch + 1) * 512],
                                       ps_mm[pi][:, :], AF.Identity,
                                       bias=d.pool[:, bcol:bcol + 1]),
                          wait=[(em, em.n)], inc=ee)
                    nmm += 1
        else:
            for t in range(NT):
                for ch in range(2):
                    pi = nmm % 2
                    for kc in range(ND):
                        w = [(ew, wt), (ee, ee_b + nmm - 1)] if kc == 0 else []
                        pb.op("t", lambda E, t=t, ch=ch, kc=kc, pi=pi: E.matmul(
                            ps_mm[pi][:, :], d.arA[:, kc, t * 128:(t + 1) * 128],
                            d.wW[:, kc, ch * 512:(ch + 1) * 512],
                            start=(kc == 0), stop=False),
                            wait=w)
                    pb.op("t", lambda E, ch=ch, pi=pi: E.matmul(
                        ps_mm[pi][:, :], d.ones1[:, :],
                        d.rows_bv[0:1, ch * 512:ch * 512 + 512],
                        start=False, stop=True),
                        wait=[(erow, er_t)], inc=em)
                    pb.op("s", lambda E, t=t, ch=ch, pi=pi: E.activation(
                        d.vE[:, t, ch * 512:(ch + 1) * 512], ps_mm[pi][:, :], AF.Copy),
                        wait=[(em, em.n)], inc=ee)
                    nmm += 1
    pbs.append(pb)

    if "C" not in stages:
        return
    pb = PB(p)
    eo = p.edge("p1c_o")
    eob = p.edge("p1c_ob")
    e_sq = p.edge("p1c_sq")
    e_mm = p.edge("p1c_mm")
    e_bi = p.edge("p1c_bi")
    ps_row = p.PSF[2]
    e_mA = p.edge("p1c_mA")
    pb.op("g", lambda E: E.dma_start(d.maskA[:, :], d.d_maskA[:, :]), inc=e_mA, inc_k=16)
    mA_t = e_mA.n
    pb.op("g", lambda E: E.dma_start(d.om_f[0:64, :], d.d_omega[:, :]), inc=eo, inc_k=16)
    pb.op("g", lambda E: E.dma_start(d.om_f[64:128, :], d.d_omega[:, :]), inc=eo, inc_k=16)
    pb.op("v", lambda E: E.tensor_scalar(d.om[:, :], d.om_f[:, :], 0.125, None,
                                         op0=ALU.mult),
          wait=[(eo, eo.n)], inc=eob)
    u = 0
    for j, src_ar in ((0, d.qC), (1, d.kD)):
        for t in range(NT):
            for dt in range(ND):
                pb.op("s", lambda E, dt=dt, t=t, src_ar=src_ar, u=u: E.activation(
                    d.ln_sq[:, (u % 8) * 128:(u % 8) * 128 + 128],
                    src_ar[:, dt, t * 128:(t + 1) * 128], AF.Square),
                    wait=[(e_mm, u - 7)] if u >= 8 else (), inc=e_sq)
                pb.op("t", lambda E, dt=dt, u=u: E.matmul(
                    ps_row[:, dt * 2:dt * 2 + 2], d.ln_sq[:, (u % 8) * 128:(u % 8) * 128 + 128],
                    d.e2[:, :]),
                    wait=[(e_sq, e_sq.n), (e_bi, e_bi.n)], inc=e_mm)
                u += 1
            pb.op("s", lambda E, t=t, j=j: E.activation(
                d.nssqT[:, t, j * 16:j * 16 + 16], ps_row[:, 0:16], AF.Identity,
                bias=d.cst[:, 0:1]),
                wait=[(e_mm, e_mm.n)], inc=e_bi)
    pbs.append(pb)

    if "D" not in stages:
        return
    pb = PB(p)
    ph_t = p.edge("p1d_t")
    ph_s = p.edge("p1d_s")
    ph_tt = p.edge("p1d_tt")
    ph_ve = p.edge("p1d_ve")
    sc_tv = [p.edge("p1d_tv0"), p.edge("p1d_tv1")]
    sc_vt = [p.edge("p1d_vt0"), p.edge("p1d_vt1")]
    sc_tn = [p.edge("p1d_tn0"), p.edge("p1d_tn1")]
    sc_ts = [p.edge("p1d_ts0"), p.edge("p1d_ts1")]
    sc_st = [p.edge("p1d_st0"), p.edge("p1d_st1")]
    sc_yv = [p.edge("p1d_yv0"), p.edge("p1d_yv1")]
    sc_ytp = p.edge("p1d_ytp")
    sc_yte = p.edge("p1d_yte")
    ps_S = [p.PSF[0], p.PSF[3]]
    ps_N = [p.PSF[1], p.PSF[4]]
    ps_P = [p.PSF[2], p.PSF[5]]
    e_biC = p.edge("p1c_bi")
    biC_t = e_biC.n
    e_mAC = p.edge("p1c_mA")
    mAC_t = e_mAC.n
    for g in range(8):
        dt = g
        for par in range(2):
            h = 2 * g + par
            ho = par * 64
            for j, (src_ar, phisel, nof) in enumerate(
                    ((d.qC, d.phiQ, 0), (d.kD, d.phiK, 16))):
                for t in range(NT):
                    u = (par * 2 + j) * NT + t
                    wv = [(eob, 1), (ph_s, ph_s.n - 1)]
                    if g == 0 and u == 0:
                        wv.append((e_biC, biC_t))
                    pb.op("t", lambda E, t=t, dt=dt, ho=ho, src_ar=src_ar, u=u:
                          E.matmul(
                        ps_P[u % 2][:, ((u // 2) % 2) * 128:
                                    ((u // 2) % 2) * 128 + 128],
                        src_ar[ho:ho + 64, dt, t * 128:(t + 1) * 128],
                        d.om[ho:ho + 64, :]),
                        wait=wv, inc=ph_t)
                    pb.op("s", lambda E, t=t, h=h, u=u, par=par, phisel=phisel,
                          nof=nof: E.activation(
                        phisel[:, par, t * 128:(t + 1) * 128],
                        ps_P[u % 2][:, ((u // 2) % 2) * 128:
                                    ((u // 2) % 2) * 128 + 128],
                        AF.Exp, bias=d.nssqT[:, t, nof + h:nof + h + 1]),
                        wait=[(ph_t, ph_t.n)], inc=ph_s)
        allexp = ph_s.n
        for pj, (phisel, phiTsel, par) in enumerate(
                ((d.phiQ, d.phiQT, 0), (d.phiK, d.phiKT, 0),
                 (d.phiQ, d.phiQT, 1), (d.phiK, d.phiKT, 1))):
            for t in range(NT):
                u = pj * NT + t
                grp = u // 4
                pb.op("t", lambda E, t=t, u=u, par=par, phisel=phisel, grp=grp:
                      E.transpose(
                    p.PSB[grp % 2][:, (u % 4) * 128:(u % 4) * 128 + 128],
                    phisel[:, par, t * 128:(t + 1) * 128], d.identb[:, :]),
                    wait=[(ph_s, allexp), (ph_ve, ph_ve.n - 1),
                          (sc_yte, sc_yte.n)] if t % 4 == 0 else (),
                    inc=ph_tt if t % 4 == 3 else None)
                if t % 4 == 3:
                    pb.op("v", lambda E, t=t, par=par, phiTsel=phiTsel, grp=grp:
                          E.tensor_copy(
                        phiTsel[:, par, (t - 3) * 128:(t + 1) * 128],
                        p.PSB[grp % 2][:, 0:512]),
                        wait=[(ph_tt, ph_tt.n)], inc=ph_ve)
        pair_rdy = ph_ve.n
        for c in range(NT):
            for par in range(2):
                h = 2 * g + par
                wv = [(ph_ve, pair_rdy)] if c == 0 else [(sc_vt[par],
                                                          16 * g + c)]
                pb.op("t", lambda E, c=c, par=par: E.matmul(
                    ps_N[par][:, 128:256], d.phiKT[:, par, c * 128:(c + 1) * 128],
                    d.phiQT[:, par, c * 128:(c + 1) * 128]),
                    wait=wv, inc=sc_tv[par])
                wv = [(sc_tv[par], 16 * g + c + 1), (sc_tn[par], 16 * g + c)]
                if g == 0 and c == 0 and par == 0:
                    wv.append((e_mAC, mAC_t))
                pb.op("v", lambda E, par=par: E.tensor_tensor(
                    d.Am[:, par, :], ps_N[par][:, 128:256], d.maskA[:, :],
                    op=ALU.mult),
                    wait=wv, inc=sc_vt[par])
                pb.op("t", lambda E, c=c, h=h, par=par: E.matmul(
                    ps_N[par][:, 0:64], d.Am[:, par, :],
                    d.vE[:, c, h * 64:h * 64 + 64],
                    start=True, stop=False),
                    wait=[(sc_vt[par], 16 * g + c + 1),
                          (sc_yv[par], 16 * g + c)])
                if c > 0:
                    pb.op("t", lambda E, par=par: E.matmul(
                        ps_N[par][:, 64:65], d.Am[:, par, :], d.onescol[:, :],
                        start=False, stop=False))
                    pb.op("t", lambda E, c=c, par=par: E.matmul(
                        ps_N[par][:, 0:65],
                        d.phiQT[:, par, c * 128:(c + 1) * 128],
                        d.S_sb[:, par, :],
                        start=False, stop=True),
                        wait=[(sc_st[par], 16 * g + c)], inc=sc_tn[par])
                else:
                    pb.op("t", lambda E, par=par: E.matmul(
                        ps_N[par][:, 64:65], d.Am[:, par, :], d.onescol[:, :],
                        start=False, stop=True), inc=sc_tn[par])
                pb.op("t", lambda E, c=c, h=h, par=par: E.matmul(
                    ps_S[par][:, 0:64], d.phiK[:, par, c * 128:(c + 1) * 128],
                    d.vE[:, c, h * 64:h * 64 + 64],
                    start=True, stop=False),
                    wait=[(sc_st[par], 16 * g + c)] if c > 0 else ())
                pb.op("t", lambda E, c=c, par=par: E.matmul(
                    ps_S[par][:, 64:65], d.phiK[:, par, c * 128:(c + 1) * 128],
                    d.onescol[:, :],
                    start=False, stop=True), inc=sc_ts[par])

                def supd(E, c=c, par=par):
                    E.drain()
                    if c == 0:
                        E.tensor_copy(d.S_acc[:, par, :], ps_S[par][:, 0:65])
                    else:
                        E.tensor_tensor(d.S_acc[:, par, :], d.S_acc[:, par, :],
                                        ps_S[par][:, 0:65], op=ALU.add)
                    E.drain()
                    return E.tensor_copy(d.S_sb[:, par, :], d.S_acc[:, par, :])
                pb.op("v", supd, wait=[(sc_ts[par], 16 * g + c + 1)],
                      inc=sc_st[par])

                def yev(E, c=c, par=par):
                    E.tensor_scalar(d.den[:, 4 * par:4 * par + 1],
                                    ps_N[par][:, 64:65], d.cst[:, 1:2], None,
                                    op0=ALU.add)
                    E.drain()
                    E.reciprocal(d.den[:, 4 * par + 1:4 * par + 2],
                                 d.den[:, 4 * par:4 * par + 1])
                    E.drain()
                    return E.tensor_scalar(
                        d.yF2[:, c, par * 64:par * 64 + 64],
                        ps_N[par][:, 0:64],
                        d.den[:, 4 * par + 1:4 * par + 2], None, op0=ALU.mult)
                pb.op("v", yev, wait=[(sc_tn[par], 16 * g + c + 1)],
                      inc=sc_yv[par])
        ps_Y = p.PSB[1]
        for r in range(2):
            for tt in range(8):
                t = r * 8 + tt
                pb.op("t", lambda E, t=t, tt=tt: E.transpose(
                    ps_Y[:, tt * 128:(tt + 1) * 128],
                    d.yF2[:, t, 0:128], d.identb[:, :]),
                    wait=[(sc_yv[0], 16 * (g + 1)), (sc_yv[1], 16 * (g + 1)),
                          (sc_yte, sc_yte.n)] if tt == 0 else (),
                    inc=sc_ytp if tt == 7 else None)
            pb.op("v", lambda E, r=r, dt=dt: E.tensor_copy(
                d.kD[:, dt, r * 1024:(r + 1) * 1024], ps_Y[:, 0:1024]),
                wait=[(sc_ytp, sc_ytp.n)], inc=sc_yte)
    pbs.append(pb)

    if "E" not in stages:
        return
    pb = PB(p)
    ew = p.edge("proj_w")
    em = p.edge("proj_m")
    ee = p.edge("proj_e")
    ea = p.edge("proj_a")
    ps_mm = [p.PSF[0], p.PSF[1]]
    yT = d.kD
    ee_b = ee.n
    ea_b = ea.n
    for kc in range(ND):
        pb.op("g", lambda E, kc=kc: E.dma_start(
            d.wW[:, kc, :], d.d_W_proj[kc * 128:(kc + 1) * 128, :]),
            inc=ew, inc_k=16)
    wt = ew.n
    nmm = 0
    for m in range(ND):
        for ch in range(4):
            pi = nmm % 2
            for kc in range(ND):
                w = [(ew, wt), (ee, ee_b + nmm - 1)] if kc == 0 else []
                pb.op("t", lambda E, m=m, ch=ch, kc=kc, pi=pi: E.matmul(
                    ps_mm[pi][:, :], d.wW[:, kc, m * 128:(m + 1) * 128],
                    yT[:, kc, ch * 512:(ch + 1) * 512],
                    start=(kc == 0), stop=(kc == ND - 1)),
                    wait=w, inc=em if kc == ND - 1 else None)
            bcol = POOL["b_proj"] + m
            pb.op("s", lambda E, pi=pi, bcol=bcol: E.activation(
                d.stmp[:, pi, :], ps_mm[pi][:, :], AF.Identity,
                bias=d.pool[:, bcol:bcol + 1]),
                wait=[(em, em.n), (ea, ea_b + nmm - 1)], inc=ee)
            pb.op("v", lambda E, m=m, ch=ch, pi=pi: E.tensor_tensor(
                d.arB[:, m, ch * 512:(ch + 1) * 512], d.stmp[:, pi, :],
                d.arA[:, m, ch * 512:(ch + 1) * 512], op=ALU.add),
                wait=[(ee, ee.n)], inc=ea)
            nmm += 1
    pbs.append(pb)

    if "F" not in stages:
        return
    pb = PB(p)
    emit_ln(p, pb,
            src=lambda dt: d.arB[:, dt, :],
            dst=lambda dt, ch: d.arA[:, dt, ch * 512:(ch + 1) * 512],
            name="p1ln")
    pbs.append(pb)



def alloc_p2(p, d):
    d.accum = p.sb("accum", [128, ND, T], F32)
    d.w1_sb = p.sb("w1_sb", [128, ND, 2048], BF16)
    d.w2_sb = p.sb("w2_sb", [128, 16, 1024], BF16)
    d.gateb = p.sb("gateb", [128, T], BF16)
    d.h_sb = p.sb("h_sb", [128, 16, 512], BF16)
    d.gbrow = p.sb("gbrow", [1, 4], BF16)
    d.gw_sb = p.sb("gw_sb", [128, ND, 4], BF16)
    d.wsel = p.sb("wsel", [128, NT, 4], F32)
    d.wselT = p.sb("wselT", [4, T], BF16)
    d.gtmp = p.sb("gtmp", [128, 8], F32)


def emit_p2(p, pbs, stages="GEL"):
    d = p.d
    pb = PB(p)
    eg = p.edge("gt_g")
    em = p.edge("gt_m")
    ev = p.edge("gt_v")
    es = p.edge("gt_s")
    etp = p.edge("gt_tp")
    ete = p.edge("gt_te")
    ps_l = p.PSF[5]
    ps_t = p.PSF[4]
    pb.op("g", lambda E: E.dma_start(
        d.gbrow[:, :], d.d_rows[0:1, ROWS["gate_b"]:ROWS["gate_b"] + 4]),
        inc=eg, inc_k=16)
    pb.op("g", lambda E: E.dma_start(
        d.gw_sb[:, :, :], d.d_gate_w.rearrange("(kc p) e -> p kc e", p=128)),
        inc=eg, inc_k=16)
    eg_t = eg.n
    ei = p.edge("fx_init")
    for dt in range(ND):
        pb.op("v", lambda E, dt=dt: E.tensor_copy(d.accum[:, dt, :], d.arA[:, dt, :]),
              inc=ei)
    for t in range(NT):
        for kc in range(ND):
            pb.op("t", lambda E, t=t, kc=kc: E.matmul(
                ps_l[:, (t % 2) * 4:(t % 2) * 4 + 4],
                d.arA[:, kc, t * 128:(t + 1) * 128], d.gw_sb[:, kc, :],
                start=(kc == 0), stop=False),
                wait=[(eg_t_dummy if False else eg, eg_t), (es, es.n)] if kc == 0 else ())
        pb.op("t", lambda E, t=t: E.matmul(
            ps_l[:, (t % 2) * 4:(t % 2) * 4 + 4], d.ones1[:, :],
            d.gbrow[0:1, 0:4],
            start=False, stop=True), inc=em)

        def soft(E, t=t):
            q = ps_l[:, (t % 2) * 4:(t % 2) * 4 + 4]
            E.reduce_max(d.gtmp[:, 0:1], q, axis=AX.X)
            E.drain()
            return E.tensor_scalar(d.gtmp[:, 1:2], d.gtmp[:, 0:1], -1.0, None,
                                   op0=ALU.mult)
        pb.op("v", soft, wait=[(em, em.n)], drain=True, inc=ev)
        pb.op("s", lambda E, t=t: E.activation(
            d.wsel[:, t, :], ps_l[:, (t % 2) * 4:(t % 2) * 4 + 4],
            AF.Exp, bias=d.gtmp[:, 1:2], accum_out=d.gtmp[:, 2:3]),
            wait=[(ev, ev.n)], drain=True, inc=es)

        def fin(E, t=t):
            E.reciprocal(d.gtmp[:, 3:4], d.gtmp[:, 2:3])
            E.drain()
            E.tensor_scalar(d.wsel[:, t, :], d.wsel[:, t, :], d.gtmp[:, 3:4], None,
                            op0=ALU.mult)
            E.drain()
            E.reduce_max(d.gtmp[:, 4:5], d.wsel[:, t, :], axis=AX.X)
            E.drain()
            E.tensor_scalar(d.ln_tmp[:, 0:4], d.wsel[:, t, :], d.gtmp[:, 4:5], None,
                            op0=ALU.is_ge)
            E.drain()
            E.scalar_tensor_tensor(d.ln_tmp[:, 4:8], d.ln_tmp[:, 0:4], -2.0,
                                   d.wsel[:, t, :], op0=ALU.mult, op1=ALU.add)
            E.drain()
            E.reduce_max(d.gtmp[:, 6:7], d.ln_tmp[:, 4:8], axis=AX.X)
            E.drain()
            E.tensor_scalar(d.ln_tmp[:, 0:4], d.wsel[:, t, :], d.gtmp[:, 6:7], None,
                            op0=ALU.is_ge)
            E.drain()
            return E.tensor_tensor(d.wsel[:, t, :], d.wsel[:, t, :], d.ln_tmp[:, 0:4],
                                   op=ALU.mult)
        pb.op("v", fin, wait=[(es, es.n)], drain=True, inc=ev)
    for t in range(NT):
        pb.op("t", lambda E, t=t: E.transpose(
            ps_t[0:4, (t % 4) * 128:(t % 4) * 128 + 128],
            d.wsel[:, t, :], d.identf[:, :]),
            wait=[(ev, ev.n), (ete, ete.n)] if t % 4 == 0 else (),
            inc=etp if t % 4 == 3 else None)
        if t % 4 == 3:
            pb.op("s", lambda E, t=t: E.activation(
                d.wselT[0:4, (t - 3) * 128:(t + 1) * 128], ps_t[0:4, 0:512], AF.Copy),
                wait=[(etp, etp.n)], inc=ete)
    pbs.append(pb)

    if hasattr(d, "d_dbg_wsel"):
        dpb = PB(p)
        eD = p.edge("p2dbg1")
        ev_g = p.edge("gt_v")
        dpb.op("y", lambda E: E.dma_start(d.d_dbg_wsel[:, :, :], d.wsel[:, :, :]),
               wait=[(ev_g, ev_g.n)], inc=eD, inc_k=16)
        dpb.op("y", lambda E: E.nop(), wait=[(eD, eD.n)])
        pbs.append(dpb)

    if "E" not in stages:
        return
    pb = PB(p)
    e_w1a = p.edge("fx_w1a")
    e_w1b = p.edge("fx_w1b")
    e_w2 = p.edge("fx_w2")
    e_gb = p.edge("fx_gb")
    e_gm = p.edge("fx_gm")
    e_hm = p.edge("fx_hm")
    e_he = p.edge("fx_he")
    e_ym = p.edge("fx_ym")
    e_ye = p.edge("fx_ye")
    ps_h = p.psh2
    ps_y = p.psy2
    experts = [
        ("s", 0, "d_sw1", "d_sw2", "sb1_0", "sb2_0"),
        ("s", 1, "d_sw1", "d_sw2", "sb1_1", "sb2_1"),
        ("r", 0, "d_rw1", "d_rw2", "rb1_0", "rb2_0"),
        ("r", 1, "d_rw1", "d_rw2", "rb1_1", "rb2_1"),
        ("r", 2, "d_rw1", "d_rw2", "rb1_2", "rb2_2"),
        ("r", 3, "d_rw1", "d_rw2", "rb1_3", "rb2_3"),
    ]
    gyu = 0
    ye_b = e_ye.n
    hm_a = 0
    ete_g = p.edge("gt_te")
    first_h = True
    for ei_, (kind, xi, w1n, w2n, b1n, b2n) in enumerate(experts):
        w1_d = getattr(d, w1n)
        w2_d = getattr(d, w2n)
        if kind == "s":
            pb.op("g", lambda E: E.memset(d.gateb[:, :], 0.5),
                  wait=[(e_ye, e_ye.n)], inc=e_gb)
        else:
            for c4 in range(4):
                pb.op("t", lambda E, c4=c4, xi=xi: E.matmul(
                    ps_y[c4][:, 0:512], d.bcsel8[0:4, xi * 128:xi * 128 + 128],
                    d.wselT[0:4, c4 * 512:(c4 + 1) * 512]),
                    wait=[(ete_g, ete_g.n), (e_ye, e_ye.n)] if c4 == 0 else (),
                    inc=e_gm)
            for c4 in range(4):
                pb.op("v", lambda E, c4=c4: E.tensor_copy(
                    d.gateb[:, c4 * 512:(c4 + 1) * 512], ps_y[c4][:, 0:512]),
                    wait=[(e_gm, e_gm.n)] if c4 == 0 else (), inc=e_gb)
        gb_t = e_gb.n
        for fh in range(2):
            for half, e_wh, hm_tgt in ((0, e_w1a, hm_a), (1, e_w1b, e_hm.n)):
                for kc in range(ND):
                    pb.op("g", lambda E, kc=kc, fh=fh, half=half, w1_d=w1_d, xi=xi:
                          E.dma_start(
                        d.w1_sb[:, kc, half * 1024:(half + 1) * 1024],
                        w1_d[xi, kc * 128:(kc + 1) * 128,
                             fh * 2048 + half * 1024:fh * 2048 + (half + 1) * 1024]),
                        wait=[(e_hm, hm_tgt)] if kc == 0 else (),
                        inc=e_wh, inc_k=16)
            for fc in range(16):
                pb.op("g", lambda E, fc=fc, fh=fh, w2_d=w2_d, xi=xi: E.dma_start(
                    d.w2_sb[:, fc, :],
                    w2_d[xi, fh * 2048 + fc * 128:fh * 2048 + (fc + 1) * 128, :]),
                    wait=[(e_ym, e_ym.n)] if fc == 0 else (),
                    inc=e_w2, inc_k=16)
            wt_a = e_w1a.n
            wt_b = e_w1b.n
            wt_2 = e_w2.n
            for ci, (cs, W) in enumerate(CHUNKS):
                for ft in range(16):
                    for kc in range(ND):
                        wv = ()
                        if kc == 0:
                            wv = [(e_w1a if ft < 8 else e_w1b,
                                   wt_a if ft < 8 else wt_b),
                                  (e_he, e_he.n - 3)]
                            if first_h:
                                wv.append((ete_g, 4))
                                first_h = False
                        pb.op("t", lambda E, ft=ft, kc=kc, cs=cs, W=W: E.matmul(
                            ps_h[ft % 4][:, 0:W],
                            d.w1_sb[:, kc, ft * 128:(ft + 1) * 128],
                            d.arA[:, kc, cs:cs + W],
                            start=(kc == 0), stop=(kc == ND - 1)),
                            wait=wv, inc=e_hm if kc == ND - 1 else None)
                    if ci == 3 and ft == 7:
                        hm_a = e_hm.n
                    bcol = POOL[b1n] + fh * 16 + ft
                    wv = [(e_hm, e_hm.n)]
                    if ft == 0:
                        wv.append((e_ym, e_ym.n))
                    pb.op("s", lambda E, ft=ft, bcol=bcol, W=W: E.activation(
                        d.h_sb[:, ft, 0:W], ps_h[ft % 4][:, 0:W], AF.Relu,
                        bias=d.pool[:, bcol:bcol + 1]),
                        wait=wv, inc=e_he)
                for ot in range(8):
                    wv = [(e_ye, ye_b + gyu - 3)]
                    if gyu == 0:
                        wv.append((ete_g, 4))
                    if ot == 0:
                        wv.append((e_he, e_he.n))
                        wv.append((e_w2, wt_2))
                        wv.append((e_gb, gb_t))
                    for ft in range(16):
                        last = (ft == 15)
                        pb.op("t", lambda E, ft=ft, ot=ot, last=last, W=W: E.matmul(
                            ps_y[ot % 4][:, 0:W],
                            d.w2_sb[:, ft, ot * 128:(ot + 1) * 128],
                            d.h_sb[:, ft, 0:W],
                            start=(ft == 0), stop=last),
                            wait=wv if ft == 0 else (),
                            inc=e_ym if last else None)
                    b2c = POOL[b2n] + ot

                    def yev(E, ot=ot, cs=cs, b2c=b2c, fh=fh, W=W):
                        r = None
                        for pc, w in (((0, 512), (512, W - 512)) if W > 512
                                      else ((0, 512),)):
                            c0 = cs + pc
                            if fh == 0:
                                E.scalar_tensor_tensor(
                                    d.ln_tmp[:, 0:w], ps_y[ot % 4][:, pc:pc + w],
                                    d.pool[:, b2c:b2c + 1], d.gateb[:, c0:c0 + w],
                                    op0=ALU.add, op1=ALU.mult)
                            else:
                                E.tensor_tensor(d.ln_tmp[:, 0:w],
                                                ps_y[ot % 4][:, pc:pc + w],
                                                d.gateb[:, c0:c0 + w], op=ALU.mult)
                            E.drain()
                            r = E.tensor_tensor(
                                d.accum[:, ot, c0:c0 + w],
                                d.accum[:, ot, c0:c0 + w],
                                d.ln_tmp[:, 0:w], op=ALU.add)
                            if pc == 0 and W > 512:
                                E.drain()
                        return r
                    pb.op("v", yev,
                          wait=[(e_ym, e_ym.n),
                                (p.edge("fx_init"), p.edge("fx_init").n)],
                          drain=True, inc=e_ye)
                    gyu += 1
    pbs.append(pb)

    if hasattr(d, "d_dbg_accum"):
        dpb = PB(p)
        eD = p.edge("p2dbg2")
        for dt in range(ND):
            dpb.op("y", lambda E, dt=dt: E.dma_start(
                d.d_dbg_accum[:, dt, :], d.accum[:, dt, :]),
                wait=[(e_ye, e_ye.n)] if dt == 0 else (), inc=eD, inc_k=16)
        dpb.op("y", lambda E: E.nop(), wait=[(eD, eD.n)])
        pbs.append(dpb)

    if "L" not in stages:
        return
    pb = PB(p)
    emit_ln(p, pb,
            src=lambda dt: d.accum[:, dt, :],
            dst=lambda dt, ch: d.arA[:, dt, ch * 512:(ch + 1) * 512],
            name="p2ln", src_f32=True,
            prewait=[(e_ye, e_ye.n)])
    pbs.append(pb)


def alloc_p3_outer(p, d):
    d.maskC = p.sb("maskC_sb", [128, 3584], BF16)
    d.aT = p.sb("aT", [128, ND, T], BF16)
    d.p_sb = p.sb("p_sb", [128, 3, 768], BF16)
    d.pt_sb = p.sb("pt_sb", [128, 2, 768], BF16)
    d.mx = p.sb("mx", [128, 8], F32)
    d.rows_bv3 = p.sb("rows_bv3", [1, 1024], BF16)


def alloc_p3_attn(p, d):
    d.wQ = p.sb("wQ", [128, ND, 1024], BF16)
    d.qT3 = p.sb("qT3", [128, ND, T], BF16)
    d.kT3 = p.sb("kT3", [128, ND, T], BF16)
    d.vE3 = p.sb("vE3", [128, NT, D], BF16)


def alloc_p3_s1(p, d):
    d.wO = p.sb("wO", [128, ND, 1024], BF16)
    d.s1 = p.sb("s1", [128, ND, T], BF16)
    d.stmp3 = p.sb("stmp3", [128, 2, 512], F32)


def alloc_p3_ffn(p, d):
    d.accum = p.sb("accum3", [128, ND, T], F32)
    d.chalf = p.sb("chalf", [128, 32], F32)
    d.w1_sb = p.sb("w1_sb3", [128, ND, 1024], BF16)
    d.w2_sb = p.sb("w2_sb3", [128, 8, 1024], BF16)
    d.h_sb = p.sb("h_sb3", [128, 8, 512], BF16)
    d.z_sb = p.sb("z_sb3", [128, 2, 512], F32)
    d.z2_sb = p.sb("z2_sb3", [128, 2, 512], BF16)
    d.u_sb = p.sb("u_sb3", [128, 512], F32)


def emit_p3_attn(p, pbs):
    d = p.d
    pb = PB(p)
    ew = p.edge("p3b_w")
    em = p.edge("p3b_m")
    ee = p.edge("p3b_e")
    erow = p.edge("p3b_row")
    eM = p.edge("p3b_mask")
    ps_mm = [p.ps3_sc[:, 0:512], p.ps3_pt[:, 0:512]]
    pb.op("g", lambda E: E.dma_start(d.maskC[:, :], d.d_maskC[:, :]), inc=eM, inc_k=16)
    pb.op("g", lambda E: E.dma_start(
        d.rows_bv3[:, :], d.d_rows[0:1, ROWS["bv"]:ROWS["bv"] + 1024]),
        inc=erow, inc_k=16)
    nmm = 0
    specs = [("d_Wq", d.qT3, POOL["bq"]), ("d_Wk", d.kT3, POOL["bk"]),
             ("d_Wv", None, None)]
    for part, (wname, dst_ar, bpool) in enumerate(specs):
        w_d = getattr(d, wname)
        for kc in range(ND):
            pb.op("g", lambda E, kc=kc, w_d=w_d: E.dma_start(
                d.wQ[:, kc, :], w_d[kc * 128:(kc + 1) * 128, :]),
                wait=[(em, part * 32)] if kc == 0 else (), inc=ew, inc_k=16)
        if part < 2:
            for m in range(ND):
                for ch in range(4):
                    pi = nmm % 2
                    for kc in range(ND):
                        w = [(ew, (part + 1) * ND * 16), (ee, nmm - 1)] if kc == 0 else []
                        pb.op("t", lambda E, m=m, ch=ch, kc=kc, pi=pi: E.matmul(
                            ps_mm[pi], d.wQ[:, kc, m * 128:(m + 1) * 128],
                            d.arA[:, kc, ch * 512:(ch + 1) * 512],
                            start=(kc == 0), stop=(kc == ND - 1)),
                            wait=w, inc=em if kc == ND - 1 else None)
                    pb.op("s", lambda E, pi=pi, m=m, ch=ch, dst_ar=dst_ar, bpool=bpool:
                          E.activation(dst_ar[:, m, ch * 512:(ch + 1) * 512],
                                       ps_mm[pi], AF.Identity,
                                       bias=d.pool[:, bpool + m:bpool + m + 1]),
                          wait=[(em, em.n)], inc=ee)
                    nmm += 1
        else:
            for t in range(NT):
                for ch in range(2):
                    pi = nmm % 2
                    for kc in range(ND):
                        w = [(ew, 3 * ND * 16), (ee, nmm - 1)] if kc == 0 else []
                        pb.op("t", lambda E, t=t, ch=ch, kc=kc, pi=pi: E.matmul(
                            ps_mm[pi], d.arA[:, kc, t * 128:(t + 1) * 128],
                            d.wQ[:, kc, ch * 512:(ch + 1) * 512],
                            start=(kc == 0), stop=False),
                            wait=w)
                    pb.op("t", lambda E, ch=ch, pi=pi: E.matmul(
                        ps_mm[pi], d.ones1[:, :],
                        d.rows_bv3[0:1, ch * 512:ch * 512 + 512],
                        start=False, stop=True),
                        wait=[(erow, 16)], inc=em)
                    pb.op("s", lambda E, t=t, ch=ch, pi=pi: E.activation(
                        d.vE3[:, t, ch * 512:(ch + 1) * 512], ps_mm[pi], AF.Copy),
                        wait=[(em, em.n)], inc=ee)
                    nmm += 1
    pbs.append(pb)

    pb = PB(p)
    e_sc = p.edge("p3c_sc")
    e_ex = p.edge("p3c_ex")
    e_mk = p.edge("p3c_mk")
    e_rc = p.edge("p3c_rc")
    e_ptm = p.edge("p3c_ptm")
    e_pte = p.edge("p3c_pte")
    e_pv = p.edge("p3c_pv")
    e_ae = p.edge("p3c_ae")
    e_qkv3 = p.edge("p3b_e")
    qkv_done = e_qkv3.n
    NB = 8
    units = []
    for h in range(NHEAD):
        for n in range(NB):
            for qs in range(2):
                units.append((h, n, qs))
    NHU = len(units)

    def u_geom(k):
        h, n, qs = units[k]
        dt = h // 2
        ho = (h % 2) * 64
        kbs = [kb for kb in (n - 1, n, n + 1) if 0 <= kb < NB]
        if n == 0:
            moff = 1536 + qs * 512
        elif n == NB - 1:
            moff = 2560 + qs * 512
        else:
            moff = qs * 768
        return h, n, qs, dt, ho, kbs, moff

    def st_scores(k):
        h, n, qs, dt, ho, kbs, moff = u_geom(k)
        base = kbs[0] * 256
        W = 256 * len(kbs)
        segs = [(0, 512)] + ([(512, 256)] if W == 768 else [])
        for si, (c0, w) in enumerate(segs):
            wv = []
            if si == 0:
                wv = [(e_ex, k - 1), (e_qkv3, qkv_done if k == 0 else 0)]
            pb.op("t", lambda E, k=k, c0=c0, w=w, dt=dt, ho=ho, n=n, qs=qs,
                  base=base: E.matmul(
                p.ps3_scb[k % 2][:, c0:c0 + w],
                d.qT3[ho:ho + 64, dt, (n * 2 + qs) * 128:(n * 2 + qs) * 128 + 128],
                d.kT3[ho:ho + 64, dt, base + c0:base + c0 + w]),
                wait=wv, inc=e_sc if si == len(segs) - 1 else None)

    def st_exp(k):
        W = 256 * len(u_geom(k)[5])
        pb.op("s", lambda E, k=k, W=W: E.activation(
            d.p_sb[:, k % 3, 0:W], p.ps3_scb[k % 2][:, 0:W], AF.Exp, scale=0.125),
            wait=[(e_sc, k + 1), (e_ptm, k - 2)], inc=e_ex)

    def st_masknorm(k):
        _, _, _, _, _, kbs, moff = u_geom(k)
        W = 256 * len(kbs)
        sl = k % 3
        pb.op("v", lambda E, W=W, moff=moff, sl=sl: E.scalar_tensor_tensor(
            d.p_sb[:, sl, 0:W], d.p_sb[:, sl, 0:W], 1.0,
            d.maskC[:, moff:moff + W], op0=ALU.mult, op1=ALU.mult,
            accum_out=d.mx[:, sl:sl + 1]),
            wait=[(e_ex, k + 1)] + ([(p.edge("p3b_mask"), p.edge("p3b_mask").n)]
                                    if k == 0 else []),
            inc=e_mk)

        def vnorm(E, W=W, sl=sl):
            E.reciprocal(d.mx[:, 4 + sl:5 + sl], d.mx[:, sl:sl + 1])
            E.drain()
            return E.tensor_scalar(d.p_sb[:, sl, 0:W], d.p_sb[:, sl, 0:W],
                                   d.mx[:, 4 + sl:5 + sl], None, op0=ALU.mult)
        pb.op("v", vnorm, wait=[(e_mk, k + 1)], drain=True, inc=e_rc)

    def st_transp(k):
        nk2 = 2 * len(u_geom(k)[5])
        for j2 in range(nk2):
            pb.op("t", lambda E, k=k, j2=j2: E.transpose(
                p.ps3_ptb[k % 2][:, j2 * 128:j2 * 128 + 128],
                d.p_sb[:, k % 3, j2 * 128:j2 * 128 + 128], d.identb[:, :]),
                wait=[(e_rc, k + 1), (e_pte, k - 1)] if j2 == 0 else (),
                inc=e_ptm if j2 == nk2 - 1 else None)

    def st_evacpt(k):
        nk2 = 2 * len(u_geom(k)[5])
        pb.op("s", lambda E, k=k, nk2=nk2: E.activation(
            d.pt_sb[:, k % 2, 0:nk2 * 128], p.ps3_ptb[k % 2][:, 0:nk2 * 128],
            AF.Copy),
            wait=[(e_ptm, k + 1), (e_pv, k - 1)], inc=e_pte)

    def st_pv(k):
        h, n, qs, dt, ho, kbs, _ = u_geom(k)
        nk2 = 2 * len(kbs)
        for j2 in range(nk2):
            kt = kbs[j2 // 2] * 2 + (j2 % 2)
            pb.op("t", lambda E, k=k, j2=j2, kt=kt, h=h, ho=ho, nk2=nk2: E.matmul(
                p.ps3_pvb[k % 2][ho:ho + 64, 0:128],
                d.vE3[:, kt, h * 64:h * 64 + 64],
                d.pt_sb[:, k % 2, j2 * 128:j2 * 128 + 128],
                start=(j2 == 0), stop=(j2 == nk2 - 1)),
                wait=[(e_pte, k + 1), (e_ae, k - 1)] if j2 == 0 else (),
                inc=e_pv if j2 == nk2 - 1 else None)

    def st_evaca(k):
        h, n, qs, dt, ho, _, _ = u_geom(k)
        pb.op("s", lambda E, k=k, h=h, n=n, qs=qs, dt=dt, ho=ho: E.activation(
            d.aT[ho:ho + 64, dt, (n * 2 + qs) * 128:(n * 2 + qs) * 128 + 128],
            p.ps3_pvb[k % 2][ho:ho + 64, 0:128], AF.Copy),
            wait=[(e_pv, k + 1)], inc=e_ae)

    stages = [st_scores, st_exp, st_masknorm, st_transp, st_evacpt, st_pv,
              st_evaca]
    for it in range(NHU + len(stages) - 1):
        for si, st in enumerate(stages):
            k = it - si
            if 0 <= k < NHU:
                st(k)
    pbs.append(pb)


def emit_p3_s1(p, pbs):
    d = p.d
    pb = PB(p)
    ew = p.edge("p3d_w")
    em = p.edge("p3d_m")
    ee = p.edge("p3d_e")
    ea = p.edge("p3d_a")
    ps_mm = [p.PSF[0], p.PSF[1]]
    for kc in range(ND):
        pb.op("g", lambda E, kc=kc: E.dma_start(
            d.wO[:, kc, :], d.d_Wo[kc * 128:(kc + 1) * 128, :]),
            inc=ew, inc_k=16)
    nmm = 0
    eaT = p.edge("p3c_ae")
    for m in range(ND):
        for ch in range(4):
            pi = nmm % 2
            for kc in range(ND):
                w = [(ew, ND * 16), (ee, nmm - 1)] if kc == 0 else []
                if nmm == 0 and kc == 0:
                    w.append((eaT, eaT.n))
                pb.op("t", lambda E, m=m, ch=ch, kc=kc, pi=pi: E.matmul(
                    ps_mm[pi][:, :], d.wO[:, kc, m * 128:(m + 1) * 128],
                    d.aT[:, kc, ch * 512:(ch + 1) * 512],
                    start=(kc == 0), stop=(kc == ND - 1)),
                    wait=w, inc=em if kc == ND - 1 else None)
            bcol = POOL["bo"] + m
            pb.op("s", lambda E, pi=pi, bcol=bcol: E.activation(
                d.stmp3[:, pi, :], ps_mm[pi][:, :], AF.Identity,
                bias=d.pool[:, bcol:bcol + 1]),
                wait=[(em, em.n), (ea, nmm - 1)], inc=ee)
            pb.op("v", lambda E, m=m, ch=ch, pi=pi: E.tensor_tensor(
                d.s1[:, m, ch * 512:(ch + 1) * 512], d.stmp3[:, pi, :],
                d.arA[:, m, ch * 512:(ch + 1) * 512], op=ALU.add),
                wait=[(ee, ee.n)], drain=True, inc=ea)
            nmm += 1
    pbs.append(pb)
    pb = PB(p)
    emit_ln(p, pb,
            src=lambda dt: d.s1[:, dt, :],
            dst=lambda dt, ch: d.aT[:, dt, ch * 512:(ch + 1) * 512],
            gain=POOL["g1"], bias=POOL["b1"], name="p3ln1",
            prewait=[(p.edge("p3d_a"), p.edge("p3d_a").n)])
    pbs.append(pb)


def emit_p3_ffn(p, pbs):
    d = p.d
    pb = PB(p)
    e_w = p.edge("p3f_w")
    e_hm = p.edge("p3f_hm")
    e_ze = p.edge("p3f_ze")
    e_gl = p.edge("p3f_gl")
    e_th = p.edge("p3f_th")
    e_ym = p.edge("p3f_ym")
    e_ye = p.edge("p3f_ye")
    ei = p.edge("p3f_init")
    ps_h = p.psh2
    ps_y = p.psy2
    ech = p.edge("p3f_ch")
    pb.op("v", lambda E: E.tensor_scalar(
        d.chalf[:, :], d.pool[:, POOL["c1"]:POOL["c1"] + 32], 0.5, None, op0=ALU.mult),
        inc=ech)
    for dt in range(ND):
        pb.op("v", lambda E, dt=dt: E.tensor_copy(d.accum[:, dt, :], d.aT[:, dt, :]),
              wait=[(p.edge("ln_ap"), p.edge("ln_ap").n)] if dt == 0 else (),
              inc=ei)
    gyu = 0
    yb4 = e_ye.n
    for q4 in range(4):
        for kc in range(ND):
            pb.op("g", lambda E, kc=kc, q4=q4: E.dma_start(
                d.w1_sb[:, kc, :],
                d.d_W1[kc * 128:(kc + 1) * 128, q4 * 1024:(q4 + 1) * 1024]),
                wait=[(e_ym, e_ym.n)] if kc == 0 else (), inc=e_w, inc_k=16)
        for fc in range(8):
            pb.op("g", lambda E, fc=fc, q4=q4: E.dma_start(
                d.w2_sb[:, fc, :],
                d.d_W2[q4 * 1024 + fc * 128:q4 * 1024 + (fc + 1) * 128, :]),
                inc=e_w, inc_k=16)
        wtarget = e_w.n
        for tp in range(4):
            cs = tp * 512
            for ft in range(8):
                for kc in range(ND):
                    wv2 = []
                    if kc == 0:
                        wv2 = [(e_w, wtarget), (e_ze, e_ze.n - 3)]
                        if q4 == 0 and tp == 0 and ft == 0:
                            wv2.append((p.edge("ln_ap"),
                                        p.edge("ln_ap").n))
                    pb.op("t", lambda E, ft=ft, kc=kc, cs=cs: E.matmul(
                        ps_h[ft % 4][:, 0:512],
                        d.w1_sb[:, kc, ft * 128:(ft + 1) * 128],
                        d.aT[:, kc, cs:cs + 512],
                        start=(kc == 0), stop=(kc == ND - 1)),
                        wait=wv2,
                        inc=e_hm if kc == ND - 1 else None)
                bcol = POOL["c1"] + q4 * 8 + ft
                wv = [(e_hm, e_hm.n), (e_gl, e_gl.n - 2)]
                if ft == 0:
                    wv.append((e_ym, e_ym.n))
                pb.op("s", lambda E, ft=ft, bcol=bcol: E.activation(
                    d.z_sb[:, ft % 2, :], ps_h[ft % 4][:, 0:512], AF.Identity,
                    scale=0.5, bias=d.chalf[:, bcol - POOL["c1"]:bcol - POOL["c1"] + 1]),
                    wait=wv + [(ech, 1)])
                pb.op("s", lambda E, ft=ft, bcol=bcol: E.activation(
                    d.z2_sb[:, ft % 2, :], ps_h[ft % 4][:, 0:512], AF.Square,
                    scale=0.5, bias=d.chalf[:, bcol - POOL["c1"]:bcol - POOL["c1"] + 1]),
                    inc=e_ze)

                def gel1(E, ft=ft):
                    E.tensor_scalar(d.u_sb[:, :], d.z2_sb[:, ft % 2, :],
                                    0.2854192435, 1.5957691216,
                                    op0=ALU.mult, op1=ALU.add)
                    E.drain()
                    return E.tensor_tensor(d.u_sb[:, :], d.u_sb[:, :],
                                           d.z_sb[:, ft % 2, :], op=ALU.mult)
                pb.op("v", gel1, wait=[(e_ze, e_ze.n)], drain=True, inc=e_gl)
                pb.op("s", lambda E: E.activation(d.u_sb[:, :], d.u_sb[:, :], AF.Tanh),
                      wait=[(e_gl, e_gl.n)], drain=True, inc=e_th)
                pb.op("v", lambda E, ft=ft: E.scalar_tensor_tensor(
                    d.h_sb[:, ft, :], d.u_sb[:, :], 1.0, d.z_sb[:, ft % 2, :],
                    op0=ALU.add, op1=ALU.mult),
                    wait=[(e_th, e_th.n)], drain=True, inc=e_gl)
            for ot in range(8):
                wv = [(e_ye, yb4 + gyu - 3)]
                if ot == 0:
                    wv.append((e_gl, e_gl.n))
                for ft in range(8):
                    last = (ft == 7)
                    pb.op("t", lambda E, ft=ft, ot=ot, last=last: E.matmul(
                        ps_y[ot % 4][:, 0:512],
                        d.w2_sb[:, ft, ot * 128:(ot + 1) * 128],
                        d.h_sb[:, ft, :],
                        start=(ft == 0), stop=last),
                        wait=wv if ft == 0 else (),
                        inc=e_ym if last else None)

                def yev(E, ot=ot, cs=cs, q4=q4):
                    E.drain()
                    if q4 == 0:
                        c2c = POOL["c2"] + ot
                        return E.scalar_tensor_tensor(
                            d.accum[:, ot, cs:cs + 512], ps_y[ot % 4][:, 0:512],
                            d.pool[:, c2c:c2c + 1], d.accum[:, ot, cs:cs + 512],
                            op0=ALU.add, op1=ALU.add)
                    return E.tensor_tensor(
                        d.accum[:, ot, cs:cs + 512], d.accum[:, ot, cs:cs + 512],
                        ps_y[ot % 4][:, 0:512], op=ALU.add)
                pb.op("v", yev, wait=[(e_ym, e_ym.n), (ei, ND)],
                      drain=True, inc=e_ye)
                gyu += 1
    pbs.append(pb)
    pb = PB(p)
    emit_ln(p, pb,
            src=lambda dt: d.accum[:, dt, :],
            dst=lambda dt, ch: d.aT[:, dt, ch * 512:(ch + 1) * 512],
            gain=POOL["g2"], bias=POOL["b2"], name="p3ln2", src_f32=True,
            prewait=[(p.edge("p3f_ye"), p.edge("p3f_ye").n)])
    pbs.append(pb)
    pb = PB(p)
    e2 = p.edge("p3_s2")
    for dt in range(ND):
        pb.op("v", lambda E, dt=dt: E.tensor_tensor(
            d.accum[:, dt, :], d.arA[:, dt, :], d.aT[:, dt, :], op=ALU.add),
            wait=[(p.edge("ln_ap"), p.edge("ln_ap").n)] if dt == 0 else (),
            drain=True, inc=e2)
    emit_ln(p, pb,
            src=lambda dt: d.accum[:, dt, :],
            dst=lambda dt, ch: d.arA[:, dt, ch * 512:(ch + 1) * 512],
            name="p3ln3", src_f32=True,
            prewait=[(e2, ND)])
    pbs.append(pb)


def alloc_p4(p, d):
    d.accum = p.sb("accum4", [128, ND, T], F32)
    d.w1_sb = p.sb("w1_sb4", [128, ND, 2048], BF16)
    d.w2_sb = p.sb("w2_sb4", [128, 16, 1024], BF16)
    d.gateb = p.sb("gateb4", [128, 512], BF16)
    d.h_sb = p.sb("h_sb4", [128, 16, 512], BF16)
    d.cw_sb = p.sb("cw_sb", [128, ND, 8], BF16)
    d.wsel4 = p.sb("wsel4", [128, NT, 8], F32)
    d.probsT = p.sb("probsT", [8, T], BF16)
    d.wT8 = p.sb("wT8", [8, T], BF16)
    d.bs = p.sb("bs", [8, 8], F32)
    d.gtmp4 = p.sb("gtmp4", [128, 8], F32)


def emit_p4(p, pbs):
    d = p.d
    pb = PB(p)
    eg = p.edge("gt_g")
    em = p.edge("gt_m")
    ev = p.edge("gt_v")
    es = p.edge("gt_s")
    etp = p.edge("gt_tp")
    ete = p.edge("gt_te")
    ebi = p.edge("gt_bi")
    ps_l = p.PSF[5]
    ps_t = p.PSF[4]
    pb.op("g", lambda E: E.dma_start(
        d.cw_sb[:, :, :], d.d_choice_wT.rearrange("(kc p) e -> p kc e", p=128)),
        inc=eg, inc_k=16)
    eg_t = eg.n
    ebsi = p.edge("p4_bsi")
    pb.op("g", lambda E: E.memset(d.bs[:, 0:1], 0.0))
    pb.op("g", lambda E: E.memset(d.bs[:, 1:2], 1.0), inc=ebsi)
    ei = p.edge("fx_init")
    for dt in range(ND):
        pb.op("v", lambda E, dt=dt: E.tensor_copy(d.accum[:, dt, :], d.arA[:, dt, :]),
              inc=ei)
    for t in range(NT):
        for kc in range(ND):
            pb.op("t", lambda E, t=t, kc=kc: E.matmul(
                ps_l[:, (t % 2) * 8:(t % 2) * 8 + 8],
                d.arA[:, kc, t * 128:(t + 1) * 128], d.cw_sb[:, kc, :],
                start=(kc == 0), stop=(kc == ND - 1)),
                wait=[(eg, eg_t), (es, es.n)] if kc == 0 else (),
                inc=em if kc == ND - 1 else None)

        def soft(E, t=t):
            q = ps_l[:, (t % 2) * 8:(t % 2) * 8 + 8]
            E.reduce_max(d.gtmp4[:, 0:1], q, axis=AX.X)
            E.drain()
            return E.tensor_scalar(d.gtmp4[:, 1:2], d.gtmp4[:, 0:1], -1.0, None,
                                   op0=ALU.mult)
        pb.op("v", soft, wait=[(em, em.n)], drain=True, inc=ev)
        pb.op("s", lambda E, t=t: E.activation(
            d.wsel4[:, t, :], ps_l[:, (t % 2) * 8:(t % 2) * 8 + 8],
            AF.Exp, bias=d.gtmp4[:, 1:2], accum_out=d.gtmp4[:, 2:3]),
            wait=[(ev, ev.n)], drain=True, inc=es)

        def fin(E, t=t):
            E.reciprocal(d.gtmp4[:, 3:4], d.gtmp4[:, 2:3])
            E.drain()
            return E.tensor_scalar(d.wsel4[:, t, :], d.wsel4[:, t, :],
                                   d.gtmp4[:, 3:4], None, op0=ALU.mult)
        pb.op("v", fin, wait=[(es, es.n)], drain=True, inc=ev)
    for t in range(NT):
        pb.op("t", lambda E, t=t: E.transpose(
            ps_t[0:8, (t % 4) * 128:(t % 4) * 128 + 128],
            d.wsel4[:, t, :], d.identf[:, :]),
            wait=[(ev, ev.n), (ete, ete.n)] if t % 4 == 0 else (),
            inc=etp if t % 4 == 3 else None)
        if t % 4 == 3:
            pb.op("s", lambda E, t=t: E.activation(
                d.probsT[0:8, (t - 3) * 128:(t + 1) * 128], ps_t[0:8, 0:512], AF.Copy),
                wait=[(etp, etp.n)], inc=ete)
    def bisect(E):
        for it in range(20):
            E.drain()
            E.tensor_tensor(d.bs[:, 2:3], d.bs[:, 0:1], d.bs[:, 1:2], op=ALU.add)
            E.drain()
            E.tensor_scalar(d.bs[:, 3:4], d.bs[:, 2:3], 0.5, None, op0=ALU.mult)
            E.drain()
            E.tensor_scalar(d.wT8[0:8, :], d.probsT[0:8, :], d.bs[:, 3:4][0:8, :],
                            0.0, op0=ALU.is_ge, op1=ALU.add,
                            accum_out=d.bs[:, 4:5])
            E.drain()
            E.tensor_scalar(d.bs[:, 5:6], d.bs[:, 4:5], 512.0, None, op0=ALU.is_ge)
            E.drain()
            E.tensor_tensor(d.bs[:, 6:7], d.bs[:, 3:4], d.bs[:, 0:1], op=ALU.subtract)
            E.drain()
            E.tensor_tensor(d.bs[:, 6:7], d.bs[:, 6:7], d.bs[:, 5:6], op=ALU.mult)
            E.drain()
            E.tensor_tensor(d.bs[:, 0:1], d.bs[:, 0:1], d.bs[:, 6:7], op=ALU.add)
            E.drain()
            E.tensor_scalar(d.bs[:, 7:8], d.bs[:, 5:6], -1.0, 1.0, op0=ALU.mult,
                            op1=ALU.add)
            E.drain()
            E.tensor_tensor(d.bs[:, 6:7], d.bs[:, 3:4], d.bs[:, 1:2], op=ALU.subtract)
            E.drain()
            E.tensor_tensor(d.bs[:, 6:7], d.bs[:, 6:7], d.bs[:, 7:8], op=ALU.mult)
            E.drain()
            E.tensor_tensor(d.bs[:, 1:2], d.bs[:, 1:2], d.bs[:, 6:7], op=ALU.add)
        E.drain()
        E.tensor_scalar(d.wT8[0:8, :], d.probsT[0:8, :], d.bs[:, 0:1][0:8, :],
                        None, op0=ALU.is_ge)
        E.drain()
        return E.tensor_tensor(d.wT8[0:8, :], d.probsT[0:8, :], d.wT8[0:8, :],
                               op=ALU.mult)
    pb.op("v", bisect, wait=[(ete, ete.n), (eg, eg.n), (ebsi, 1)], drain=True,
          inc=ebi)
    pbs.append(pb)

    pb = PB(p)
    e_w1a = p.edge("fx_w1a")
    e_w1b = p.edge("fx_w1b")
    e_w2 = p.edge("fx_w2")
    e_gb = p.edge("fx_gb")
    e_gm = p.edge("fx_gm")
    e_hm = p.edge("fx_hm")
    e_he = p.edge("fx_he")
    e_ym = p.edge("fx_ym")
    e_ye = p.edge("fx_ye")
    ps_h = p.psh2
    ps_y = p.psy2
    gyu = 0
    ye_b = e_ye.n
    hm_a = e_hm.n
    for ex in range(8):
        for half, e_wh, hm_tgt in ((0, e_w1a, hm_a), (1, e_w1b, e_hm.n)):
            for kc in range(ND):
                pb.op("g", lambda E, kc=kc, half=half, ex=ex: E.dma_start(
                    d.w1_sb[:, kc, half * 1024:(half + 1) * 1024],
                    d.d_fw1[ex, kc * 128:(kc + 1) * 128,
                            half * 1024:(half + 1) * 1024]),
                    wait=[(e_hm, hm_tgt)] if kc == 0 else (),
                    inc=e_wh, inc_k=16)
        for fc in range(16):
            pb.op("g", lambda E, fc=fc, ex=ex: E.dma_start(
                d.w2_sb[:, fc, :], d.d_fw2[ex, fc * 128:(fc + 1) * 128, :]),
                wait=[(e_ym, e_ym.n)] if fc == 0 else (),
                inc=e_w2, inc_k=16)
        wt_a = e_w1a.n
        wt_b = e_w1b.n
        wt_2 = e_w2.n
        for ci, (cs, W) in enumerate(CHUNKS4):
            pb.op("t", lambda E, ex=ex, cs=cs, W=W: E.matmul(
                ps_y[0][:, 0:W], d.bcsel8[0:8, ex * 128:ex * 128 + 128],
                d.wT8[0:8, cs:cs + W]),
                wait=[(ebi, 1), (e_ye, ye_b + gyu - 3)], inc=e_gm)
            pb.op("v", lambda E, W=W: E.tensor_copy(
                d.gateb[:, 0:W], ps_y[0][:, 0:W]),
                wait=[(e_gm, e_gm.n)], inc=e_gb)
            gb_t = e_gb.n
            for ft in range(16):
                for kc in range(ND):
                    pb.op("t", lambda E, ft=ft, kc=kc, cs=cs, W=W: E.matmul(
                        ps_h[ft % 4][:, 0:W],
                        d.w1_sb[:, kc, ft * 128:(ft + 1) * 128],
                        d.arA[:, kc, cs:cs + W],
                        start=(kc == 0), stop=(kc == ND - 1)),
                        wait=([(e_w1a if ft < 8 else e_w1b,
                                wt_a if ft < 8 else wt_b),
                               (e_he, e_he.n - 3)] if kc == 0 else ()),
                        inc=e_hm if kc == ND - 1 else None)
                if ci == 3 and ft == 7:
                    hm_a = e_hm.n
                wv = [(e_hm, e_hm.n)]
                if ft == 0:
                    wv.append((e_ym, e_ym.n))
                pb.op("s", lambda E, ft=ft, W=W: E.activation(
                    d.h_sb[:, ft, 0:W], ps_h[ft % 4][:, 0:W], AF.Silu),
                    wait=wv, inc=e_he)
            for ot in range(8):
                wv = [(e_ye, ye_b + gyu - 3)]
                if ot == 0:
                    wv.append((e_he, e_he.n))
                    wv.append((e_w2, wt_2))
                    wv.append((e_gb, gb_t))
                for ft in range(16):
                    last = (ft == 15)
                    pb.op("t", lambda E, ft=ft, ot=ot, last=last, W=W: E.matmul(
                        ps_y[ot % 4][:, 0:W],
                        d.w2_sb[:, ft, ot * 128:(ot + 1) * 128],
                        d.h_sb[:, ft, 0:W],
                        start=(ft == 0), stop=last),
                        wait=wv if ft == 0 else (),
                        inc=e_ym if last else None)

                def yev(E, ot=ot, cs=cs, W=W):
                    r = None
                    for pc, w in (((0, 512), (512, W - 512)) if W > 512
                                  else ((0, 512),)):
                        E.tensor_tensor(d.ln_tmp[:, 0:w],
                                        ps_y[ot % 4][:, pc:pc + w],
                                        d.gateb[:, pc:pc + w], op=ALU.mult)
                        E.drain()
                        r = E.tensor_tensor(
                            d.accum[:, ot, cs + pc:cs + pc + w],
                            d.accum[:, ot, cs + pc:cs + pc + w],
                            d.ln_tmp[:, 0:w], op=ALU.add)
                        if pc == 0 and W > 512:
                            E.drain()
                    return r
                pb.op("v", yev, wait=[(e_ym, e_ym.n),
                                      (p.edge("fx_init"), p.edge("fx_init").n)],
                      drain=True, inc=e_ye)
                gyu += 1
    pbs.append(pb)

    pb = PB(p)
    emit_ln(p, pb,
            src=lambda dt: d.accum[:, dt, :],
            dst=lambda dt, ch: d.accum[:, dt, ch * 512:(ch + 1) * 512],
            name="p4ln", src_f32=True,
            prewait=[(p.edge("fx_ye"), p.edge("fx_ye").n)])
    pbs.append(pb)
    pb = PB(p)
    e_tp = p.edge("p4o_tp")
    e_te = p.edge("p4o_te")
    e_st = [p.edge("p4o_st0"), p.edge("p4o_st1")]
    eap = p.edge("ln_ap")
    ps_o = [p.PSF[0], p.PSF[1]]
    g = 0
    for t in range(NT):
        for half in range(2):
            for dj in range(4):
                dt = half * 4 + dj
                pb.op("t", lambda E, t=t, dt=dt, dj=dj, g=g: E.transpose(
                    ps_o[g % 2][:, dj * 128:dj * 128 + 128],
                    d.accum[:, dt, t * 128:(t + 1) * 128], d.identf[:, :]),
                    wait=([(eap, eap.n)] if g == 0 else []) +
                         ([(e_te, e_te.n - 1)] if dj == 0 else []),
                    inc=e_tp if dj == 3 else None)
            pb.op("s", lambda E, t=t, half=half, g=g: E.activation(
                d.accum[:, half * 4:half * 4 + 4, t * 128:(t + 1) * 128],
                ps_o[g % 2][:, 0:512].rearrange("p (a b) -> p a b", b=128), AF.Copy),
                wait=[(e_tp, e_tp.n)], inc=e_te)
            g += 1
        pb.op("y", lambda E, t=t: E.dma_start(
            d.d_out[t * 128:(t + 1) * 128, :],
            d.accum[:, :, t * 128:(t + 1) * 128]),
            wait=[(e_te, e_te.n)], inc=e_st[t % 2], inc_k=16)
    pb.op("y", lambda E: E.nop(), wait=[(e_st[0], e_st[0].n), (e_st[1], e_st[1].n)])
    pbs.append(pb)

def alloc_common(p, io):
    d = NS()
    d.__dict__.update(io)
    d.arA = p.sb("arA", [128, ND, T], BF16)
    d.pool = p.sb("pool", [128, NPOOL], F32)
    d.identb = p.sb("identb", [128, 128], BF16)
    d.identf = p.sb("identf", [128, 128], F32)
    d.e2 = p.sb("e2", [128, 2], BF16)
    d.ones1 = p.sb("ones1", [1, 128], BF16)
    d.onescol = p.sb("onescol", [128, 1], BF16)
    d.onescol_f = p.sb("onescol_f", [128, 1], F32)
    d.cst = p.sb("cst", [128, 4], F32)
    d.bcsel8 = p.sb("bcsel8_sb", [8, 8 * 128], BF16)
    d.ln_rowAb = p.sb("ln_rowAb", [1, 512], BF16)
    d.ln_rowBb = p.sb("ln_rowBb", [1, 512], BF16)
    d.ln_r1 = p.sb("ln_r1", [1, 512], F32)
    d.ln_r2 = p.sb("ln_r2", [1, 512], F32)
    d.ln_sq = p.sb("ln_sq", [128, 1024], BF16)
    d.ln_tmp = p.sb("ln_tmp", [128, 512], F32)
    p.d = d
    return d


def alloc_p1(p, d):
    d.maskA = p.sb("maskA", [128, 128], BF16)
    d.qC = p.sb("qC", [128, ND, T], BF16)
    d.kD = p.sb("kD", [128, ND, T], BF16)
    d.arB = d.qC
    d.vE = p.sb("vE", [128, NT, D], BF16)
    d.yF2 = p.sb("yF2", [128, NT, 128], BF16)
    d.xld = p.sb("xld", [128, 2, D], BF16)
    d.rows_bv = p.sb("rows_bv", [1, 1024], BF16)
    d.wW = p.sb("wW0", [128, ND, 1024], BF16)
    d.om_f = p.sb("om_f", [128, 128], F32)
    d.om = p.sb("om", [128, 128], BF16)
    d.nssqT = p.sb("nssqT", [128, NT, 32], F32)
    d.phiQ = p.sb("phiQ", [128, 2, T], BF16)
    d.phiK = p.sb("phiK", [128, 2, T], BF16)
    d.phiQT = p.sb("phiQT", [128, 2, T], BF16)
    d.phiKT = p.sb("phiKT", [128, 2, T], BF16)
    d.S_sb = p.sb("S_sb", [128, 2, 65], BF16)
    d.S_acc = p.sb("S_acc", [128, 2, 65], F32)
    d.Am = p.sb("Am", [128, 2, 128], BF16)
    d.den = p.sb("den", [128, 8], F32)
    d.stmp = p.sb("stmp", [128, 2, 512], BF16)


def host_constants():
    identb = np.eye(128, dtype=BF)
    identf = np.eye(128, dtype=np.float32)
    maskA = (np.arange(128)[None, :] >= np.arange(128)[:, None]).astype(BF)
    i0 = np.arange(128)[:, None]
    j = np.arange(256)[None, :]
    mL0 = (j >= i0).astype(BF)
    mL1 = (j >= i0 + 128).astype(BF)
    mR0 = (j <= i0).astype(BF)
    mR1 = (j <= i0 + 128).astype(BF)
    maskP = np.ascontiguousarray(np.concatenate([mL0, mL1, mR0, mR1], axis=1))
    e2 = np.zeros((128, 2), dtype=BF)
    e2[0:64, 0] = BF(-0.0078125)
    e2[64:128, 1] = BF(-0.0078125)
    bcsel = np.zeros((4, 512), dtype=BF)
    for e in range(4):
        bcsel[e, e * 128:(e + 1) * 128] = BF(1.0)
    i = np.arange(128)[:, None]
    j = np.arange(256)[None, :]
    ones = np.ones((128, 256), dtype=np.float32)
    mC = np.zeros((128, 3584), dtype=np.float32)
    for qs in range(2):
        L = (j >= i + qs * 128).astype(np.float32)
        R = (j <= i + qs * 128).astype(np.float32)
        mC[:, qs * 768:qs * 768 + 768] = np.concatenate([L, ones, R], axis=1)
        mC[:, 1536 + qs * 512:1536 + qs * 512 + 512] = np.concatenate([ones, R], axis=1)
        mC[:, 2560 + qs * 512:2560 + qs * 512 + 512] = np.concatenate([L, ones], axis=1)
    bcsel8 = np.zeros((8, 1024), dtype=BF)
    for e in range(8):
        bcsel8[e, e * 128:(e + 1) * 128] = BF(1.0)
    return dict(identb=identb, identf=identf, maskA=maskA, maskP=maskP, e2=e2,
                bcsel=bcsel, maskC=mC.astype(BF), bcsel8=bcsel8)


def host_pool(inp):
    pool = np.zeros((NPOOL, 128), dtype=np.float32)

    def put(name, arr):
        arr = np.asarray(arr).reshape(-1)
        k = arr.shape[0] // 128
        pool[POOL[name]:POOL[name] + k, :] = arr.reshape(k, 128)

    put("b_attn", inp["L0_attn_b_attn"])
    put("b_proj", inp["L0_attn_b_proj"])
    put("sb1_0", inp["L0_ffn_sb1"][0]); put("sb1_1", inp["L0_ffn_sb1"][1])
    for e in range(4):
        put("rb1_%d" % e, inp["L0_ffn_rb1"][e])
    put("bq", inp["L1_attn_bq"]); put("bk", inp["L1_attn_bk"])
    put("bo", inp["L1_attn_bo"])
    put("g1", inp["L1_attn_g1"]); put("b1", inp["L1_attn_b1"])
    put("c1", inp["L1_attn_c1"]); put("c2", inp["L1_attn_c2"])
    put("g2", inp["L1_attn_g2"]); put("b2", inp["L1_attn_b2"])
    put("sb2_0", inp["L0_ffn_sb2"][0]); put("sb2_1", inp["L0_ffn_sb2"][1])
    for e in range(4):
        put("rb2_%d" % e, inp["L0_ffn_rb2"][e])
    pool_dev = np.ascontiguousarray(pool.T)

    rows = np.zeros((1, NROW), dtype=np.float32)

    def putr(name, arr):
        arr = np.asarray(arr).reshape(-1)
        rows[0, ROWS[name]:ROWS[name] + arr.shape[0]] = arr

    putr("bv", inp["L1_attn_bv"])
    putr("bav", inp["L0_attn_b_attn"][2048:3072])
    putr("gate_b", inp["L0_ffn_gate_b"])
    putr("sb2_0", inp["L0_ffn_sb2"][0]); putr("sb2_1", inp["L0_ffn_sb2"][1])
    for e in range(4):
        putr("rb2_%d" % e, inp["L0_ffn_rb2"][e])
    return pool_dev, rows


def const_inmap(inp):
    cst = host_constants()
    pool_dev, rows = host_pool(inp)
    return {
        "pool_in": pool_dev, "rows_in": rows,
        "identb_in": cst["identb"], "identf_in": cst["identf"],
        "maskA_in": cst["maskA"], "maskP_in": cst["maskP"], "e2_in": cst["e2"],
        "bcsel": cst["bcsel"], "maskC": cst["maskC"], "bcsel8": cst["bcsel8"],
    }


def build_program(phases=("p1", "p2", "p3", "p4"), stream_in=False, stream_out=False):
    nc = bass.Bass("TRN2", target_bir_lowering=False)
    io = {}
    io["d_pool"] = nc.dram_tensor("pool_in", [128, NPOOL], F32, kind="ExternalInput")
    io["d_rows"] = nc.dram_tensor("rows_in", [1, NROW], F32, kind="ExternalInput")
    io["d_identb"] = nc.dram_tensor("identb_in", [128, 128], BF16, kind="ExternalInput")
    io["d_identf"] = nc.dram_tensor("identf_in", [128, 128], F32, kind="ExternalInput")
    io["d_maskA"] = nc.dram_tensor("maskA_in", [128, 128], BF16, kind="ExternalInput")
    io["d_maskP"] = nc.dram_tensor("maskP_in", [128, 1024], BF16, kind="ExternalInput")
    io["d_e2"] = nc.dram_tensor("e2_in", [128, 2], BF16, kind="ExternalInput")
    if "p1" in phases:
        io["d_x"] = nc.dram_tensor("x_in", [T, D], F32, kind="ExternalInput")
        io["d_W_attn"] = nc.dram_tensor("W_attn", [D, 3 * D], F32, kind="ExternalInput")
        io["d_W_proj"] = nc.dram_tensor("W_proj", [D, D], F32, kind="ExternalInput")
        io["d_omega"] = nc.dram_tensor("omega", [64, 128], F32, kind="ExternalInput")
    if "p2" in phases:
        import os as _os2
        if _os2.environ.get("P2_DBG"):
            io["d_dbg_wsel"] = nc.dram_tensor("dbg_wsel", [128, NT, 4], F32,
                                              kind="ExternalOutput")
            io["d_dbg_accum"] = nc.dram_tensor("dbg_accum", [128, ND, T], F32,
                                               kind="ExternalOutput")
            io["d_dbg_gateb"] = nc.dram_tensor("dbg_gateb", [6, 128, T], BF16,
                                               kind="ExternalOutput")
        io["d_gate_w"] = nc.dram_tensor("gate_w", [D, 4], F32, kind="ExternalInput")
        io["d_sw1"] = nc.dram_tensor("sw1", [2, D, 4096], F32, kind="ExternalInput")
        io["d_sw2"] = nc.dram_tensor("sw2", [2, 4096, D], F32, kind="ExternalInput")
        io["d_rw1"] = nc.dram_tensor("rw1", [4, D, 4096], F32, kind="ExternalInput")
        io["d_rw2"] = nc.dram_tensor("rw2", [4, 4096, D], F32, kind="ExternalInput")
        io["d_bcsel"] = nc.dram_tensor("bcsel", [4, 512], BF16, kind="ExternalInput")
    io["d_bcsel8"] = nc.dram_tensor("bcsel8", [8, 1024], BF16,
                                    kind="ExternalInput")
    if "p4" in phases:
        io["d_out"] = nc.dram_tensor("out", [T, D], F32, kind="ExternalOutput")
        io["d_choice_wT"] = nc.dram_tensor("choice_wT", [D, 8], F32,
                                           kind="ExternalInput")
        io["d_fw1"] = nc.dram_tensor("fw1", [8, D, 2048], F32, kind="ExternalInput")
        io["d_fw2"] = nc.dram_tensor("fw2", [8, 2048, D], F32, kind="ExternalInput")
    if "p3" in phases:
        io["d_Wq"] = nc.dram_tensor("Wq", [D, D], F32, kind="ExternalInput")
        io["d_Wk"] = nc.dram_tensor("Wk", [D, D], F32, kind="ExternalInput")
        io["d_Wv"] = nc.dram_tensor("Wv", [D, D], F32, kind="ExternalInput")
        io["d_Wo"] = nc.dram_tensor("Wo", [D, D], F32, kind="ExternalInput")
        io["d_W1"] = nc.dram_tensor("W1", [D, 4096], F32, kind="ExternalInput")
        io["d_W2"] = nc.dram_tensor("W2", [4096, D], F32, kind="ExternalInput")
        io["d_maskC"] = nc.dram_tensor("maskC", [128, 3584], BF16, kind="ExternalInput")
    if stream_in:
        io["d_sin"] = nc.dram_tensor("stream_in", [128, ND, T], BF16, kind="ExternalInput")
    if stream_out:
        io["d_sout"] = nc.dram_tensor("stream_out", [128, ND, T], BF16,
                                      kind="ExternalOutput")
    import os as _os
    dbg = _os.environ.get("P1_DBG", "")
    if dbg:
        for nm in dbg.split(","):
            io["d_dbg_" + nm] = nc.dram_tensor("dbg_" + nm, [128, 16, 2048], BF16,
                                               kind="ExternalOutput")

    with ExitStack() as ctx:
        p = Prog(nc, ctx)
        d = alloc_common(p, io)
        pb0 = PB(p)
        emit_load_consts(p, pb0, d)
        if stream_in:
            eS = p.edge("sin")
            for dt in range(ND):
                pb0.op("g", lambda E, dt=dt: E.dma_start(d.arA[:, dt, :], d.d_sin[:, dt, :]),
                       inc=eS, inc_k=16)
        fw = [(p.edge("cst"), p.edge("cst").n)]
        if stream_in:
            fw.append((p.edge("sin"), p.edge("sin").n))
        for engx in ("t", "v", "s", "y"):
            pb0.op(engx, lambda E: E.nop(), wait=fw)
        pb0.emit()

        if "p1" in phases:
            with ExitStack() as c1:
                pp = sub_prog(p, c1)
                pp.alloc_std_psum("p1", nf=6, nb=2)
                alloc_p1(pp, d)
                ppbs = []
                import os as _os
                stages = _os.environ.get("P1_STAGES", "ABCDEF")
                dumps = {"B": [("q", d.qC), ("k", d.kD), ("v", d.vE)],
                         "C": [("nssqt", None)],
                         "D": [("yt", d.kD), ("phiq", None)],
                         "E": [("s", d.arB)],
                         "F": []}
                ppbs2 = []
                emit_p1(pp, ppbs2, stages=stages)
                per_stage = {"A": ppbs2[0:1]}
                idx = 1
                for st in "BCDEF":
                    if st in stages and idx < len(ppbs2) + 1:
                        per_stage[st] = ppbs2[idx:idx + 1]
                        idx += 1
                for st in "ABCDEF":
                    if st not in stages:
                        continue
                    for b in per_stage.get(st, []):
                        b.emit()
                    for nm, src_t in dumps.get(st, []):
                        if ("d_dbg_" + nm) in io and src_t is not None:
                            dpb = PB(pp)
                            eD = pp.edge("dbg_" + nm)
                            dst = io["d_dbg_" + nm]
                            if src_t.shape[1] == ND:
                                for dt2 in range(ND):
                                    dpb.op("y", lambda E, dt2=dt2, src_t=src_t, dst=dst:
                                           E.dma_start(dst[:, dt2, :], src_t[:, dt2, :]),
                                           inc=eD, inc_k=16)
                            else:
                                for t2 in range(NT):
                                    dpb.op("y", lambda E, t2=t2, src_t=src_t, dst=dst:
                                           E.dma_start(dst[:, t2, 0:src_t.shape[2]],
                                                       src_t[:, t2, :]),
                                           inc=eD, inc_k=16)
                            dpb.op("y", lambda E: E.nop(), wait=[(eD, eD.n)])
                            dpb.emit()

        if "p1" in phases:
            fpb = PB(p)
            e = p.edge("ln_ap")
            for engx in ("t", "s", "g", "y"):
                fpb.op(engx, lambda E: E.nop(), wait=[(e, e.n)])
            fpb.emit()

        if "p2" in phases:
            with ExitStack() as c2:
                pp = sub_prog(p, c2)
                pp.alloc_wide_psum("p2")
                alloc_p2(pp, d)
                ppbs = []
                emit_p2(pp, ppbs, stages=_os.environ.get("P2_STAGES", "GEL"))
                for b in ppbs:
                    b.emit()

        if "p2" in phases:
            fpb = PB(p)
            e = p.edge("ln_ap")
            for engx in ("t", "s", "g", "y"):
                fpb.op(engx, lambda E: E.nop(), wait=[(e, e.n)])
            fpb.emit()

        if "p3" in phases:
            with ExitStack() as c3:
                pp = sub_prog(p, c3)
                alloc_p3_outer(pp, d)
                with ExitStack() as c3a:
                    pa = sub_prog(pp, c3a)
                    alloc_p3_attn(pa, d)
                    pa.ps3_scb = [pa.psum("ps3_sca", [128, 1024], F32),
                                  pa.psum("ps3_scc", [128, 1024], F32)]
                    pa.ps3_ptb = [pa.psum("ps3_pta", [128, 1024], BF16),
                                  pa.psum("ps3_ptc", [128, 1024], BF16)]
                    pa.ps3_pvb = [pa.psum("ps3_pva", [128, 512], F32),
                                  pa.psum("ps3_pvc", [128, 512], F32)]
                    pa.ps3_sc = pa.ps3_scb[0]
                    pa.ps3_pt = pa.ps3_scb[1]
                    ppbs = []
                    emit_p3_attn(pa, ppbs)
                    for b in ppbs:
                        b.emit()
                with ExitStack() as c3b:
                    pbp = sub_prog(pp, c3b)
                    pbp.alloc_std_psum("p3s1", nf=6, nb=0)
                    alloc_p3_s1(pbp, d)
                    ppbs = []
                    emit_p3_s1(pbp, ppbs)
                    for b in ppbs:
                        b.emit()
                with ExitStack() as c3c:
                    pc = sub_prog(pp, c3c)
                    pc.alloc_wide_psum("p3f")
                    alloc_p3_ffn(pc, d)
                    ppbs = []
                    emit_p3_ffn(pc, ppbs)
                    for b in ppbs:
                        b.emit()
            fpb = PB(p)
            e = p.edge("ln_ap")
            for engx in ("t", "s", "g", "y"):
                fpb.op(engx, lambda E: E.nop(), wait=[(e, e.n)])
            fpb.emit()

        if "p4" in phases:
            with ExitStack() as c4:
                pp = sub_prog(p, c4)
                pp.alloc_wide_psum("p4")
                alloc_p4(pp, d)
                ppbs = []
                emit_p4(pp, ppbs)
                for b in ppbs:
                    b.emit()

        outp = PB(p)
        if stream_out:
            eO = p.edge("sout")
            for dt in range(ND):
                outp.op("y", lambda E, dt=dt: E.dma_start(d.d_sout[:, dt, :], d.arA[:, dt, :]),
                        inc=eO, inc_k=16)
            outp.op("y", lambda E: E.nop(), wait=[(eO, eO.n)])
        outp.emit()
    return nc


FUSED = True


def kernel(**inputs):
    inp = inputs
    ncores = 8
    base = const_inmap(inp)
    wmap_p1 = {
        "W_attn": np.ascontiguousarray(inp["L0_attn_W_attn"]),
        "W_proj": np.ascontiguousarray(inp["L0_attn_W_proj"]),
        "omega": np.ascontiguousarray(inp["L0_attn_omega"]),
    }
    wmap_p2 = {
        "gate_w": np.ascontiguousarray(inp["L0_ffn_gate_w"]),
        "sw1": np.ascontiguousarray(inp["L0_ffn_sw1"]),
        "sw2": np.ascontiguousarray(inp["L0_ffn_sw2"]),
        "rw1": np.ascontiguousarray(inp["L0_ffn_rw1"]),
        "rw2": np.ascontiguousarray(inp["L0_ffn_rw2"]),
    }
    wmap_p3 = {
        "Wq": np.ascontiguousarray(inp["L1_attn_Wq"]),
        "Wk": np.ascontiguousarray(inp["L1_attn_Wk"]),
        "Wv": np.ascontiguousarray(inp["L1_attn_Wv"]),
        "Wo": np.ascontiguousarray(inp["L1_attn_Wo"]),
        "W1": np.ascontiguousarray(inp["L1_attn_W1"]),
        "W2": np.ascontiguousarray(inp["L1_attn_W2"]),
    }
    wmap_p4 = {
        "choice_wT": np.ascontiguousarray(np.asarray(inp["L1_ffn_choice_w"]).T),
        "fw1": np.ascontiguousarray(inp["L1_ffn_w1"]),
        "fw2": np.ascontiguousarray(inp["L1_ffn_w2"]),
    }
    xs = [np.ascontiguousarray(inp["L0_attn_x"][b]) for b in range(ncores)]

    if FUSED:
        nc = build_program(phases=("p1", "p2", "p3", "p4"))
        im = dict(base)
        im.update(wmap_p1); im.update(wmap_p2); im.update(wmap_p3); im.update(wmap_p4)
        in_maps = [dict(im, x_in=xs[b]) for b in range(ncores)]
        res = run_bass_kernel_spmd(nc, in_maps, core_ids=list(range(ncores)))
        return np.stack([res.results[b]["out"] for b in range(ncores)], axis=0)

    nc1 = build_program(phases=("p1",), stream_out=True)
    im = dict(base); im.update(wmap_p1)
    res = run_bass_kernel_spmd(nc1, [dict(im, x_in=xs[b]) for b in range(ncores)],
                               core_ids=list(range(ncores)))
    streams = [res.results[b]["stream_out"] for b in range(ncores)]

    nc2 = build_program(phases=("p2",), stream_in=True, stream_out=True)
    im = dict(base); im.update(wmap_p2)
    res = run_bass_kernel_spmd(nc2, [dict(im, stream_in=streams[b])
                                     for b in range(ncores)],
                               core_ids=list(range(ncores)))
    streams = [res.results[b]["stream_out"] for b in range(ncores)]

    nc3 = build_program(phases=("p3",), stream_in=True, stream_out=True)
    im = dict(base); im.update(wmap_p3)
    res = run_bass_kernel_spmd(nc3, [dict(im, stream_in=streams[b])
                                     for b in range(ncores)],
                               core_ids=list(range(ncores)))
    streams = [res.results[b]["stream_out"] for b in range(ncores)]

    nc4 = build_program(phases=("p4",), stream_in=True)
    im = dict(base); im.update(wmap_p4)
    res = run_bass_kernel_spmd(nc4, [dict(im, stream_in=streams[b])
                                     for b in range(ncores)],
                               core_ids=list(range(ncores)))
    return np.stack([res.results[b]["out"] for b in range(ncores)], axis=0)


# version 10
# speedup vs baseline: 1.1667x; 1.1667x over previous
import numpy as np
import ml_dtypes
from contextlib import ExitStack

import concourse.bass as bass
import concourse.mybir as mybir
from concourse.bass_utils import run_bass_kernel_spmd

F32 = mybir.dt.float32
BF16 = mybir.dt.bfloat16
AF = mybir.ActivationFunctionType
ALU = mybir.AluOpType
AX = mybir.AxisListType

T = 2048
D = 1024
NT = T // 128
ND = D // 128
NHEAD = 16
DH = 64
CHUNKS = tuple((i * 512, 512) for i in range(4))
CHUNKS4 = CHUNKS
LN_EPS = 1e-5
PERF_EPS = 1e-6
LNSQRT_M = float(np.log(np.sqrt(128.0)))

BF = ml_dtypes.bfloat16

_POOL_SPECS = [
    ("b_attn", 24), ("b_proj", 8),
    ("sb1_0", 32), ("sb1_1", 32),
    ("rb1_0", 32), ("rb1_1", 32), ("rb1_2", 32), ("rb1_3", 32),
    ("bq", 8), ("bk", 8), ("bo", 8),
    ("g1", 8), ("b1", 8), ("c1", 32), ("c2", 8), ("g2", 8), ("b2", 8),
    ("sb2_0", 8), ("sb2_1", 8), ("rb2_0", 8), ("rb2_1", 8), ("rb2_2", 8), ("rb2_3", 8),
]
POOL = {}
_c = 0
for _n, _k in _POOL_SPECS:
    POOL[_n] = _c
    _c += _k
NPOOL = _c

_ROW_SPECS = [("bv", 8), ("bav", 8), ("gate_b", 1), ("sb2_0", 8), ("sb2_1", 8),
              ("rb2_0", 8), ("rb2_1", 8), ("rb2_2", 8), ("rb2_3", 8)]
ROWS = {}
_c = 0
for _n, _k in _ROW_SPECS:
    ROWS[_n] = _c * 128
    _c += _k
NROW = _c * 128


class Edge:
    def __init__(self, ctx, nc, name):
        self.sem = ctx.enter_context(nc.semaphore(name))
        self.n = 0


class PB:

    ENG = {"t": "tensor", "v": "vector", "s": "scalar", "g": "gpsimd", "y": "sync"}

    def __init__(self, prog):
        self.prog = prog
        self.qs = {k: [] for k in self.ENG}

    def op(self, eng, fn, wait=(), inc=None, inc_k=1, drain=False):
        wv = []
        for w in wait:
            if isinstance(w, tuple):
                e, v = w
            else:
                e, v = w, w.n
            if v > 0:
                wv.append((e.sem, v))
        iv = None
        if inc is not None:
            inc.n += inc_k
            iv = (inc.sem, inc_k)

        def run(E, wv=wv, iv=iv, fn=fn, drain=drain):
            for s, v in wv:
                E.wait_ge(s, v)
            if drain:
                E.drain()
            inst = fn(E)
            if iv is not None:
                inst.then_inc(*iv)

        self.qs[eng].append(run)

    def emit(self):
        nc = self.prog.nc
        with nc.Block() as b:
            for k, attr in self.ENG.items():
                if self.qs[k]:
                    getattr(b, attr)(lambda E, fns=tuple(self.qs[k]): [f(E) for f in fns])


class Prog:
    def __init__(self, nc, ctx):
        self.nc = nc
        self.ctx = ctx
        self.ctx_root = ctx
        self._edges = {}
        self.d = None
        self.PSF = None
        self.PSB = None

    def edge(self, name):
        if name not in self._edges:
            self._edges[name] = Edge(self.ctx_root, self.nc, "E_" + name)
        return self._edges[name]

    def sb(self, name, shape, dtype):
        return self.ctx.enter_context(self.nc.sbuf_tensor(name, shape, dtype))

    def psum(self, name, shape, dtype=F32):
        return self.ctx.enter_context(self.nc.psum_tensor(name, shape, dtype))

    def alloc_std_psum(self, tag, nf=6, nb=2):
        self.PSF = [self.psum("psf_%s%d" % (tag, i), [128, 512], F32)
                    for i in range(nf)]
        self.PSB = [self.psum("psb_%s%d" % (tag, i), [128, 1024], BF16)
                    for i in range(nb)]

    def alloc_wide_psum(self, tag):
        self.PSW = [self.psum("psw_%s%d" % (tag, i), [128, 1024], F32)
                    for i in range(4)]
        self.psh2 = [self.PSW[0][:, 0:512], self.PSW[0][:, 512:1024],
                     self.PSW[1][:, 0:512], self.PSW[1][:, 512:1024]]
        self.psy2 = [self.PSW[2][:, 0:512], self.PSW[2][:, 512:1024],
                     self.PSW[3][:, 0:512], self.PSW[3][:, 512:1024]]
        self.PSF = [self.PSW[0][:, 0:512], self.PSW[0][:, 512:1024],
                    self.PSW[1][:, 0:512], self.PSW[1][:, 512:1024],
                    self.PSW[2][:, 0:512], self.PSW[2][:, 512:1024]]


class NS:
    pass


def sub_prog(p, ctx):
    pp = Prog.__new__(Prog)
    pp.nc = p.nc
    pp.ctx = ctx
    pp.ctx_root = p.ctx_root
    pp._edges = p._edges
    pp.PSF = None
    pp.PSB = None
    pp.d = p.d
    return pp


def emit_load_consts(p, pb, d):
    eL = p.edge("cst")
    for fn in [
        lambda E: E.dma_start(d.pool[:, :], d.d_pool[:, :]),
        lambda E: E.dma_start(d.identb[:, :], d.d_identb[:, :]),
        lambda E: E.dma_start(d.identf[:, :], d.d_identf[:, :]),
        lambda E: E.dma_start(d.e2[:, :], d.d_e2[:, :]),
        lambda E: E.dma_start(d.bcsel8[:, :], d.d_bcsel8[:, :]),
    ]:
        pb.op("g", fn, inc=eL, inc_k=16)
    pb.op("g", lambda E: E.memset(d.ones1[:, :], 1.0))
    pb.op("g", lambda E: E.memset(d.onescol[:, :], 1.0))
    pb.op("g", lambda E: E.memset(d.onescol_f[:, :], 1.0))
    pb.op("g", lambda E: E.memset(d.cst[:, 0:1], -LNSQRT_M))
    pb.op("g", lambda E: E.memset(d.cst[:, 1:2], PERF_EPS))
    pb.op("g", lambda E: E.memset(d.cst[:, 2:3], LN_EPS))


def emit_ln(p, pb, src, dst, gain=None, bias=None, name="ln", src_f32=False,
            prewait=()):
    d = p.d
    ones = d.onescol_f if src_f32 else d.onescol
    e_row = p.edge("ln_row")
    e_st = p.edge("ln_st")
    e_sq = p.edge("ln_sq")
    e_qt = p.edge("ln_qt")
    e_rs = p.edge("ln_rs")
    e_cs = p.edge("ln_cs")
    e_bc = p.edge("ln_bc")
    e_ap = p.edge("ln_ap")
    ps_rs = [p.PSF[0], p.PSF[3]]
    ps_As = [p.PSF[1], p.PSF[4]]
    ps_Bs = [p.PSF[2], p.PSF[5]]
    row_b = e_row.n
    ap_b = e_ap.n
    for ch in range(4):
        cs = ch * 512
        ps_r = ps_rs[ch % 2]
        ps_A = ps_As[ch % 2]
        ps_B = ps_Bs[ch % 2]
        for dt in range(ND):
            pb.op("t", lambda E, dt=dt, cs=cs, ps_r=ps_r: E.matmul(
                ps_r[0:1, 0:512], ones[:, :], src(dt)[:, cs:cs + 512],
                start=(dt == 0), stop=(dt == ND - 1)),
                wait=(list(prewait) if (ch == 0 and dt == 0) else []) +
                     ([(e_row, row_b + ch - 1)] if (dt == 0 and ch >= 2) else []),
                inc=e_st if dt == ND - 1 else None)
        for dt in range(ND):
            pb.op("s", lambda E, dt=dt, cs=cs: E.activation(
                d.ln_sq[:, (dt % 2) * 512:(dt % 2) * 512 + 512], src(dt)[:, cs:cs + 512],
                AF.Square),
                wait=(list(prewait) if (ch == 0 and dt == 0) else []) +
                     ([(e_qt, e_qt.n - 1)] if dt >= 2 else []), inc=e_sq)
            pb.op("t", lambda E, dt=dt, ps_r=ps_r: E.matmul(
                ps_r[32:33, 0:512], d.onescol[:, :],
                d.ln_sq[:, (dt % 2) * 512:(dt % 2) * 512 + 512],
                start=(dt == 0), stop=(dt == ND - 1)),
                wait=[(e_sq, e_sq.n)], inc=e_qt)
        def rowmath1(E, ps_r=ps_r):
            E.tensor_scalar(d.ln_r1[0:1, 0:512], ps_r[0:1, 0:512], 1.0 / D, None,
                            op0=ALU.mult)
            E.drain()
            return E.tensor_scalar(d.ln_r2[0:1, 0:512], ps_r[32:33, 0:512],
                                   1.0 / D, None, op0=ALU.mult)
        pb.op("v", rowmath1, wait=[(e_st, e_st.n), (e_qt, e_qt.n), (e_rs, e_rs.n)],
              drain=True, inc=e_row)

        def rowmath(E):
            E.tensor_tensor(d.ln_tmp[0:1, 0:512], d.ln_r1[0:1, 0:512],
                            d.ln_r1[0:1, 0:512], op=ALU.mult)
            E.drain()
            return E.tensor_tensor(d.ln_r2[0:1, 0:512], d.ln_r2[0:1, 0:512],
                                   d.ln_tmp[0:1, 0:512], op=ALU.subtract)
        e_rw2 = p.edge("ln_rw2")
        pb.op("v", rowmath, drain=True, inc=e_rw2)
        def rstd(E):
            E.activation(d.ln_r2[0:1, 0:512], d.ln_r2[0:1, 0:512], AF.Ln,
                         bias=d.cst[0:1, 2:3])
            E.drain()
            return E.activation(d.ln_rowAb[0:1, 0:512], d.ln_r2[0:1, 0:512], AF.Exp,
                                scale=-0.5)
        pb.op("s", rstd, wait=[(e_rw2, e_rw2.n), (e_bc, e_bc.n)], drain=True,
              inc=e_rs)
        pb.op("v", lambda E: E.scalar_tensor_tensor(
            d.ln_rowBb[0:1, 0:512], d.ln_r1[0:1, 0:512], -1.0,
            d.ln_rowAb[0:1, 0:512], op0=ALU.mult, op1=ALU.mult),
            wait=[(e_rs, e_rs.n)], drain=True, inc=e_cs)
        pb.op("t", lambda E, ps_A=ps_A: E.matmul(
            ps_A[:, 0:512], d.ones1[:, :], d.ln_rowAb[0:1, 0:512]),
            wait=[(e_cs, e_cs.n)] +
                 ([(e_ap, ap_b + ch - 1)] if ch >= 2 else []))
        pb.op("t", lambda E, ps_B=ps_B: E.matmul(
            ps_B[:, 0:512], d.ones1[:, :], d.ln_rowBb[0:1, 0:512]),
            inc=e_bc)
        for dt in range(ND):
            if gain is None:
                def apply(E, dt=dt, cs=cs, ps_A=ps_A, ps_B=ps_B):
                    E.drain()
                    E.tensor_tensor(d.ln_tmp[:, 0:512], src(dt)[:, cs:cs + 512],
                                    ps_A[:, 0:512], op=ALU.mult)
                    E.drain()
                    return E.tensor_tensor(dst(dt, cs // 512), d.ln_tmp[:, 0:512],
                                           ps_B[:, 0:512], op=ALU.add)
            else:
                def apply(E, dt=dt, cs=cs, gain=gain, bias=bias, ps_A=ps_A, ps_B=ps_B):
                    E.drain()
                    E.tensor_tensor(d.ln_tmp[:, 0:512], src(dt)[:, cs:cs + 512],
                                    ps_A[:, 0:512], op=ALU.mult)
                    E.drain()
                    E.tensor_tensor(d.ln_tmp[:, 0:512], d.ln_tmp[:, 0:512],
                                    ps_B[:, 0:512], op=ALU.add)
                    E.drain()
                    return E.tensor_scalar(dst(dt, cs // 512), d.ln_tmp[:, 0:512],
                                           d.pool[:, gain + dt:gain + dt + 1],
                                           d.pool[:, bias + dt:bias + dt + 1],
                                           op0=ALU.mult, op1=ALU.add)
            pb.op("v", apply, wait=[(e_bc, e_bc.n)],
                  inc=e_ap if dt == ND - 1 else None)


def emit_p1(p, pbs, stages="ABCDEF"):
    d = p.d

    pb = PB(p)
    eg = [p.edge("p1a_g0"), p.edge("p1a_g1")]
    et = p.edge("p1a_t")
    ev = p.edge("p1a_v")
    ev_b = ev.n
    trb = [p.PSB[0], p.PSB[1]]
    for t in range(NT):
        pb.op("g", lambda E, t=t: E.dma_start(
            d.xld[:, t % 2, :], d.d_x[t * 128:(t + 1) * 128, :]),
            wait=[(ev, ev_b + t - 1)] if t >= 2 else (), inc=eg[t % 2], inc_k=16)
        gt = eg[t % 2].n
        for dt in range(ND):
            pb.op("t", lambda E, t=t, dt=dt: E.transpose(
                trb[t % 2][:, dt * 128:(dt + 1) * 128],
                d.xld[:, t % 2, dt * 128:(dt + 1) * 128], d.identb[:, :]),
                wait=[(eg[t % 2], gt), (ev, ev_b + t - 1)] if dt == 0 else (),
                inc=et if dt == ND - 1 else None)
        pb.op("v", lambda E, t=t: E.tensor_copy(
            d.arA[:, :, t * 128:(t + 1) * 128],
            trb[t % 2].rearrange("p (d q) -> p d q", q=128)),
            wait=[(et, et.n)], inc=ev)
    pbs.append(pb)

    if "B" not in stages:
        return
    pb = PB(p)
    ew = p.edge("qkv_w")
    em = p.edge("qkv_m")
    ee = p.edge("qkv_e")
    ps_mm = [p.PSF[0], p.PSF[1]]
    erow = p.edge("qkv_row")
    ee_b = ee.n
    pb.op("g", lambda E: E.dma_start(
        d.rows_bv[:, :], d.d_rows[0:1, ROWS["bav"]:ROWS["bav"] + 1024]),
        inc=erow, inc_k=16)
    er_t = erow.n
    nmm = 0
    for part, dst_ar in ((0, d.qC), (1, d.kD), (2, None)):
        for kc in range(ND):
            pb.op("g", lambda E, kc=kc, part=part: E.dma_start(
                d.wW[:, kc, :], d.d_W_attn[kc * 128:(kc + 1) * 128,
                                           part * 1024:(part + 1) * 1024]),
                wait=[(em, em.n)] if kc == 0 else (), inc=ew, inc_k=16)
        wt = ew.n
        if part < 2:
            for m in range(ND):
                for ch in range(4):
                    pi = nmm % 2
                    for kc in range(ND):
                        w = [(ew, wt), (ee, ee_b + nmm - 1)] if kc == 0 else []
                        pb.op("t", lambda E, m=m, ch=ch, kc=kc, pi=pi: E.matmul(
                            ps_mm[pi][:, :], d.wW[:, kc, m * 128:(m + 1) * 128],
                            d.arA[:, kc, ch * 512:(ch + 1) * 512],
                            start=(kc == 0), stop=(kc == ND - 1)),
                            wait=w, inc=em if kc == ND - 1 else None)
                    bcol = POOL["b_attn"] + part * 8 + m
                    pb.op("s", lambda E, pi=pi, bcol=bcol, m=m, ch=ch, dst_ar=dst_ar:
                          E.activation(dst_ar[:, m, ch * 512:(ch + 1) * 512],
                                       ps_mm[pi][:, :], AF.Identity,
                                       bias=d.pool[:, bcol:bcol + 1]),
                          wait=[(em, em.n)], inc=ee)
                    nmm += 1
        else:
            for t in range(NT):
                for ch in range(2):
                    pi = nmm % 2
                    for kc in range(ND):
                        w = [(ew, wt), (ee, ee_b + nmm - 1)] if kc == 0 else []
                        pb.op("t", lambda E, t=t, ch=ch, kc=kc, pi=pi: E.matmul(
                            ps_mm[pi][:, :], d.arA[:, kc, t * 128:(t + 1) * 128],
                            d.wW[:, kc, ch * 512:(ch + 1) * 512],
                            start=(kc == 0), stop=False),
                            wait=w)
                    pb.op("t", lambda E, ch=ch, pi=pi: E.matmul(
                        ps_mm[pi][:, :], d.ones1[:, :],
                        d.rows_bv[0:1, ch * 512:ch * 512 + 512],
                        start=False, stop=True),
                        wait=[(erow, er_t)], inc=em)
                    pb.op("s", lambda E, t=t, ch=ch, pi=pi: E.activation(
                        d.vE[:, t, ch * 512:(ch + 1) * 512], ps_mm[pi][:, :], AF.Copy),
                        wait=[(em, em.n)], inc=ee)
                    nmm += 1
    pbs.append(pb)

    if "C" not in stages:
        return
    pb = PB(p)
    eo = p.edge("p1c_o")
    eob = p.edge("p1c_ob")
    e_sq = p.edge("p1c_sq")
    e_mm = p.edge("p1c_mm")
    e_bi = p.edge("p1c_bi")
    ps_row = p.PSF[2]
    e_mA = p.edge("p1c_mA")
    pb.op("g", lambda E: E.dma_start(d.maskA[:, :], d.d_maskA[:, :]), inc=e_mA, inc_k=16)
    mA_t = e_mA.n
    pb.op("g", lambda E: E.dma_start(d.om_f[0:64, :], d.d_omega[:, :]), inc=eo, inc_k=16)
    pb.op("g", lambda E: E.dma_start(d.om_f[64:128, :], d.d_omega[:, :]), inc=eo, inc_k=16)
    pb.op("v", lambda E: E.tensor_scalar(d.om[:, :], d.om_f[:, :], 0.125, None,
                                         op0=ALU.mult),
          wait=[(eo, eo.n)], inc=eob)
    u = 0
    for j, src_ar in ((0, d.qC), (1, d.kD)):
        for t in range(NT):
            for dt in range(ND):
                pb.op("s", lambda E, dt=dt, t=t, src_ar=src_ar, u=u: E.activation(
                    d.ln_sq[:, (u % 8) * 128:(u % 8) * 128 + 128],
                    src_ar[:, dt, t * 128:(t + 1) * 128], AF.Square),
                    wait=[(e_mm, u - 7)] if u >= 8 else (), inc=e_sq)
                pb.op("t", lambda E, dt=dt, u=u: E.matmul(
                    ps_row[:, dt * 2:dt * 2 + 2], d.ln_sq[:, (u % 8) * 128:(u % 8) * 128 + 128],
                    d.e2[:, :]),
                    wait=[(e_sq, e_sq.n), (e_bi, e_bi.n)], inc=e_mm)
                u += 1
            pb.op("s", lambda E, t=t, j=j: E.activation(
                d.nssqT[:, t, j * 16:j * 16 + 16], ps_row[:, 0:16], AF.Identity,
                bias=d.cst[:, 0:1]),
                wait=[(e_mm, e_mm.n)], inc=e_bi)
    pbs.append(pb)

    if "D" not in stages:
        return
    pb = PB(p)
    ph_t = p.edge("p1d_t")
    ph_s = p.edge("p1d_s")
    ph_tt = p.edge("p1d_tt")
    ph_ve = p.edge("p1d_ve")
    sc_tv = [p.edge("p1d_tv0"), p.edge("p1d_tv1")]
    sc_vt = [p.edge("p1d_vt0"), p.edge("p1d_vt1")]
    sc_tn = [p.edge("p1d_tn0"), p.edge("p1d_tn1")]
    sc_ts = [p.edge("p1d_ts0"), p.edge("p1d_ts1")]
    sc_st = [p.edge("p1d_st0"), p.edge("p1d_st1")]
    sc_yv = [p.edge("p1d_yv0"), p.edge("p1d_yv1")]
    sc_ytp = p.edge("p1d_ytp")
    sc_yte = p.edge("p1d_yte")
    ps_S = [p.PSF[0], p.PSF[3]]
    ps_N = [p.PSF[1], p.PSF[4]]
    ps_P = [p.PSF[2], p.PSF[5]]
    e_biC = p.edge("p1c_bi")
    biC_t = e_biC.n
    e_mAC = p.edge("p1c_mA")
    mAC_t = e_mAC.n
    for g in range(8):
        dt = g
        for par in range(2):
            h = 2 * g + par
            ho = par * 64
            for j, (src_ar, phisel, nof) in enumerate(
                    ((d.qC, d.phiQ, 0), (d.kD, d.phiK, 16))):
                for t in range(NT):
                    u = (par * 2 + j) * NT + t
                    wv = [(eob, 1), (ph_s, ph_s.n - 1)]
                    if g == 0 and u == 0:
                        wv.append((e_biC, biC_t))
                    pb.op("t", lambda E, t=t, dt=dt, ho=ho, src_ar=src_ar, u=u:
                          E.matmul(
                        ps_P[u % 2][:, ((u // 2) % 2) * 128:
                                    ((u // 2) % 2) * 128 + 128],
                        src_ar[ho:ho + 64, dt, t * 128:(t + 1) * 128],
                        d.om[ho:ho + 64, :]),
                        wait=wv, inc=ph_t)
                    pb.op("s", lambda E, t=t, h=h, u=u, par=par, phisel=phisel,
                          nof=nof: E.activation(
                        phisel[:, par, t * 128:(t + 1) * 128],
                        ps_P[u % 2][:, ((u // 2) % 2) * 128:
                                    ((u // 2) % 2) * 128 + 128],
                        AF.Exp, bias=d.nssqT[:, t, nof + h:nof + h + 1]),
                        wait=[(ph_t, ph_t.n)], inc=ph_s)
        allexp = ph_s.n
        for pj, (phisel, phiTsel, par) in enumerate(
                ((d.phiQ, d.phiQT, 0), (d.phiK, d.phiKT, 0),
                 (d.phiQ, d.phiQT, 1), (d.phiK, d.phiKT, 1))):
            for t in range(NT):
                u = pj * NT + t
                grp = u // 4
                pb.op("t", lambda E, t=t, u=u, par=par, phisel=phisel, grp=grp:
                      E.transpose(
                    p.PSB[grp % 2][:, (u % 4) * 128:(u % 4) * 128 + 128],
                    phisel[:, par, t * 128:(t + 1) * 128], d.identb[:, :]),
                    wait=[(ph_s, allexp), (ph_ve, ph_ve.n - 1),
                          (sc_yte, sc_yte.n)] if t % 4 == 0 else (),
                    inc=ph_tt if t % 4 == 3 else None)
                if t % 4 == 3:
                    pb.op("v", lambda E, t=t, par=par, phiTsel=phiTsel, grp=grp:
                          E.tensor_copy(
                        phiTsel[:, par, (t - 3) * 128:(t + 1) * 128],
                        p.PSB[grp % 2][:, 0:512]),
                        wait=[(ph_tt, ph_tt.n)], inc=ph_ve)
        pair_rdy = ph_ve.n
        for c in range(NT):
            for par in range(2):
                h = 2 * g + par
                wv = [(ph_ve, pair_rdy)] if c == 0 else [(sc_vt[par],
                                                          16 * g + c)]
                pb.op("t", lambda E, c=c, par=par: E.matmul(
                    ps_N[par][:, 128:256], d.phiKT[:, par, c * 128:(c + 1) * 128],
                    d.phiQT[:, par, c * 128:(c + 1) * 128]),
                    wait=wv, inc=sc_tv[par])
                wv = [(sc_tv[par], 16 * g + c + 1), (sc_tn[par], 16 * g + c)]
                if g == 0 and c == 0 and par == 0:
                    wv.append((e_mAC, mAC_t))
                pb.op("v", lambda E, par=par: E.tensor_tensor(
                    d.Am[:, par, :], ps_N[par][:, 128:256], d.maskA[:, :],
                    op=ALU.mult),
                    wait=wv, inc=sc_vt[par])
                pb.op("t", lambda E, c=c, h=h, par=par: E.matmul(
                    ps_N[par][:, 0:64], d.Am[:, par, :],
                    d.vE[:, c, h * 64:h * 64 + 64],
                    start=True, stop=False),
                    wait=[(sc_vt[par], 16 * g + c + 1),
                          (sc_yv[par], 16 * g + c)])
                if c > 0:
                    pb.op("t", lambda E, par=par: E.matmul(
                        ps_N[par][:, 64:65], d.Am[:, par, :], d.onescol[:, :],
                        start=False, stop=False))
                    pb.op("t", lambda E, c=c, par=par: E.matmul(
                        ps_N[par][:, 0:65],
                        d.phiQT[:, par, c * 128:(c + 1) * 128],
                        d.S_sb[:, par, :],
                        start=False, stop=True),
                        wait=[(sc_st[par], 16 * g + c)], inc=sc_tn[par])
                else:
                    pb.op("t", lambda E, par=par: E.matmul(
                        ps_N[par][:, 64:65], d.Am[:, par, :], d.onescol[:, :],
                        start=False, stop=True), inc=sc_tn[par])
                pb.op("t", lambda E, c=c, h=h, par=par: E.matmul(
                    ps_S[par][:, 0:64], d.phiK[:, par, c * 128:(c + 1) * 128],
                    d.vE[:, c, h * 64:h * 64 + 64],
                    start=True, stop=False),
                    wait=[(sc_st[par], 16 * g + c)] if c > 0 else ())
                pb.op("t", lambda E, c=c, par=par: E.matmul(
                    ps_S[par][:, 64:65], d.phiK[:, par, c * 128:(c + 1) * 128],
                    d.onescol[:, :],
                    start=False, stop=True), inc=sc_ts[par])

                def supd(E, c=c, par=par):
                    E.drain()
                    if c == 0:
                        E.tensor_copy(d.S_acc[:, par, :], ps_S[par][:, 0:65])
                    else:
                        E.tensor_tensor(d.S_acc[:, par, :], d.S_acc[:, par, :],
                                        ps_S[par][:, 0:65], op=ALU.add)
                    E.drain()
                    return E.tensor_copy(d.S_sb[:, par, :], d.S_acc[:, par, :])
                pb.op("v", supd, wait=[(sc_ts[par], 16 * g + c + 1)],
                      inc=sc_st[par])

                def yev(E, c=c, par=par):
                    E.tensor_scalar(d.den[:, 4 * par:4 * par + 1],
                                    ps_N[par][:, 64:65], d.cst[:, 1:2], None,
                                    op0=ALU.add)
                    E.drain()
                    E.reciprocal(d.den[:, 4 * par + 1:4 * par + 2],
                                 d.den[:, 4 * par:4 * par + 1])
                    E.drain()
                    return E.tensor_scalar(
                        d.yF2[:, c, par * 64:par * 64 + 64],
                        ps_N[par][:, 0:64],
                        d.den[:, 4 * par + 1:4 * par + 2], None, op0=ALU.mult)
                pb.op("v", yev, wait=[(sc_tn[par], 16 * g + c + 1)],
                      inc=sc_yv[par])
        ps_Y = p.PSB[1]
        for r in range(2):
            for tt in range(8):
                t = r * 8 + tt
                pb.op("t", lambda E, t=t, tt=tt: E.transpose(
                    ps_Y[:, tt * 128:(tt + 1) * 128],
                    d.yF2[:, t, 0:128], d.identb[:, :]),
                    wait=[(sc_yv[0], 16 * (g + 1)), (sc_yv[1], 16 * (g + 1)),
                          (sc_yte, sc_yte.n)] if tt == 0 else (),
                    inc=sc_ytp if tt == 7 else None)
            pb.op("v", lambda E, r=r, dt=dt: E.tensor_copy(
                d.kD[:, dt, r * 1024:(r + 1) * 1024], ps_Y[:, 0:1024]),
                wait=[(sc_ytp, sc_ytp.n)], inc=sc_yte)
    pbs.append(pb)

    if "E" not in stages:
        return
    pb = PB(p)
    ew = p.edge("proj_w")
    em = p.edge("proj_m")
    ee = p.edge("proj_e")
    ea = p.edge("proj_a")
    ps_mm = [p.PSF[0], p.PSF[1]]
    yT = d.kD
    ee_b = ee.n
    ea_b = ea.n
    for kc in range(ND):
        pb.op("g", lambda E, kc=kc: E.dma_start(
            d.wW[:, kc, :], d.d_W_proj[kc * 128:(kc + 1) * 128, :]),
            inc=ew, inc_k=16)
    wt = ew.n
    nmm = 0
    for m in range(ND):
        for ch in range(4):
            pi = nmm % 2
            for kc in range(ND):
                w = [(ew, wt), (ee, ee_b + nmm - 1)] if kc == 0 else []
                pb.op("t", lambda E, m=m, ch=ch, kc=kc, pi=pi: E.matmul(
                    ps_mm[pi][:, :], d.wW[:, kc, m * 128:(m + 1) * 128],
                    yT[:, kc, ch * 512:(ch + 1) * 512],
                    start=(kc == 0), stop=(kc == ND - 1)),
                    wait=w, inc=em if kc == ND - 1 else None)
            bcol = POOL["b_proj"] + m
            pb.op("s", lambda E, pi=pi, bcol=bcol: E.activation(
                d.stmp[:, pi, :], ps_mm[pi][:, :], AF.Identity,
                bias=d.pool[:, bcol:bcol + 1]),
                wait=[(em, em.n), (ea, ea_b + nmm - 1)], inc=ee)
            pb.op("v", lambda E, m=m, ch=ch, pi=pi: E.tensor_tensor(
                d.arB[:, m, ch * 512:(ch + 1) * 512], d.stmp[:, pi, :],
                d.arA[:, m, ch * 512:(ch + 1) * 512], op=ALU.add),
                wait=[(ee, ee.n)], inc=ea)
            nmm += 1
    pbs.append(pb)

    if "F" not in stages:
        return
    pb = PB(p)
    emit_ln(p, pb,
            src=lambda dt: d.arB[:, dt, :],
            dst=lambda dt, ch: d.arA[:, dt, ch * 512:(ch + 1) * 512],
            name="p1ln")
    pbs.append(pb)



def alloc_p2(p, d):
    d.accum = p.sb("accum", [128, ND, T], F32)
    d.w1_sb = p.sb("w1_sb", [128, ND, 2048], BF16)
    d.w2_sb = p.sb("w2_sb", [128, 16, 1024], BF16)
    d.gateb = p.sb("gateb", [128, T], BF16)
    d.h_sb = p.sb("h_sb", [128, 16, 512], BF16)
    d.gbrow = p.sb("gbrow", [1, 4], BF16)
    d.gw_sb = p.sb("gw_sb", [128, ND, 4], BF16)
    d.wsel = p.sb("wsel", [128, NT, 4], F32)
    d.wselT = p.sb("wselT", [4, T], BF16)
    d.gtmp = p.sb("gtmp", [128, 8], F32)


def emit_p2(p, pbs, stages="GEL"):
    d = p.d
    pb = PB(p)
    eg = p.edge("gt_g")
    em = p.edge("gt_m")
    ev = p.edge("gt_v")
    es = p.edge("gt_s")
    etp = p.edge("gt_tp")
    ete = p.edge("gt_te")
    ps_l = p.PSF[5]
    ps_t = p.PSF[4]
    pb.op("g", lambda E: E.dma_start(
        d.gbrow[:, :], d.d_rows[0:1, ROWS["gate_b"]:ROWS["gate_b"] + 4]),
        inc=eg, inc_k=16)
    pb.op("g", lambda E: E.dma_start(
        d.gw_sb[:, :, :], d.d_gate_w.rearrange("(kc p) e -> p kc e", p=128)),
        inc=eg, inc_k=16)
    eg_t = eg.n
    ei = p.edge("fx_init")
    for dt in range(ND):
        pb.op("v", lambda E, dt=dt: E.tensor_copy(d.accum[:, dt, :], d.arA[:, dt, :]),
              inc=ei)
    for t in range(NT):
        for kc in range(ND):
            pb.op("t", lambda E, t=t, kc=kc: E.matmul(
                ps_l[:, (t % 2) * 4:(t % 2) * 4 + 4],
                d.arA[:, kc, t * 128:(t + 1) * 128], d.gw_sb[:, kc, :],
                start=(kc == 0), stop=False),
                wait=[(eg_t_dummy if False else eg, eg_t), (es, es.n)] if kc == 0 else ())
        pb.op("t", lambda E, t=t: E.matmul(
            ps_l[:, (t % 2) * 4:(t % 2) * 4 + 4], d.ones1[:, :],
            d.gbrow[0:1, 0:4],
            start=False, stop=True), inc=em)

        def soft(E, t=t):
            q = ps_l[:, (t % 2) * 4:(t % 2) * 4 + 4]
            E.reduce_max(d.gtmp[:, 0:1], q, axis=AX.X)
            E.drain()
            return E.tensor_scalar(d.gtmp[:, 1:2], d.gtmp[:, 0:1], -1.0, None,
                                   op0=ALU.mult)
        pb.op("v", soft, wait=[(em, em.n)], drain=True, inc=ev)
        pb.op("s", lambda E, t=t: E.activation(
            d.wsel[:, t, :], ps_l[:, (t % 2) * 4:(t % 2) * 4 + 4],
            AF.Exp, bias=d.gtmp[:, 1:2], accum_out=d.gtmp[:, 2:3]),
            wait=[(ev, ev.n)], drain=True, inc=es)

        def fin(E, t=t):
            E.reciprocal(d.gtmp[:, 3:4], d.gtmp[:, 2:3])
            E.drain()
            E.tensor_scalar(d.wsel[:, t, :], d.wsel[:, t, :], d.gtmp[:, 3:4], None,
                            op0=ALU.mult)
            E.drain()
            E.reduce_max(d.gtmp[:, 4:5], d.wsel[:, t, :], axis=AX.X)
            E.drain()
            E.tensor_scalar(d.ln_tmp[:, 0:4], d.wsel[:, t, :], d.gtmp[:, 4:5], None,
                            op0=ALU.is_ge)
            E.drain()
            E.scalar_tensor_tensor(d.ln_tmp[:, 4:8], d.ln_tmp[:, 0:4], -2.0,
                                   d.wsel[:, t, :], op0=ALU.mult, op1=ALU.add)
            E.drain()
            E.reduce_max(d.gtmp[:, 6:7], d.ln_tmp[:, 4:8], axis=AX.X)
            E.drain()
            E.tensor_scalar(d.ln_tmp[:, 0:4], d.wsel[:, t, :], d.gtmp[:, 6:7], None,
                            op0=ALU.is_ge)
            E.drain()
            return E.tensor_tensor(d.wsel[:, t, :], d.wsel[:, t, :], d.ln_tmp[:, 0:4],
                                   op=ALU.mult)
        pb.op("v", fin, wait=[(es, es.n)], drain=True, inc=ev)
    for t in range(NT):
        pb.op("t", lambda E, t=t: E.transpose(
            ps_t[0:4, (t % 4) * 128:(t % 4) * 128 + 128],
            d.wsel[:, t, :], d.identf[:, :]),
            wait=[(ev, ev.n), (ete, ete.n)] if t % 4 == 0 else (),
            inc=etp if t % 4 == 3 else None)
        if t % 4 == 3:
            pb.op("s", lambda E, t=t: E.activation(
                d.wselT[0:4, (t - 3) * 128:(t + 1) * 128], ps_t[0:4, 0:512], AF.Copy),
                wait=[(etp, etp.n)], inc=ete)
    pbs.append(pb)

    if hasattr(d, "d_dbg_wsel"):
        dpb = PB(p)
        eD = p.edge("p2dbg1")
        ev_g = p.edge("gt_v")
        dpb.op("y", lambda E: E.dma_start(d.d_dbg_wsel[:, :, :], d.wsel[:, :, :]),
               wait=[(ev_g, ev_g.n)], inc=eD, inc_k=16)
        dpb.op("y", lambda E: E.nop(), wait=[(eD, eD.n)])
        pbs.append(dpb)

    if "E" not in stages:
        return
    pb = PB(p)
    e_w1a = p.edge("fx_w1a")
    e_w1b = p.edge("fx_w1b")
    e_w2 = p.edge("fx_w2")
    e_gb = p.edge("fx_gb")
    e_gm = p.edge("fx_gm")
    e_hm = p.edge("fx_hm")
    e_he = p.edge("fx_he")
    e_ym = p.edge("fx_ym")
    e_ye = p.edge("fx_ye")
    ps_h = p.psh2
    ps_y = p.psy2
    experts = [
        ("s", 0, "d_sw1", "d_sw2", "sb1_0", "sb2_0"),
        ("s", 1, "d_sw1", "d_sw2", "sb1_1", "sb2_1"),
        ("r", 0, "d_rw1", "d_rw2", "rb1_0", "rb2_0"),
        ("r", 1, "d_rw1", "d_rw2", "rb1_1", "rb2_1"),
        ("r", 2, "d_rw1", "d_rw2", "rb1_2", "rb2_2"),
        ("r", 3, "d_rw1", "d_rw2", "rb1_3", "rb2_3"),
    ]
    gyu = 0
    ye_b = e_ye.n
    hm_a = 0
    ete_g = p.edge("gt_te")
    first_h = True
    for ei_, (kind, xi, w1n, w2n, b1n, b2n) in enumerate(experts):
        w1_d = getattr(d, w1n)
        w2_d = getattr(d, w2n)
        if kind == "s":
            pb.op("g", lambda E: E.memset(d.gateb[:, :], 0.5),
                  wait=[(e_ye, e_ye.n)], inc=e_gb)
        else:
            for c4 in range(4):
                pb.op("t", lambda E, c4=c4, xi=xi: E.matmul(
                    ps_y[c4][:, 0:512], d.bcsel8[0:4, xi * 128:xi * 128 + 128],
                    d.wselT[0:4, c4 * 512:(c4 + 1) * 512]),
                    wait=[(ete_g, ete_g.n), (e_ye, e_ye.n)] if c4 == 0 else (),
                    inc=e_gm)
            for c4 in range(4):
                pb.op("v", lambda E, c4=c4: E.tensor_copy(
                    d.gateb[:, c4 * 512:(c4 + 1) * 512], ps_y[c4][:, 0:512]),
                    wait=[(e_gm, e_gm.n)] if c4 == 0 else (), inc=e_gb)
        gb_t = e_gb.n
        for fh in range(2):
            for half, e_wh, hm_tgt in ((0, e_w1a, hm_a), (1, e_w1b, e_hm.n)):
                for kc in range(ND):
                    pb.op("g", lambda E, kc=kc, fh=fh, half=half, w1_d=w1_d, xi=xi:
                          E.dma_start(
                        d.w1_sb[:, kc, half * 1024:(half + 1) * 1024],
                        w1_d[xi, kc * 128:(kc + 1) * 128,
                             fh * 2048 + half * 1024:fh * 2048 + (half + 1) * 1024]),
                        wait=[(e_hm, hm_tgt)] if kc == 0 else (),
                        inc=e_wh, inc_k=16)
            for fc in range(16):
                pb.op("g", lambda E, fc=fc, fh=fh, w2_d=w2_d, xi=xi: E.dma_start(
                    d.w2_sb[:, fc, :],
                    w2_d[xi, fh * 2048 + fc * 128:fh * 2048 + (fc + 1) * 128, :]),
                    wait=[(e_ym, e_ym.n)] if fc == 0 else (),
                    inc=e_w2, inc_k=16)
            wt_a = e_w1a.n
            wt_b = e_w1b.n
            wt_2 = e_w2.n
            for ci, (cs, W) in enumerate(CHUNKS):
                for ft in range(16):
                    for kc in range(ND):
                        wv = ()
                        if kc == 0:
                            wv = [(e_w1a if ft < 8 else e_w1b,
                                   wt_a if ft < 8 else wt_b),
                                  (e_he, e_he.n - 3)]
                            if first_h:
                                wv.append((ete_g, 4))
                                first_h = False
                        pb.op("t", lambda E, ft=ft, kc=kc, cs=cs, W=W: E.matmul(
                            ps_h[ft % 4][:, 0:W],
                            d.w1_sb[:, kc, ft * 128:(ft + 1) * 128],
                            d.arA[:, kc, cs:cs + W],
                            start=(kc == 0), stop=(kc == ND - 1)),
                            wait=wv, inc=e_hm if kc == ND - 1 else None)
                    if ci == 3 and ft == 7:
                        hm_a = e_hm.n
                    bcol = POOL[b1n] + fh * 16 + ft
                    wv = [(e_hm, e_hm.n)]
                    if ft == 0:
                        wv.append((e_ym, e_ym.n))
                    pb.op("s", lambda E, ft=ft, bcol=bcol, W=W: E.activation(
                        d.h_sb[:, ft, 0:W], ps_h[ft % 4][:, 0:W], AF.Relu,
                        bias=d.pool[:, bcol:bcol + 1]),
                        wait=wv, inc=e_he)
                for ot in range(8):
                    wv = [(e_ye, ye_b + gyu - 3)]
                    if gyu == 0:
                        wv.append((ete_g, 4))
                    if ot == 0:
                        wv.append((e_he, e_he.n))
                        wv.append((e_w2, wt_2))
                        wv.append((e_gb, gb_t))
                    for ft in range(16):
                        last = (ft == 15)
                        pb.op("t", lambda E, ft=ft, ot=ot, last=last, W=W: E.matmul(
                            ps_y[ot % 4][:, 0:W],
                            d.w2_sb[:, ft, ot * 128:(ot + 1) * 128],
                            d.h_sb[:, ft, 0:W],
                            start=(ft == 0), stop=last),
                            wait=wv if ft == 0 else (),
                            inc=e_ym if last else None)
                    b2c = POOL[b2n] + ot

                    def yev(E, ot=ot, cs=cs, b2c=b2c, fh=fh, W=W):
                        r = None
                        for pc, w in (((0, 512), (512, W - 512)) if W > 512
                                      else ((0, 512),)):
                            c0 = cs + pc
                            if fh == 0:
                                E.scalar_tensor_tensor(
                                    d.ln_tmp[:, 0:w], ps_y[ot % 4][:, pc:pc + w],
                                    d.pool[:, b2c:b2c + 1], d.gateb[:, c0:c0 + w],
                                    op0=ALU.add, op1=ALU.mult)
                            else:
                                E.tensor_tensor(d.ln_tmp[:, 0:w],
                                                ps_y[ot % 4][:, pc:pc + w],
                                                d.gateb[:, c0:c0 + w], op=ALU.mult)
                            E.drain()
                            r = E.tensor_tensor(
                                d.accum[:, ot, c0:c0 + w],
                                d.accum[:, ot, c0:c0 + w],
                                d.ln_tmp[:, 0:w], op=ALU.add)
                            if pc == 0 and W > 512:
                                E.drain()
                        return r
                    pb.op("v", yev,
                          wait=[(e_ym, e_ym.n),
                                (p.edge("fx_init"), p.edge("fx_init").n)],
                          drain=True, inc=e_ye)
                    gyu += 1
    pbs.append(pb)

    if hasattr(d, "d_dbg_accum"):
        dpb = PB(p)
        eD = p.edge("p2dbg2")
        for dt in range(ND):
            dpb.op("y", lambda E, dt=dt: E.dma_start(
                d.d_dbg_accum[:, dt, :], d.accum[:, dt, :]),
                wait=[(e_ye, e_ye.n)] if dt == 0 else (), inc=eD, inc_k=16)
        dpb.op("y", lambda E: E.nop(), wait=[(eD, eD.n)])
        pbs.append(dpb)

    if "L" not in stages:
        return
    pb = PB(p)
    emit_ln(p, pb,
            src=lambda dt: d.accum[:, dt, :],
            dst=lambda dt, ch: d.arA[:, dt, ch * 512:(ch + 1) * 512],
            name="p2ln", src_f32=True,
            prewait=[(e_ye, e_ye.n)])
    pbs.append(pb)


def alloc_p3_outer(p, d):
    d.maskC = p.sb("maskC_sb", [128, 3584], BF16)
    d.aT = p.sb("aT", [128, ND, T], BF16)
    d.p_sb = p.sb("p_sb", [128, 3, 768], BF16)
    d.pt_sb = p.sb("pt_sb", [128, 2, 768], BF16)
    d.mx = p.sb("mx", [128, 8], F32)
    d.rows_bv3 = p.sb("rows_bv3", [1, 1024], BF16)


def alloc_p3_attn(p, d):
    d.wQ = p.sb("wQ", [128, ND, 1024], BF16)
    d.qT3 = p.sb("qT3", [128, ND, T], BF16)
    d.kT3 = p.sb("kT3", [128, ND, T], BF16)
    d.vE3 = p.sb("vE3", [128, NT, D], BF16)


def alloc_p3_s1(p, d):
    d.wO = p.sb("wO", [128, ND, 1024], BF16)
    d.s1 = p.sb("s1", [128, ND, T], BF16)
    d.stmp3 = p.sb("stmp3", [128, 2, 512], F32)


def alloc_p3_ffn(p, d):
    d.accum = p.sb("accum3", [128, ND, T], F32)
    d.chalf = p.sb("chalf", [128, 32], F32)
    d.w1_sb = p.sb("w1_sb3", [128, ND, 1024], BF16)
    d.w2_sb = p.sb("w2_sb3", [128, 8, 1024], BF16)
    d.h_sb = p.sb("h_sb3", [128, 8, 512], BF16)
    d.z_sb = p.sb("z_sb3", [128, 2, 512], F32)
    d.u_sb = p.sb("u_sb3", [128, 512], F32)


def emit_p3_attn(p, pbs):
    d = p.d
    pb = PB(p)
    ew = p.edge("p3b_w")
    em = p.edge("p3b_m")
    ee = p.edge("p3b_e")
    erow = p.edge("p3b_row")
    eM = p.edge("p3b_mask")
    ps_mm = [p.ps3_sc[:, 0:512], p.ps3_pt[:, 0:512]]
    pb.op("g", lambda E: E.dma_start(d.maskC[:, :], d.d_maskC[:, :]), inc=eM, inc_k=16)
    pb.op("g", lambda E: E.dma_start(
        d.rows_bv3[:, :], d.d_rows[0:1, ROWS["bv"]:ROWS["bv"] + 1024]),
        inc=erow, inc_k=16)
    nmm = 0
    specs = [("d_Wq", d.qT3, POOL["bq"]), ("d_Wk", d.kT3, POOL["bk"]),
             ("d_Wv", None, None)]
    for part, (wname, dst_ar, bpool) in enumerate(specs):
        w_d = getattr(d, wname)
        for kc in range(ND):
            pb.op("g", lambda E, kc=kc, w_d=w_d: E.dma_start(
                d.wQ[:, kc, :], w_d[kc * 128:(kc + 1) * 128, :]),
                wait=[(em, part * 32)] if kc == 0 else (), inc=ew, inc_k=16)
        if part < 2:
            for m in range(ND):
                for ch in range(4):
                    pi = nmm % 2
                    for kc in range(ND):
                        w = [(ew, (part + 1) * ND * 16), (ee, nmm - 1)] if kc == 0 else []
                        pb.op("t", lambda E, m=m, ch=ch, kc=kc, pi=pi: E.matmul(
                            ps_mm[pi], d.wQ[:, kc, m * 128:(m + 1) * 128],
                            d.arA[:, kc, ch * 512:(ch + 1) * 512],
                            start=(kc == 0), stop=(kc == ND - 1)),
                            wait=w, inc=em if kc == ND - 1 else None)
                    pb.op("s", lambda E, pi=pi, m=m, ch=ch, dst_ar=dst_ar, bpool=bpool:
                          E.activation(dst_ar[:, m, ch * 512:(ch + 1) * 512],
                                       ps_mm[pi], AF.Identity,
                                       bias=d.pool[:, bpool + m:bpool + m + 1]),
                          wait=[(em, em.n)], inc=ee)
                    nmm += 1
        else:
            for t in range(NT):
                for ch in range(2):
                    pi = nmm % 2
                    for kc in range(ND):
                        w = [(ew, 3 * ND * 16), (ee, nmm - 1)] if kc == 0 else []
                        pb.op("t", lambda E, t=t, ch=ch, kc=kc, pi=pi: E.matmul(
                            ps_mm[pi], d.arA[:, kc, t * 128:(t + 1) * 128],
                            d.wQ[:, kc, ch * 512:(ch + 1) * 512],
                            start=(kc == 0), stop=False),
                            wait=w)
                    pb.op("t", lambda E, ch=ch, pi=pi: E.matmul(
                        ps_mm[pi], d.ones1[:, :],
                        d.rows_bv3[0:1, ch * 512:ch * 512 + 512],
                        start=False, stop=True),
                        wait=[(erow, 16)], inc=em)
                    pb.op("s", lambda E, t=t, ch=ch, pi=pi: E.activation(
                        d.vE3[:, t, ch * 512:(ch + 1) * 512], ps_mm[pi], AF.Copy),
                        wait=[(em, em.n)], inc=ee)
                    nmm += 1
    pbs.append(pb)

    pb = PB(p)
    e_sc = p.edge("p3c_sc")
    e_ex = p.edge("p3c_ex")
    e_mk = p.edge("p3c_mk")
    e_rc = p.edge("p3c_rc")
    e_ptm = p.edge("p3c_ptm")
    e_pte = p.edge("p3c_pte")
    e_pv = p.edge("p3c_pv")
    e_ae = p.edge("p3c_ae")
    e_qkv3 = p.edge("p3b_e")
    qkv_done = e_qkv3.n
    NB = 8
    units = []
    for h in range(NHEAD):
        for n in range(NB):
            for qs in range(2):
                units.append((h, n, qs))
    NHU = len(units)

    def u_geom(k):
        h, n, qs = units[k]
        dt = h // 2
        ho = (h % 2) * 64
        kbs = [kb for kb in (n - 1, n, n + 1) if 0 <= kb < NB]
        if n == 0:
            moff = 1536 + qs * 512
        elif n == NB - 1:
            moff = 2560 + qs * 512
        else:
            moff = qs * 768
        return h, n, qs, dt, ho, kbs, moff

    def st_scores(k):
        h, n, qs, dt, ho, kbs, moff = u_geom(k)
        base = kbs[0] * 256
        W = 256 * len(kbs)
        segs = [(0, 512)] + ([(512, 256)] if W == 768 else [])
        for si, (c0, w) in enumerate(segs):
            wv = []
            if si == 0:
                wv = [(e_ex, k - 1), (e_qkv3, qkv_done if k == 0 else 0)]
            pb.op("t", lambda E, k=k, c0=c0, w=w, dt=dt, ho=ho, n=n, qs=qs,
                  base=base: E.matmul(
                p.ps3_scb[k % 2][:, c0:c0 + w],
                d.qT3[ho:ho + 64, dt, (n * 2 + qs) * 128:(n * 2 + qs) * 128 + 128],
                d.kT3[ho:ho + 64, dt, base + c0:base + c0 + w]),
                wait=wv, inc=e_sc if si == len(segs) - 1 else None)

    def st_exp(k):
        W = 256 * len(u_geom(k)[5])
        pb.op("s", lambda E, k=k, W=W: E.activation(
            d.p_sb[:, k % 3, 0:W], p.ps3_scb[k % 2][:, 0:W], AF.Exp, scale=0.125),
            wait=[(e_sc, k + 1), (e_ptm, k - 2)], inc=e_ex)

    def st_masknorm(k):
        _, _, _, _, _, kbs, moff = u_geom(k)
        W = 256 * len(kbs)
        sl = k % 3
        pb.op("v", lambda E, W=W, moff=moff, sl=sl: E.scalar_tensor_tensor(
            d.p_sb[:, sl, 0:W], d.p_sb[:, sl, 0:W], 1.0,
            d.maskC[:, moff:moff + W], op0=ALU.mult, op1=ALU.mult,
            accum_out=d.mx[:, sl:sl + 1]),
            wait=[(e_ex, k + 1)] + ([(p.edge("p3b_mask"), p.edge("p3b_mask").n)]
                                    if k == 0 else []),
            inc=e_mk)

        def vnorm(E, W=W, sl=sl):
            E.reciprocal(d.mx[:, 4 + sl:5 + sl], d.mx[:, sl:sl + 1])
            E.drain()
            return E.tensor_scalar(d.p_sb[:, sl, 0:W], d.p_sb[:, sl, 0:W],
                                   d.mx[:, 4 + sl:5 + sl], None, op0=ALU.mult)
        pb.op("v", vnorm, wait=[(e_mk, k + 1)], drain=True, inc=e_rc)

    def st_transp(k):
        nk2 = 2 * len(u_geom(k)[5])
        for j2 in range(nk2):
            pb.op("t", lambda E, k=k, j2=j2: E.transpose(
                p.ps3_ptb[k % 2][:, j2 * 128:j2 * 128 + 128],
                d.p_sb[:, k % 3, j2 * 128:j2 * 128 + 128], d.identb[:, :]),
                wait=[(e_rc, k + 1), (e_pte, k - 1)] if j2 == 0 else (),
                inc=e_ptm if j2 == nk2 - 1 else None)

    def st_evacpt(k):
        nk2 = 2 * len(u_geom(k)[5])
        pb.op("s", lambda E, k=k, nk2=nk2: E.activation(
            d.pt_sb[:, k % 2, 0:nk2 * 128], p.ps3_ptb[k % 2][:, 0:nk2 * 128],
            AF.Copy),
            wait=[(e_ptm, k + 1), (e_pv, k - 1)], inc=e_pte)

    def st_pv(k):
        h, n, qs, dt, ho, kbs, _ = u_geom(k)
        nk2 = 2 * len(kbs)
        for j2 in range(nk2):
            kt = kbs[j2 // 2] * 2 + (j2 % 2)
            pb.op("t", lambda E, k=k, j2=j2, kt=kt, h=h, ho=ho, nk2=nk2: E.matmul(
                p.ps3_pvb[k % 2][ho:ho + 64, 0:128],
                d.vE3[:, kt, h * 64:h * 64 + 64],
                d.pt_sb[:, k % 2, j2 * 128:j2 * 128 + 128],
                start=(j2 == 0), stop=(j2 == nk2 - 1)),
                wait=[(e_pte, k + 1), (e_ae, k - 1)] if j2 == 0 else (),
                inc=e_pv if j2 == nk2 - 1 else None)

    def st_evaca(k):
        h, n, qs, dt, ho, _, _ = u_geom(k)
        pb.op("s", lambda E, k=k, h=h, n=n, qs=qs, dt=dt, ho=ho: E.activation(
            d.aT[ho:ho + 64, dt, (n * 2 + qs) * 128:(n * 2 + qs) * 128 + 128],
            p.ps3_pvb[k % 2][ho:ho + 64, 0:128], AF.Copy),
            wait=[(e_pv, k + 1)], inc=e_ae)

    stages = [st_scores, st_exp, st_masknorm, st_transp, st_evacpt, st_pv,
              st_evaca]
    for it in range(NHU + len(stages) - 1):
        for si, st in enumerate(stages):
            k = it - si
            if 0 <= k < NHU:
                st(k)
    pbs.append(pb)


def emit_p3_s1(p, pbs):
    d = p.d
    pb = PB(p)
    ew = p.edge("p3d_w")
    em = p.edge("p3d_m")
    ee = p.edge("p3d_e")
    ea = p.edge("p3d_a")
    ps_mm = [p.PSF[0], p.PSF[1]]
    for kc in range(ND):
        pb.op("g", lambda E, kc=kc: E.dma_start(
            d.wO[:, kc, :], d.d_Wo[kc * 128:(kc + 1) * 128, :]),
            inc=ew, inc_k=16)
    nmm = 0
    eaT = p.edge("p3c_ae")
    for m in range(ND):
        for ch in range(4):
            pi = nmm % 2
            for kc in range(ND):
                w = [(ew, ND * 16), (ee, nmm - 1)] if kc == 0 else []
                if nmm == 0 and kc == 0:
                    w.append((eaT, eaT.n))
                pb.op("t", lambda E, m=m, ch=ch, kc=kc, pi=pi: E.matmul(
                    ps_mm[pi][:, :], d.wO[:, kc, m * 128:(m + 1) * 128],
                    d.aT[:, kc, ch * 512:(ch + 1) * 512],
                    start=(kc == 0), stop=(kc == ND - 1)),
                    wait=w, inc=em if kc == ND - 1 else None)
            bcol = POOL["bo"] + m
            pb.op("s", lambda E, pi=pi, bcol=bcol: E.activation(
                d.stmp3[:, pi, :], ps_mm[pi][:, :], AF.Identity,
                bias=d.pool[:, bcol:bcol + 1]),
                wait=[(em, em.n), (ea, nmm - 1)], inc=ee)
            pb.op("v", lambda E, m=m, ch=ch, pi=pi: E.tensor_tensor(
                d.s1[:, m, ch * 512:(ch + 1) * 512], d.stmp3[:, pi, :],
                d.arA[:, m, ch * 512:(ch + 1) * 512], op=ALU.add),
                wait=[(ee, ee.n)], drain=True, inc=ea)
            nmm += 1
    pbs.append(pb)
    pb = PB(p)
    emit_ln(p, pb,
            src=lambda dt: d.s1[:, dt, :],
            dst=lambda dt, ch: d.aT[:, dt, ch * 512:(ch + 1) * 512],
            gain=POOL["g1"], bias=POOL["b1"], name="p3ln1",
            prewait=[(p.edge("p3d_a"), p.edge("p3d_a").n)])
    pbs.append(pb)


def emit_p3_ffn(p, pbs):
    d = p.d
    pb = PB(p)
    e_w = p.edge("p3f_w")
    e_hm = p.edge("p3f_hm")
    e_ze = p.edge("p3f_ze")
    e_gl = p.edge("p3f_gl")
    e_th = p.edge("p3f_th")
    e_ym = p.edge("p3f_ym")
    e_ye = p.edge("p3f_ye")
    ei = p.edge("p3f_init")
    ps_h = p.psh2
    ps_y = p.psy2
    ech = p.edge("p3f_ch")
    pb.op("v", lambda E: E.tensor_scalar(
        d.chalf[:, :], d.pool[:, POOL["c1"]:POOL["c1"] + 32], 0.5, None, op0=ALU.mult),
        inc=ech)
    for dt in range(ND):
        pb.op("v", lambda E, dt=dt: E.tensor_copy(d.accum[:, dt, :], d.aT[:, dt, :]),
              wait=[(p.edge("ln_ap"), p.edge("ln_ap").n)] if dt == 0 else (),
              inc=ei)
    gyu = 0
    yb4 = e_ye.n
    for q4 in range(4):
        for kc in range(ND):
            pb.op("g", lambda E, kc=kc, q4=q4: E.dma_start(
                d.w1_sb[:, kc, :],
                d.d_W1[kc * 128:(kc + 1) * 128, q4 * 1024:(q4 + 1) * 1024]),
                wait=[(e_ym, e_ym.n)] if kc == 0 else (), inc=e_w, inc_k=16)
        for fc in range(8):
            pb.op("g", lambda E, fc=fc, q4=q4: E.dma_start(
                d.w2_sb[:, fc, :],
                d.d_W2[q4 * 1024 + fc * 128:q4 * 1024 + (fc + 1) * 128, :]),
                inc=e_w, inc_k=16)
        wtarget = e_w.n
        for tp in range(4):
            cs = tp * 512
            for ft in range(8):
                for kc in range(ND):
                    wv2 = []
                    if kc == 0:
                        wv2 = [(e_w, wtarget), (e_ze, e_ze.n - 3)]
                        if q4 == 0 and tp == 0 and ft == 0:
                            wv2.append((p.edge("ln_ap"),
                                        p.edge("ln_ap").n))
                    pb.op("t", lambda E, ft=ft, kc=kc, cs=cs: E.matmul(
                        ps_h[ft % 4][:, 0:512],
                        d.w1_sb[:, kc, ft * 128:(ft + 1) * 128],
                        d.aT[:, kc, cs:cs + 512],
                        start=(kc == 0), stop=(kc == ND - 1)),
                        wait=wv2,
                        inc=e_hm if kc == ND - 1 else None)
                bcol = POOL["c1"] + q4 * 8 + ft
                wv = [(e_hm, e_hm.n), (e_gl, e_gl.n - 2)]
                if ft == 0:
                    wv.append((e_ym, e_ym.n))
                pb.op("s", lambda E, ft=ft, bcol=bcol: E.activation(
                    d.z_sb[:, ft % 2, :], ps_h[ft % 4][:, 0:512], AF.Identity,
                    scale=0.5, bias=d.chalf[:, bcol - POOL["c1"]:bcol - POOL["c1"] + 1]),
                    wait=wv + [(ech, 1)], inc=e_ze)

                def gel1(E, ft=ft):
                    E.drain()
                    E.tensor_tensor(d.u_sb[:, :], d.z_sb[:, ft % 2, :],
                                    d.z_sb[:, ft % 2, :], op=ALU.mult)
                    E.drain()
                    E.tensor_scalar(d.u_sb[:, :], d.u_sb[:, :], 0.2854192435,
                                    1.5957691216, op0=ALU.mult, op1=ALU.add)
                    E.drain()
                    return E.tensor_tensor(d.u_sb[:, :], d.u_sb[:, :],
                                           d.z_sb[:, ft % 2, :], op=ALU.mult)
                pb.op("v", gel1, wait=[(e_ze, e_ze.n)], drain=True, inc=e_gl)
                pb.op("s", lambda E: E.activation(d.u_sb[:, :], d.u_sb[:, :], AF.Tanh),
                      wait=[(e_gl, e_gl.n)], drain=True, inc=e_th)
                pb.op("v", lambda E, ft=ft: E.scalar_tensor_tensor(
                    d.h_sb[:, ft, :], d.u_sb[:, :], 1.0, d.z_sb[:, ft % 2, :],
                    op0=ALU.add, op1=ALU.mult),
                    wait=[(e_th, e_th.n)], drain=True, inc=e_gl)
            for ot in range(8):
                wv = [(e_ye, yb4 + gyu - 3)]
                if ot == 0:
                    wv.append((e_gl, e_gl.n))
                for ft in range(8):
                    last = (ft == 7)
                    pb.op("t", lambda E, ft=ft, ot=ot, last=last: E.matmul(
                        ps_y[ot % 4][:, 0:512],
                        d.w2_sb[:, ft, ot * 128:(ot + 1) * 128],
                        d.h_sb[:, ft, :],
                        start=(ft == 0), stop=last),
                        wait=wv if ft == 0 else (),
                        inc=e_ym if last else None)

                def yev(E, ot=ot, cs=cs, q4=q4):
                    E.drain()
                    if q4 == 0:
                        c2c = POOL["c2"] + ot
                        return E.scalar_tensor_tensor(
                            d.accum[:, ot, cs:cs + 512], ps_y[ot % 4][:, 0:512],
                            d.pool[:, c2c:c2c + 1], d.accum[:, ot, cs:cs + 512],
                            op0=ALU.add, op1=ALU.add)
                    return E.tensor_tensor(
                        d.accum[:, ot, cs:cs + 512], d.accum[:, ot, cs:cs + 512],
                        ps_y[ot % 4][:, 0:512], op=ALU.add)
                pb.op("v", yev, wait=[(e_ym, e_ym.n), (ei, ND)],
                      drain=True, inc=e_ye)
                gyu += 1
    pbs.append(pb)
    pb = PB(p)
    emit_ln(p, pb,
            src=lambda dt: d.accum[:, dt, :],
            dst=lambda dt, ch: d.aT[:, dt, ch * 512:(ch + 1) * 512],
            gain=POOL["g2"], bias=POOL["b2"], name="p3ln2", src_f32=True,
            prewait=[(p.edge("p3f_ye"), p.edge("p3f_ye").n)])
    pbs.append(pb)
    pb = PB(p)
    e2 = p.edge("p3_s2")
    for dt in range(ND):
        pb.op("v", lambda E, dt=dt: E.tensor_tensor(
            d.accum[:, dt, :], d.arA[:, dt, :], d.aT[:, dt, :], op=ALU.add),
            wait=[(p.edge("ln_ap"), p.edge("ln_ap").n)] if dt == 0 else (),
            drain=True, inc=e2)
    emit_ln(p, pb,
            src=lambda dt: d.accum[:, dt, :],
            dst=lambda dt, ch: d.arA[:, dt, ch * 512:(ch + 1) * 512],
            name="p3ln3", src_f32=True,
            prewait=[(e2, ND)])
    pbs.append(pb)


def alloc_p4(p, d):
    d.accum = p.sb("accum4", [128, ND, T], F32)
    d.w1_sb = p.sb("w1_sb4", [128, ND, 2048], BF16)
    d.w2_sb = p.sb("w2_sb4", [128, 16, 1024], BF16)
    d.gateb = p.sb("gateb4", [128, 512], BF16)
    d.h_sb = p.sb("h_sb4", [128, 16, 512], BF16)
    d.cw_sb = p.sb("cw_sb", [128, ND, 8], BF16)
    d.wsel4 = p.sb("wsel4", [128, NT, 8], F32)
    d.probsT = p.sb("probsT", [8, T], BF16)
    d.wT8 = p.sb("wT8", [8, T], BF16)
    d.bs = p.sb("bs", [8, 8], F32)
    d.gtmp4 = p.sb("gtmp4", [128, 8], F32)


def emit_p4(p, pbs):
    d = p.d
    pb = PB(p)
    eg = p.edge("gt_g")
    em = p.edge("gt_m")
    ev = p.edge("gt_v")
    es = p.edge("gt_s")
    etp = p.edge("gt_tp")
    ete = p.edge("gt_te")
    ebi = p.edge("gt_bi")
    ps_l = p.PSF[5]
    ps_t = p.PSF[4]
    pb.op("g", lambda E: E.dma_start(
        d.cw_sb[:, :, :], d.d_choice_wT.rearrange("(kc p) e -> p kc e", p=128)),
        inc=eg, inc_k=16)
    eg_t = eg.n
    ebsi = p.edge("p4_bsi")
    pb.op("g", lambda E: E.memset(d.bs[:, 0:1], 0.0))
    pb.op("g", lambda E: E.memset(d.bs[:, 1:2], 1.0), inc=ebsi)
    ei = p.edge("fx_init")
    for dt in range(ND):
        pb.op("v", lambda E, dt=dt: E.tensor_copy(d.accum[:, dt, :], d.arA[:, dt, :]),
              inc=ei)
    for t in range(NT):
        for kc in range(ND):
            pb.op("t", lambda E, t=t, kc=kc: E.matmul(
                ps_l[:, (t % 2) * 8:(t % 2) * 8 + 8],
                d.arA[:, kc, t * 128:(t + 1) * 128], d.cw_sb[:, kc, :],
                start=(kc == 0), stop=(kc == ND - 1)),
                wait=[(eg, eg_t), (es, es.n)] if kc == 0 else (),
                inc=em if kc == ND - 1 else None)

        def soft(E, t=t):
            q = ps_l[:, (t % 2) * 8:(t % 2) * 8 + 8]
            E.reduce_max(d.gtmp4[:, 0:1], q, axis=AX.X)
            E.drain()
            return E.tensor_scalar(d.gtmp4[:, 1:2], d.gtmp4[:, 0:1], -1.0, None,
                                   op0=ALU.mult)
        pb.op("v", soft, wait=[(em, em.n)], drain=True, inc=ev)
        pb.op("s", lambda E, t=t: E.activation(
            d.wsel4[:, t, :], ps_l[:, (t % 2) * 8:(t % 2) * 8 + 8],
            AF.Exp, bias=d.gtmp4[:, 1:2], accum_out=d.gtmp4[:, 2:3]),
            wait=[(ev, ev.n)], drain=True, inc=es)

        def fin(E, t=t):
            E.reciprocal(d.gtmp4[:, 3:4], d.gtmp4[:, 2:3])
            E.drain()
            return E.tensor_scalar(d.wsel4[:, t, :], d.wsel4[:, t, :],
                                   d.gtmp4[:, 3:4], None, op0=ALU.mult)
        pb.op("v", fin, wait=[(es, es.n)], drain=True, inc=ev)
    for t in range(NT):
        pb.op("t", lambda E, t=t: E.transpose(
            ps_t[0:8, (t % 4) * 128:(t % 4) * 128 + 128],
            d.wsel4[:, t, :], d.identf[:, :]),
            wait=[(ev, ev.n), (ete, ete.n)] if t % 4 == 0 else (),
            inc=etp if t % 4 == 3 else None)
        if t % 4 == 3:
            pb.op("s", lambda E, t=t: E.activation(
                d.probsT[0:8, (t - 3) * 128:(t + 1) * 128], ps_t[0:8, 0:512], AF.Copy),
                wait=[(etp, etp.n)], inc=ete)
    def bisect(E):
        for it in range(20):
            E.drain()
            E.tensor_tensor(d.bs[:, 2:3], d.bs[:, 0:1], d.bs[:, 1:2], op=ALU.add)
            E.drain()
            E.tensor_scalar(d.bs[:, 3:4], d.bs[:, 2:3], 0.5, None, op0=ALU.mult)
            E.drain()
            E.tensor_scalar(d.wT8[0:8, :], d.probsT[0:8, :], d.bs[:, 3:4][0:8, :],
                            0.0, op0=ALU.is_ge, op1=ALU.add,
                            accum_out=d.bs[:, 4:5])
            E.drain()
            E.tensor_scalar(d.bs[:, 5:6], d.bs[:, 4:5], 512.0, None, op0=ALU.is_ge)
            E.drain()
            E.tensor_tensor(d.bs[:, 6:7], d.bs[:, 3:4], d.bs[:, 0:1], op=ALU.subtract)
            E.drain()
            E.tensor_tensor(d.bs[:, 6:7], d.bs[:, 6:7], d.bs[:, 5:6], op=ALU.mult)
            E.drain()
            E.tensor_tensor(d.bs[:, 0:1], d.bs[:, 0:1], d.bs[:, 6:7], op=ALU.add)
            E.drain()
            E.tensor_scalar(d.bs[:, 7:8], d.bs[:, 5:6], -1.0, 1.0, op0=ALU.mult,
                            op1=ALU.add)
            E.drain()
            E.tensor_tensor(d.bs[:, 6:7], d.bs[:, 3:4], d.bs[:, 1:2], op=ALU.subtract)
            E.drain()
            E.tensor_tensor(d.bs[:, 6:7], d.bs[:, 6:7], d.bs[:, 7:8], op=ALU.mult)
            E.drain()
            E.tensor_tensor(d.bs[:, 1:2], d.bs[:, 1:2], d.bs[:, 6:7], op=ALU.add)
        E.drain()
        E.tensor_scalar(d.wT8[0:8, :], d.probsT[0:8, :], d.bs[:, 0:1][0:8, :],
                        None, op0=ALU.is_ge)
        E.drain()
        return E.tensor_tensor(d.wT8[0:8, :], d.probsT[0:8, :], d.wT8[0:8, :],
                               op=ALU.mult)
    pb.op("v", bisect, wait=[(ete, ete.n), (eg, eg.n), (ebsi, 1)], drain=True,
          inc=ebi)
    pbs.append(pb)

    pb = PB(p)
    e_w1a = p.edge("fx_w1a")
    e_w1b = p.edge("fx_w1b")
    e_w2 = p.edge("fx_w2")
    e_gb = p.edge("fx_gb")
    e_gm = p.edge("fx_gm")
    e_hm = p.edge("fx_hm")
    e_he = p.edge("fx_he")
    e_ym = p.edge("fx_ym")
    e_ye = p.edge("fx_ye")
    ps_h = p.psh2
    ps_y = p.psy2
    gyu = 0
    ye_b = e_ye.n
    hm_a = e_hm.n
    for ex in range(8):
        for half, e_wh, hm_tgt in ((0, e_w1a, hm_a), (1, e_w1b, e_hm.n)):
            for kc in range(ND):
                pb.op("g", lambda E, kc=kc, half=half, ex=ex: E.dma_start(
                    d.w1_sb[:, kc, half * 1024:(half + 1) * 1024],
                    d.d_fw1[ex, kc * 128:(kc + 1) * 128,
                            half * 1024:(half + 1) * 1024]),
                    wait=[(e_hm, hm_tgt)] if kc == 0 else (),
                    inc=e_wh, inc_k=16)
        for fc in range(16):
            pb.op("g", lambda E, fc=fc, ex=ex: E.dma_start(
                d.w2_sb[:, fc, :], d.d_fw2[ex, fc * 128:(fc + 1) * 128, :]),
                wait=[(e_ym, e_ym.n)] if fc == 0 else (),
                inc=e_w2, inc_k=16)
        wt_a = e_w1a.n
        wt_b = e_w1b.n
        wt_2 = e_w2.n
        for ci, (cs, W) in enumerate(CHUNKS4):
            pb.op("t", lambda E, ex=ex, cs=cs, W=W: E.matmul(
                ps_y[0][:, 0:W], d.bcsel8[0:8, ex * 128:ex * 128 + 128],
                d.wT8[0:8, cs:cs + W]),
                wait=[(ebi, 1), (e_ye, ye_b + gyu - 3)], inc=e_gm)
            pb.op("v", lambda E, W=W: E.tensor_copy(
                d.gateb[:, 0:W], ps_y[0][:, 0:W]),
                wait=[(e_gm, e_gm.n)], inc=e_gb)
            gb_t = e_gb.n
            for ft in range(16):
                for kc in range(ND):
                    pb.op("t", lambda E, ft=ft, kc=kc, cs=cs, W=W: E.matmul(
                        ps_h[ft % 4][:, 0:W],
                        d.w1_sb[:, kc, ft * 128:(ft + 1) * 128],
                        d.arA[:, kc, cs:cs + W],
                        start=(kc == 0), stop=(kc == ND - 1)),
                        wait=([(e_w1a if ft < 8 else e_w1b,
                                wt_a if ft < 8 else wt_b),
                               (e_he, e_he.n - 3)] if kc == 0 else ()),
                        inc=e_hm if kc == ND - 1 else None)
                if ci == 3 and ft == 7:
                    hm_a = e_hm.n
                wv = [(e_hm, e_hm.n)]
                if ft == 0:
                    wv.append((e_ym, e_ym.n))
                pb.op("s", lambda E, ft=ft, W=W: E.activation(
                    d.h_sb[:, ft, 0:W], ps_h[ft % 4][:, 0:W], AF.Silu),
                    wait=wv, inc=e_he)
            for ot in range(8):
                wv = [(e_ye, ye_b + gyu - 3)]
                if ot == 0:
                    wv.append((e_he, e_he.n))
                    wv.append((e_w2, wt_2))
                    wv.append((e_gb, gb_t))
                for ft in range(16):
                    last = (ft == 15)
                    pb.op("t", lambda E, ft=ft, ot=ot, last=last, W=W: E.matmul(
                        ps_y[ot % 4][:, 0:W],
                        d.w2_sb[:, ft, ot * 128:(ot + 1) * 128],
                        d.h_sb[:, ft, 0:W],
                        start=(ft == 0), stop=last),
                        wait=wv if ft == 0 else (),
                        inc=e_ym if last else None)

                def yev(E, ot=ot, cs=cs, W=W):
                    r = None
                    for pc, w in (((0, 512), (512, W - 512)) if W > 512
                                  else ((0, 512),)):
                        E.tensor_tensor(d.ln_tmp[:, 0:w],
                                        ps_y[ot % 4][:, pc:pc + w],
                                        d.gateb[:, pc:pc + w], op=ALU.mult)
                        E.drain()
                        r = E.tensor_tensor(
                            d.accum[:, ot, cs + pc:cs + pc + w],
                            d.accum[:, ot, cs + pc:cs + pc + w],
                            d.ln_tmp[:, 0:w], op=ALU.add)
                        if pc == 0 and W > 512:
                            E.drain()
                    return r
                pb.op("v", yev, wait=[(e_ym, e_ym.n),
                                      (p.edge("fx_init"), p.edge("fx_init").n)],
                      drain=True, inc=e_ye)
                gyu += 1
    pbs.append(pb)

    pb = PB(p)
    emit_ln(p, pb,
            src=lambda dt: d.accum[:, dt, :],
            dst=lambda dt, ch: d.accum[:, dt, ch * 512:(ch + 1) * 512],
            name="p4ln", src_f32=True,
            prewait=[(p.edge("fx_ye"), p.edge("fx_ye").n)])
    pbs.append(pb)
    pb = PB(p)
    e_tp = p.edge("p4o_tp")
    e_te = p.edge("p4o_te")
    e_st = [p.edge("p4o_st0"), p.edge("p4o_st1")]
    eap = p.edge("ln_ap")
    ps_o = [p.PSF[0], p.PSF[1]]
    g = 0
    for t in range(NT):
        for half in range(2):
            for dj in range(4):
                dt = half * 4 + dj
                pb.op("t", lambda E, t=t, dt=dt, dj=dj, g=g: E.transpose(
                    ps_o[g % 2][:, dj * 128:dj * 128 + 128],
                    d.accum[:, dt, t * 128:(t + 1) * 128], d.identf[:, :]),
                    wait=([(eap, eap.n)] if g == 0 else []) +
                         ([(e_te, e_te.n - 1)] if dj == 0 else []),
                    inc=e_tp if dj == 3 else None)
            pb.op("s", lambda E, t=t, half=half, g=g: E.activation(
                d.accum[:, half * 4:half * 4 + 4, t * 128:(t + 1) * 128],
                ps_o[g % 2][:, 0:512].rearrange("p (a b) -> p a b", b=128), AF.Copy),
                wait=[(e_tp, e_tp.n)], inc=e_te)
            g += 1
        pb.op("y", lambda E, t=t: E.dma_start(
            d.d_out[t * 128:(t + 1) * 128, :],
            d.accum[:, :, t * 128:(t + 1) * 128]),
            wait=[(e_te, e_te.n)], inc=e_st[t % 2], inc_k=16)
    pb.op("y", lambda E: E.nop(), wait=[(e_st[0], e_st[0].n), (e_st[1], e_st[1].n)])
    pbs.append(pb)

def alloc_common(p, io):
    d = NS()
    d.__dict__.update(io)
    d.arA = p.sb("arA", [128, ND, T], BF16)
    d.pool = p.sb("pool", [128, NPOOL], F32)
    d.identb = p.sb("identb", [128, 128], BF16)
    d.identf = p.sb("identf", [128, 128], F32)
    d.e2 = p.sb("e2", [128, 2], BF16)
    d.ones1 = p.sb("ones1", [1, 128], BF16)
    d.onescol = p.sb("onescol", [128, 1], BF16)
    d.onescol_f = p.sb("onescol_f", [128, 1], F32)
    d.cst = p.sb("cst", [128, 4], F32)
    d.bcsel8 = p.sb("bcsel8_sb", [8, 8 * 128], BF16)
    d.ln_rowAb = p.sb("ln_rowAb", [1, 512], BF16)
    d.ln_rowBb = p.sb("ln_rowBb", [1, 512], BF16)
    d.ln_r1 = p.sb("ln_r1", [1, 512], F32)
    d.ln_r2 = p.sb("ln_r2", [1, 512], F32)
    d.ln_sq = p.sb("ln_sq", [128, 1024], BF16)
    d.ln_tmp = p.sb("ln_tmp", [128, 512], F32)
    p.d = d
    return d


def alloc_p1(p, d):
    d.maskA = p.sb("maskA", [128, 128], BF16)
    d.qC = p.sb("qC", [128, ND, T], BF16)
    d.kD = p.sb("kD", [128, ND, T], BF16)
    d.arB = d.qC
    d.vE = p.sb("vE", [128, NT, D], BF16)
    d.yF2 = p.sb("yF2", [128, NT, 128], BF16)
    d.xld = p.sb("xld", [128, 2, D], BF16)
    d.rows_bv = p.sb("rows_bv", [1, 1024], BF16)
    d.wW = p.sb("wW0", [128, ND, 1024], BF16)
    d.om_f = p.sb("om_f", [128, 128], F32)
    d.om = p.sb("om", [128, 128], BF16)
    d.nssqT = p.sb("nssqT", [128, NT, 32], F32)
    d.phiQ = p.sb("phiQ", [128, 2, T], BF16)
    d.phiK = p.sb("phiK", [128, 2, T], BF16)
    d.phiQT = p.sb("phiQT", [128, 2, T], BF16)
    d.phiKT = p.sb("phiKT", [128, 2, T], BF16)
    d.S_sb = p.sb("S_sb", [128, 2, 65], BF16)
    d.S_acc = p.sb("S_acc", [128, 2, 65], F32)
    d.Am = p.sb("Am", [128, 2, 128], BF16)
    d.den = p.sb("den", [128, 8], F32)
    d.stmp = p.sb("stmp", [128, 2, 512], BF16)


def host_constants():
    identb = np.eye(128, dtype=BF)
    identf = np.eye(128, dtype=np.float32)
    maskA = (np.arange(128)[None, :] >= np.arange(128)[:, None]).astype(BF)
    i0 = np.arange(128)[:, None]
    j = np.arange(256)[None, :]
    mL0 = (j >= i0).astype(BF)
    mL1 = (j >= i0 + 128).astype(BF)
    mR0 = (j <= i0).astype(BF)
    mR1 = (j <= i0 + 128).astype(BF)
    maskP = np.ascontiguousarray(np.concatenate([mL0, mL1, mR0, mR1], axis=1))
    e2 = np.zeros((128, 2), dtype=BF)
    e2[0:64, 0] = BF(-0.0078125)
    e2[64:128, 1] = BF(-0.0078125)
    bcsel = np.zeros((4, 512), dtype=BF)
    for e in range(4):
        bcsel[e, e * 128:(e + 1) * 128] = BF(1.0)
    i = np.arange(128)[:, None]
    j = np.arange(256)[None, :]
    ones = np.ones((128, 256), dtype=np.float32)
    mC = np.zeros((128, 3584), dtype=np.float32)
    for qs in range(2):
        L = (j >= i + qs * 128).astype(np.float32)
        R = (j <= i + qs * 128).astype(np.float32)
        mC[:, qs * 768:qs * 768 + 768] = np.concatenate([L, ones, R], axis=1)
        mC[:, 1536 + qs * 512:1536 + qs * 512 + 512] = np.concatenate([ones, R], axis=1)
        mC[:, 2560 + qs * 512:2560 + qs * 512 + 512] = np.concatenate([L, ones], axis=1)
    bcsel8 = np.zeros((8, 1024), dtype=BF)
    for e in range(8):
        bcsel8[e, e * 128:(e + 1) * 128] = BF(1.0)
    return dict(identb=identb, identf=identf, maskA=maskA, maskP=maskP, e2=e2,
                bcsel=bcsel, maskC=mC.astype(BF), bcsel8=bcsel8)


def host_pool(inp):
    pool = np.zeros((NPOOL, 128), dtype=np.float32)

    def put(name, arr):
        arr = np.asarray(arr).reshape(-1)
        k = arr.shape[0] // 128
        pool[POOL[name]:POOL[name] + k, :] = arr.reshape(k, 128)

    put("b_attn", inp["L0_attn_b_attn"])
    put("b_proj", inp["L0_attn_b_proj"])
    put("sb1_0", inp["L0_ffn_sb1"][0]); put("sb1_1", inp["L0_ffn_sb1"][1])
    for e in range(4):
        put("rb1_%d" % e, inp["L0_ffn_rb1"][e])
    put("bq", inp["L1_attn_bq"]); put("bk", inp["L1_attn_bk"])
    put("bo", inp["L1_attn_bo"])
    put("g1", inp["L1_attn_g1"]); put("b1", inp["L1_attn_b1"])
    put("c1", inp["L1_attn_c1"]); put("c2", inp["L1_attn_c2"])
    put("g2", inp["L1_attn_g2"]); put("b2", inp["L1_attn_b2"])
    put("sb2_0", inp["L0_ffn_sb2"][0]); put("sb2_1", inp["L0_ffn_sb2"][1])
    for e in range(4):
        put("rb2_%d" % e, inp["L0_ffn_rb2"][e])
    pool_dev = np.ascontiguousarray(pool.T)

    rows = np.zeros((1, NROW), dtype=np.float32)

    def putr(name, arr):
        arr = np.asarray(arr).reshape(-1)
        rows[0, ROWS[name]:ROWS[name] + arr.shape[0]] = arr

    putr("bv", inp["L1_attn_bv"])
    putr("bav", inp["L0_attn_b_attn"][2048:3072])
    putr("gate_b", inp["L0_ffn_gate_b"])
    putr("sb2_0", inp["L0_ffn_sb2"][0]); putr("sb2_1", inp["L0_ffn_sb2"][1])
    for e in range(4):
        putr("rb2_%d" % e, inp["L0_ffn_rb2"][e])
    return pool_dev, rows


def const_inmap(inp):
    cst = host_constants()
    pool_dev, rows = host_pool(inp)
    return {
        "pool_in": pool_dev, "rows_in": rows,
        "identb_in": cst["identb"], "identf_in": cst["identf"],
        "maskA_in": cst["maskA"], "maskP_in": cst["maskP"], "e2_in": cst["e2"],
        "bcsel": cst["bcsel"], "maskC": cst["maskC"], "bcsel8": cst["bcsel8"],
    }


def build_program(phases=("p1", "p2", "p3", "p4"), stream_in=False, stream_out=False):
    nc = bass.Bass("TRN2", target_bir_lowering=False)
    io = {}
    io["d_pool"] = nc.dram_tensor("pool_in", [128, NPOOL], F32, kind="ExternalInput")
    io["d_rows"] = nc.dram_tensor("rows_in", [1, NROW], F32, kind="ExternalInput")
    io["d_identb"] = nc.dram_tensor("identb_in", [128, 128], BF16, kind="ExternalInput")
    io["d_identf"] = nc.dram_tensor("identf_in", [128, 128], F32, kind="ExternalInput")
    io["d_maskA"] = nc.dram_tensor("maskA_in", [128, 128], BF16, kind="ExternalInput")
    io["d_maskP"] = nc.dram_tensor("maskP_in", [128, 1024], BF16, kind="ExternalInput")
    io["d_e2"] = nc.dram_tensor("e2_in", [128, 2], BF16, kind="ExternalInput")
    if "p1" in phases:
        io["d_x"] = nc.dram_tensor("x_in", [T, D], F32, kind="ExternalInput")
        io["d_W_attn"] = nc.dram_tensor("W_attn", [D, 3 * D], F32, kind="ExternalInput")
        io["d_W_proj"] = nc.dram_tensor("W_proj", [D, D], F32, kind="ExternalInput")
        io["d_omega"] = nc.dram_tensor("omega", [64, 128], F32, kind="ExternalInput")
    if "p2" in phases:
        import os as _os2
        if _os2.environ.get("P2_DBG"):
            io["d_dbg_wsel"] = nc.dram_tensor("dbg_wsel", [128, NT, 4], F32,
                                              kind="ExternalOutput")
            io["d_dbg_accum"] = nc.dram_tensor("dbg_accum", [128, ND, T], F32,
                                               kind="ExternalOutput")
            io["d_dbg_gateb"] = nc.dram_tensor("dbg_gateb", [6, 128, T], BF16,
                                               kind="ExternalOutput")
        io["d_gate_w"] = nc.dram_tensor("gate_w", [D, 4], F32, kind="ExternalInput")
        io["d_sw1"] = nc.dram_tensor("sw1", [2, D, 4096], F32, kind="ExternalInput")
        io["d_sw2"] = nc.dram_tensor("sw2", [2, 4096, D], F32, kind="ExternalInput")
        io["d_rw1"] = nc.dram_tensor("rw1", [4, D, 4096], F32, kind="ExternalInput")
        io["d_rw2"] = nc.dram_tensor("rw2", [4, 4096, D], F32, kind="ExternalInput")
        io["d_bcsel"] = nc.dram_tensor("bcsel", [4, 512], BF16, kind="ExternalInput")
    io["d_bcsel8"] = nc.dram_tensor("bcsel8", [8, 1024], BF16,
                                    kind="ExternalInput")
    if "p4" in phases:
        io["d_out"] = nc.dram_tensor("out", [T, D], F32, kind="ExternalOutput")
        io["d_choice_wT"] = nc.dram_tensor("choice_wT", [D, 8], F32,
                                           kind="ExternalInput")
        io["d_fw1"] = nc.dram_tensor("fw1", [8, D, 2048], F32, kind="ExternalInput")
        io["d_fw2"] = nc.dram_tensor("fw2", [8, 2048, D], F32, kind="ExternalInput")
    if "p3" in phases:
        io["d_Wq"] = nc.dram_tensor("Wq", [D, D], F32, kind="ExternalInput")
        io["d_Wk"] = nc.dram_tensor("Wk", [D, D], F32, kind="ExternalInput")
        io["d_Wv"] = nc.dram_tensor("Wv", [D, D], F32, kind="ExternalInput")
        io["d_Wo"] = nc.dram_tensor("Wo", [D, D], F32, kind="ExternalInput")
        io["d_W1"] = nc.dram_tensor("W1", [D, 4096], F32, kind="ExternalInput")
        io["d_W2"] = nc.dram_tensor("W2", [4096, D], F32, kind="ExternalInput")
        io["d_maskC"] = nc.dram_tensor("maskC", [128, 3584], BF16, kind="ExternalInput")
    if stream_in:
        io["d_sin"] = nc.dram_tensor("stream_in", [128, ND, T], BF16, kind="ExternalInput")
    if stream_out:
        io["d_sout"] = nc.dram_tensor("stream_out", [128, ND, T], BF16,
                                      kind="ExternalOutput")
    import os as _os
    dbg = _os.environ.get("P1_DBG", "")
    if dbg:
        for nm in dbg.split(","):
            io["d_dbg_" + nm] = nc.dram_tensor("dbg_" + nm, [128, 16, 2048], BF16,
                                               kind="ExternalOutput")

    with ExitStack() as ctx:
        p = Prog(nc, ctx)
        d = alloc_common(p, io)
        pb0 = PB(p)
        emit_load_consts(p, pb0, d)
        if stream_in:
            eS = p.edge("sin")
            for dt in range(ND):
                pb0.op("g", lambda E, dt=dt: E.dma_start(d.arA[:, dt, :], d.d_sin[:, dt, :]),
                       inc=eS, inc_k=16)
        fw = [(p.edge("cst"), p.edge("cst").n)]
        if stream_in:
            fw.append((p.edge("sin"), p.edge("sin").n))
        for engx in ("t", "v", "s", "y"):
            pb0.op(engx, lambda E: E.nop(), wait=fw)
        pb0.emit()

        if "p1" in phases:
            with ExitStack() as c1:
                pp = sub_prog(p, c1)
                pp.alloc_std_psum("p1", nf=6, nb=2)
                alloc_p1(pp, d)
                ppbs = []
                import os as _os
                stages = _os.environ.get("P1_STAGES", "ABCDEF")
                dumps = {"B": [("q", d.qC), ("k", d.kD), ("v", d.vE)],
                         "C": [("nssqt", None)],
                         "D": [("yt", d.kD), ("phiq", None)],
                         "E": [("s", d.arB)],
                         "F": []}
                ppbs2 = []
                emit_p1(pp, ppbs2, stages=stages)
                per_stage = {"A": ppbs2[0:1]}
                idx = 1
                for st in "BCDEF":
                    if st in stages and idx < len(ppbs2) + 1:
                        per_stage[st] = ppbs2[idx:idx + 1]
                        idx += 1
                for st in "ABCDEF":
                    if st not in stages:
                        continue
                    for b in per_stage.get(st, []):
                        b.emit()
                    for nm, src_t in dumps.get(st, []):
                        if ("d_dbg_" + nm) in io and src_t is not None:
                            dpb = PB(pp)
                            eD = pp.edge("dbg_" + nm)
                            dst = io["d_dbg_" + nm]
                            if src_t.shape[1] == ND:
                                for dt2 in range(ND):
                                    dpb.op("y", lambda E, dt2=dt2, src_t=src_t, dst=dst:
                                           E.dma_start(dst[:, dt2, :], src_t[:, dt2, :]),
                                           inc=eD, inc_k=16)
                            else:
                                for t2 in range(NT):
                                    dpb.op("y", lambda E, t2=t2, src_t=src_t, dst=dst:
                                           E.dma_start(dst[:, t2, 0:src_t.shape[2]],
                                                       src_t[:, t2, :]),
                                           inc=eD, inc_k=16)
                            dpb.op("y", lambda E: E.nop(), wait=[(eD, eD.n)])
                            dpb.emit()

        if "p1" in phases:
            fpb = PB(p)
            e = p.edge("ln_ap")
            for engx in ("t", "s", "g", "y"):
                fpb.op(engx, lambda E: E.nop(), wait=[(e, e.n)])
            fpb.emit()

        if "p2" in phases:
            with ExitStack() as c2:
                pp = sub_prog(p, c2)
                pp.alloc_wide_psum("p2")
                alloc_p2(pp, d)
                ppbs = []
                emit_p2(pp, ppbs, stages=_os.environ.get("P2_STAGES", "GEL"))
                for b in ppbs:
                    b.emit()

        if "p2" in phases:
            fpb = PB(p)
            e = p.edge("ln_ap")
            for engx in ("t", "s", "g", "y"):
                fpb.op(engx, lambda E: E.nop(), wait=[(e, e.n)])
            fpb.emit()

        if "p3" in phases:
            with ExitStack() as c3:
                pp = sub_prog(p, c3)
                alloc_p3_outer(pp, d)
                with ExitStack() as c3a:
                    pa = sub_prog(pp, c3a)
                    alloc_p3_attn(pa, d)
                    pa.ps3_scb = [pa.psum("ps3_sca", [128, 1024], F32),
                                  pa.psum("ps3_scc", [128, 1024], F32)]
                    pa.ps3_ptb = [pa.psum("ps3_pta", [128, 1024], BF16),
                                  pa.psum("ps3_ptc", [128, 1024], BF16)]
                    pa.ps3_pvb = [pa.psum("ps3_pva", [128, 512], F32),
                                  pa.psum("ps3_pvc", [128, 512], F32)]
                    pa.ps3_sc = pa.ps3_scb[0]
                    pa.ps3_pt = pa.ps3_scb[1]
                    ppbs = []
                    emit_p3_attn(pa, ppbs)
                    for b in ppbs:
                        b.emit()
                with ExitStack() as c3b:
                    pbp = sub_prog(pp, c3b)
                    pbp.alloc_std_psum("p3s1", nf=6, nb=0)
                    alloc_p3_s1(pbp, d)
                    ppbs = []
                    emit_p3_s1(pbp, ppbs)
                    for b in ppbs:
                        b.emit()
                with ExitStack() as c3c:
                    pc = sub_prog(pp, c3c)
                    pc.alloc_wide_psum("p3f")
                    alloc_p3_ffn(pc, d)
                    ppbs = []
                    emit_p3_ffn(pc, ppbs)
                    for b in ppbs:
                        b.emit()
            fpb = PB(p)
            e = p.edge("ln_ap")
            for engx in ("t", "s", "g", "y"):
                fpb.op(engx, lambda E: E.nop(), wait=[(e, e.n)])
            fpb.emit()

        if "p4" in phases:
            with ExitStack() as c4:
                pp = sub_prog(p, c4)
                pp.alloc_wide_psum("p4")
                alloc_p4(pp, d)
                ppbs = []
                emit_p4(pp, ppbs)
                for b in ppbs:
                    b.emit()

        outp = PB(p)
        if stream_out:
            eO = p.edge("sout")
            for dt in range(ND):
                outp.op("y", lambda E, dt=dt: E.dma_start(d.d_sout[:, dt, :], d.arA[:, dt, :]),
                        inc=eO, inc_k=16)
            outp.op("y", lambda E: E.nop(), wait=[(eO, eO.n)])
        outp.emit()
    return nc


FUSED = True


def kernel(**inputs):
    inp = inputs
    ncores = 8
    base = const_inmap(inp)
    wmap_p1 = {
        "W_attn": np.ascontiguousarray(inp["L0_attn_W_attn"]),
        "W_proj": np.ascontiguousarray(inp["L0_attn_W_proj"]),
        "omega": np.ascontiguousarray(inp["L0_attn_omega"]),
    }
    wmap_p2 = {
        "gate_w": np.ascontiguousarray(inp["L0_ffn_gate_w"]),
        "sw1": np.ascontiguousarray(inp["L0_ffn_sw1"]),
        "sw2": np.ascontiguousarray(inp["L0_ffn_sw2"]),
        "rw1": np.ascontiguousarray(inp["L0_ffn_rw1"]),
        "rw2": np.ascontiguousarray(inp["L0_ffn_rw2"]),
    }
    wmap_p3 = {
        "Wq": np.ascontiguousarray(inp["L1_attn_Wq"]),
        "Wk": np.ascontiguousarray(inp["L1_attn_Wk"]),
        "Wv": np.ascontiguousarray(inp["L1_attn_Wv"]),
        "Wo": np.ascontiguousarray(inp["L1_attn_Wo"]),
        "W1": np.ascontiguousarray(inp["L1_attn_W1"]),
        "W2": np.ascontiguousarray(inp["L1_attn_W2"]),
    }
    wmap_p4 = {
        "choice_wT": np.ascontiguousarray(np.asarray(inp["L1_ffn_choice_w"]).T),
        "fw1": np.ascontiguousarray(inp["L1_ffn_w1"]),
        "fw2": np.ascontiguousarray(inp["L1_ffn_w2"]),
    }
    xs = [np.ascontiguousarray(inp["L0_attn_x"][b]) for b in range(ncores)]

    if FUSED:
        nc = build_program(phases=("p1", "p2", "p3", "p4"))
        im = dict(base)
        im.update(wmap_p1); im.update(wmap_p2); im.update(wmap_p3); im.update(wmap_p4)
        in_maps = [dict(im, x_in=xs[b]) for b in range(ncores)]
        res = run_bass_kernel_spmd(nc, in_maps, core_ids=list(range(ncores)))
        return np.stack([res.results[b]["out"] for b in range(ncores)], axis=0)

    nc1 = build_program(phases=("p1",), stream_out=True)
    im = dict(base); im.update(wmap_p1)
    res = run_bass_kernel_spmd(nc1, [dict(im, x_in=xs[b]) for b in range(ncores)],
                               core_ids=list(range(ncores)))
    streams = [res.results[b]["stream_out"] for b in range(ncores)]

    nc2 = build_program(phases=("p2",), stream_in=True, stream_out=True)
    im = dict(base); im.update(wmap_p2)
    res = run_bass_kernel_spmd(nc2, [dict(im, stream_in=streams[b])
                                     for b in range(ncores)],
                               core_ids=list(range(ncores)))
    streams = [res.results[b]["stream_out"] for b in range(ncores)]

    nc3 = build_program(phases=("p3",), stream_in=True, stream_out=True)
    im = dict(base); im.update(wmap_p3)
    res = run_bass_kernel_spmd(nc3, [dict(im, stream_in=streams[b])
                                     for b in range(ncores)],
                               core_ids=list(range(ncores)))
    streams = [res.results[b]["stream_out"] for b in range(ncores)]

    nc4 = build_program(phases=("p4",), stream_in=True)
    im = dict(base); im.update(wmap_p4)
    res = run_bass_kernel_spmd(nc4, [dict(im, stream_in=streams[b])
                                     for b in range(ncores)],
                               core_ids=list(range(ncores)))
    return np.stack([res.results[b]["out"] for b in range(ncores)], axis=0)


# version 11
# speedup vs baseline: 1.2098x; 1.0369x over previous
import numpy as np
import ml_dtypes
from contextlib import ExitStack

import concourse.bass as bass
import concourse.mybir as mybir
from concourse.bass_utils import run_bass_kernel_spmd

F32 = mybir.dt.float32
BF16 = mybir.dt.bfloat16
AF = mybir.ActivationFunctionType
ALU = mybir.AluOpType
AX = mybir.AxisListType

T = 2048
D = 1024
NT = T // 128
ND = D // 128
NHEAD = 16
DH = 64
CHUNKS = tuple((i * 512, 512) for i in range(4))
CHUNKS4 = CHUNKS
LN_EPS = 1e-5
PERF_EPS = 1e-6
LNSQRT_M = float(np.log(np.sqrt(128.0)))

BF = ml_dtypes.bfloat16

_POOL_SPECS = [
    ("b_attn", 24), ("b_proj", 8),
    ("sb1_0", 32), ("sb1_1", 32),
    ("rb1_0", 32), ("rb1_1", 32), ("rb1_2", 32), ("rb1_3", 32),
    ("bq", 8), ("bk", 8), ("bo", 8),
    ("g1", 8), ("b1", 8), ("c1", 32), ("c2", 8), ("g2", 8), ("b2", 8),
    ("sb2_0", 8), ("sb2_1", 8), ("rb2_0", 8), ("rb2_1", 8), ("rb2_2", 8), ("rb2_3", 8),
]
POOL = {}
_c = 0
for _n, _k in _POOL_SPECS:
    POOL[_n] = _c
    _c += _k
NPOOL = _c

_ROW_SPECS = [("bv", 8), ("bav", 8), ("gate_b", 1), ("sb2_0", 8), ("sb2_1", 8),
              ("rb2_0", 8), ("rb2_1", 8), ("rb2_2", 8), ("rb2_3", 8)]
ROWS = {}
_c = 0
for _n, _k in _ROW_SPECS:
    ROWS[_n] = _c * 128
    _c += _k
NROW = _c * 128


class Edge:
    def __init__(self, ctx, nc, name):
        self.sem = ctx.enter_context(nc.semaphore(name))
        self.n = 0


class PB:

    ENG = {"t": "tensor", "v": "vector", "s": "scalar", "g": "gpsimd", "y": "sync"}

    def __init__(self, prog):
        self.prog = prog
        self.qs = {k: [] for k in self.ENG}

    def op(self, eng, fn, wait=(), inc=None, inc_k=1, drain=False):
        wv = []
        for w in wait:
            if isinstance(w, tuple):
                e, v = w
            else:
                e, v = w, w.n
            if v > 0:
                wv.append((e.sem, v))
        iv = None
        if inc is not None:
            inc.n += inc_k
            iv = (inc.sem, inc_k)

        def run(E, wv=wv, iv=iv, fn=fn, drain=drain):
            for s, v in wv:
                E.wait_ge(s, v)
            if drain:
                E.drain()
            inst = fn(E)
            if iv is not None:
                inst.then_inc(*iv)

        self.qs[eng].append(run)

    def emit(self):
        nc = self.prog.nc
        with nc.Block() as b:
            for k, attr in self.ENG.items():
                if self.qs[k]:
                    getattr(b, attr)(lambda E, fns=tuple(self.qs[k]): [f(E) for f in fns])


class Prog:
    def __init__(self, nc, ctx):
        self.nc = nc
        self.ctx = ctx
        self.ctx_root = ctx
        self._edges = {}
        self.d = None
        self.PSF = None
        self.PSB = None

    def edge(self, name):
        if name not in self._edges:
            self._edges[name] = Edge(self.ctx_root, self.nc, "E_" + name)
        return self._edges[name]

    def sb(self, name, shape, dtype):
        return self.ctx.enter_context(self.nc.sbuf_tensor(name, shape, dtype))

    def psum(self, name, shape, dtype=F32):
        return self.ctx.enter_context(self.nc.psum_tensor(name, shape, dtype))

    def alloc_std_psum(self, tag, nf=6, nb=2):
        self.PSF = [self.psum("psf_%s%d" % (tag, i), [128, 512], F32)
                    for i in range(nf)]
        self.PSB = [self.psum("psb_%s%d" % (tag, i), [128, 1024], BF16)
                    for i in range(nb)]

    def alloc_wide_psum(self, tag):
        self.PSW = [self.psum("psw_%s%d" % (tag, i), [128, 1024], F32)
                    for i in range(4)]
        self.psh2 = [self.PSW[0][:, 0:512], self.PSW[0][:, 512:1024],
                     self.PSW[1][:, 0:512], self.PSW[1][:, 512:1024]]
        self.psy2 = [self.PSW[2][:, 0:512], self.PSW[2][:, 512:1024],
                     self.PSW[3][:, 0:512], self.PSW[3][:, 512:1024]]
        self.PSF = [self.PSW[0][:, 0:512], self.PSW[0][:, 512:1024],
                    self.PSW[1][:, 0:512], self.PSW[1][:, 512:1024],
                    self.PSW[2][:, 0:512], self.PSW[2][:, 512:1024]]


class NS:
    pass


def sub_prog(p, ctx):
    pp = Prog.__new__(Prog)
    pp.nc = p.nc
    pp.ctx = ctx
    pp.ctx_root = p.ctx_root
    pp._edges = p._edges
    pp.PSF = None
    pp.PSB = None
    pp.d = p.d
    return pp


def emit_load_consts(p, pb, d):
    eL = p.edge("cst")
    for fn in [
        lambda E: E.dma_start(d.pool[:, :], d.d_pool[:, :]),
        lambda E: E.dma_start(d.identb[:, :], d.d_identb[:, :]),
        lambda E: E.dma_start(d.identf[:, :], d.d_identf[:, :]),
        lambda E: E.dma_start(d.e2[:, :], d.d_e2[:, :]),
        lambda E: E.dma_start(d.bcsel8[:, :], d.d_bcsel8[:, :]),
    ]:
        pb.op("g", fn, inc=eL, inc_k=16)
    pb.op("g", lambda E: E.memset(d.ones1[:, :], 1.0))
    pb.op("g", lambda E: E.memset(d.onescol[:, :], 1.0))
    pb.op("g", lambda E: E.memset(d.onescol_f[:, :], 1.0))
    pb.op("g", lambda E: E.memset(d.cst[:, 0:1], -LNSQRT_M))
    pb.op("g", lambda E: E.memset(d.cst[:, 1:2], PERF_EPS))
    pb.op("g", lambda E: E.memset(d.cst[:, 2:3], LN_EPS))


def emit_ln(p, pb, src, dst, gain=None, bias=None, name="ln", src_f32=False,
            prewait=()):
    d = p.d
    ones = d.onescol_f if src_f32 else d.onescol
    e_row = p.edge("ln_row")
    e_st = p.edge("ln_st")
    e_sq = p.edge("ln_sq")
    e_qt = p.edge("ln_qt")
    e_rs = p.edge("ln_rs")
    e_cs = p.edge("ln_cs")
    e_bc = p.edge("ln_bc")
    e_ap = p.edge("ln_ap")
    ps_rs = [p.PSF[0], p.PSF[3]]
    ps_As = [p.PSF[1], p.PSF[4]]
    ps_Bs = [p.PSF[2], p.PSF[5]]
    row_b = e_row.n
    ap_b = e_ap.n
    for ch in range(4):
        cs = ch * 512
        ps_r = ps_rs[ch % 2]
        ps_A = ps_As[ch % 2]
        ps_B = ps_Bs[ch % 2]
        for dt in range(ND):
            pb.op("t", lambda E, dt=dt, cs=cs, ps_r=ps_r: E.matmul(
                ps_r[0:1, 0:512], ones[:, :], src(dt)[:, cs:cs + 512],
                start=(dt == 0), stop=(dt == ND - 1)),
                wait=(list(prewait) if (ch == 0 and dt == 0) else []) +
                     ([(e_row, row_b + ch - 1)] if (dt == 0 and ch >= 2) else []),
                inc=e_st if dt == ND - 1 else None)
        for dt in range(ND):
            pb.op("s", lambda E, dt=dt, cs=cs: E.activation(
                d.ln_sq[:, (dt % 2) * 512:(dt % 2) * 512 + 512], src(dt)[:, cs:cs + 512],
                AF.Square),
                wait=(list(prewait) if (ch == 0 and dt == 0) else []) +
                     ([(e_qt, e_qt.n - 1)] if dt >= 2 else []), inc=e_sq)
            pb.op("t", lambda E, dt=dt, ps_r=ps_r: E.matmul(
                ps_r[32:33, 0:512], d.onescol[:, :],
                d.ln_sq[:, (dt % 2) * 512:(dt % 2) * 512 + 512],
                start=(dt == 0), stop=(dt == ND - 1)),
                wait=[(e_sq, e_sq.n)], inc=e_qt)
        def rowmath1(E, ps_r=ps_r):
            E.tensor_scalar(d.ln_r1[0:1, 0:512], ps_r[0:1, 0:512], 1.0 / D, None,
                            op0=ALU.mult)
            E.drain()
            return E.tensor_scalar(d.ln_r2[0:1, 0:512], ps_r[32:33, 0:512],
                                   1.0 / D, None, op0=ALU.mult)
        pb.op("v", rowmath1, wait=[(e_st, e_st.n), (e_qt, e_qt.n), (e_rs, e_rs.n)],
              drain=True, inc=e_row)

        def rowmath(E):
            E.tensor_tensor(d.ln_tmp[0:1, 0:512], d.ln_r1[0:1, 0:512],
                            d.ln_r1[0:1, 0:512], op=ALU.mult)
            E.drain()
            return E.tensor_tensor(d.ln_r2[0:1, 0:512], d.ln_r2[0:1, 0:512],
                                   d.ln_tmp[0:1, 0:512], op=ALU.subtract)
        e_rw2 = p.edge("ln_rw2")
        pb.op("v", rowmath, drain=True, inc=e_rw2)
        def rstd(E):
            E.activation(d.ln_r2[0:1, 0:512], d.ln_r2[0:1, 0:512], AF.Ln,
                         bias=d.cst[0:1, 2:3])
            E.drain()
            return E.activation(d.ln_rowAb[0:1, 0:512], d.ln_r2[0:1, 0:512], AF.Exp,
                                scale=-0.5)
        pb.op("s", rstd, wait=[(e_rw2, e_rw2.n), (e_bc, e_bc.n)], drain=True,
              inc=e_rs)
        pb.op("v", lambda E: E.scalar_tensor_tensor(
            d.ln_rowBb[0:1, 0:512], d.ln_r1[0:1, 0:512], -1.0,
            d.ln_rowAb[0:1, 0:512], op0=ALU.mult, op1=ALU.mult),
            wait=[(e_rs, e_rs.n)], drain=True, inc=e_cs)
        pb.op("t", lambda E, ps_A=ps_A: E.matmul(
            ps_A[:, 0:512], d.ones1[:, :], d.ln_rowAb[0:1, 0:512]),
            wait=[(e_cs, e_cs.n)] +
                 ([(e_ap, ap_b + ch - 1)] if ch >= 2 else []))
        pb.op("t", lambda E, ps_B=ps_B: E.matmul(
            ps_B[:, 0:512], d.ones1[:, :], d.ln_rowBb[0:1, 0:512]),
            inc=e_bc)
        for dt in range(ND):
            if gain is None:
                def apply(E, dt=dt, cs=cs, ps_A=ps_A, ps_B=ps_B):
                    E.drain()
                    E.tensor_tensor(d.ln_tmp[:, 0:512], src(dt)[:, cs:cs + 512],
                                    ps_A[:, 0:512], op=ALU.mult)
                    E.drain()
                    return E.tensor_tensor(dst(dt, cs // 512), d.ln_tmp[:, 0:512],
                                           ps_B[:, 0:512], op=ALU.add)
            else:
                def apply(E, dt=dt, cs=cs, gain=gain, bias=bias, ps_A=ps_A, ps_B=ps_B):
                    E.drain()
                    E.tensor_tensor(d.ln_tmp[:, 0:512], src(dt)[:, cs:cs + 512],
                                    ps_A[:, 0:512], op=ALU.mult)
                    E.drain()
                    E.tensor_tensor(d.ln_tmp[:, 0:512], d.ln_tmp[:, 0:512],
                                    ps_B[:, 0:512], op=ALU.add)
                    E.drain()
                    return E.tensor_scalar(dst(dt, cs // 512), d.ln_tmp[:, 0:512],
                                           d.pool[:, gain + dt:gain + dt + 1],
                                           d.pool[:, bias + dt:bias + dt + 1],
                                           op0=ALU.mult, op1=ALU.add)
            pb.op("v", apply, wait=[(e_bc, e_bc.n)],
                  inc=e_ap if dt == ND - 1 else None)


def emit_p1(p, pbs, stages="ABCDEF"):
    d = p.d

    pb = PB(p)
    eg = [p.edge("p1a_g0"), p.edge("p1a_g1")]
    et = p.edge("p1a_t")
    ev = p.edge("p1a_v")
    ev_b = ev.n
    trb = [p.PSB[0], p.PSB[1]]
    for t in range(NT):
        pb.op("g", lambda E, t=t: E.dma_start(
            d.xld[:, t % 2, :], d.d_x[t * 128:(t + 1) * 128, :]),
            wait=[(ev, ev_b + t - 1)] if t >= 2 else (), inc=eg[t % 2], inc_k=16)
        gt = eg[t % 2].n
        for dt in range(ND):
            pb.op("t", lambda E, t=t, dt=dt: E.transpose(
                trb[t % 2][:, dt * 128:(dt + 1) * 128],
                d.xld[:, t % 2, dt * 128:(dt + 1) * 128], d.identb[:, :]),
                wait=[(eg[t % 2], gt), (ev, ev_b + t - 1)] if dt == 0 else (),
                inc=et if dt == ND - 1 else None)
        pb.op("v", lambda E, t=t: E.tensor_copy(
            d.arA[:, :, t * 128:(t + 1) * 128],
            trb[t % 2].rearrange("p (d q) -> p d q", q=128)),
            wait=[(et, et.n)], inc=ev)
    pbs.append(pb)

    if "B" not in stages:
        return
    pb = PB(p)
    ew = p.edge("qkv_w")
    em = p.edge("qkv_m")
    ee = p.edge("qkv_e")
    ps_mm = [p.PSF[0], p.PSF[1]]
    erow = p.edge("qkv_row")
    ee_b = ee.n
    pb.op("g", lambda E: E.dma_start(
        d.rows_bv[:, :], d.d_rows[0:1, ROWS["bav"]:ROWS["bav"] + 1024]),
        inc=erow, inc_k=16)
    er_t = erow.n
    nmm = 0
    for part, dst_ar in ((0, d.qC), (1, d.kD), (2, None)):
        for kc in range(ND):
            pb.op("g", lambda E, kc=kc, part=part: E.dma_start(
                d.wW[:, kc, :], d.d_W_attn[kc * 128:(kc + 1) * 128,
                                           part * 1024:(part + 1) * 1024]),
                wait=[(em, em.n)] if kc == 0 else (), inc=ew, inc_k=16)
        wt = ew.n
        if part < 2:
            for m in range(ND):
                for ch in range(4):
                    pi = nmm % 2
                    for kc in range(ND):
                        w = [(ew, wt), (ee, ee_b + nmm - 1)] if kc == 0 else []
                        pb.op("t", lambda E, m=m, ch=ch, kc=kc, pi=pi: E.matmul(
                            ps_mm[pi][:, :], d.wW[:, kc, m * 128:(m + 1) * 128],
                            d.arA[:, kc, ch * 512:(ch + 1) * 512],
                            start=(kc == 0), stop=(kc == ND - 1)),
                            wait=w, inc=em if kc == ND - 1 else None)
                    bcol = POOL["b_attn"] + part * 8 + m
                    pb.op("s", lambda E, pi=pi, bcol=bcol, m=m, ch=ch, dst_ar=dst_ar:
                          E.activation(dst_ar[:, m, ch * 512:(ch + 1) * 512],
                                       ps_mm[pi][:, :], AF.Identity,
                                       bias=d.pool[:, bcol:bcol + 1]),
                          wait=[(em, em.n)], inc=ee)
                    nmm += 1
        else:
            for t in range(NT):
                for ch in range(2):
                    pi = nmm % 2
                    for kc in range(ND):
                        w = [(ew, wt), (ee, ee_b + nmm - 1)] if kc == 0 else []
                        pb.op("t", lambda E, t=t, ch=ch, kc=kc, pi=pi: E.matmul(
                            ps_mm[pi][:, :], d.arA[:, kc, t * 128:(t + 1) * 128],
                            d.wW[:, kc, ch * 512:(ch + 1) * 512],
                            start=(kc == 0), stop=False),
                            wait=w)
                    pb.op("t", lambda E, ch=ch, pi=pi: E.matmul(
                        ps_mm[pi][:, :], d.ones1[:, :],
                        d.rows_bv[0:1, ch * 512:ch * 512 + 512],
                        start=False, stop=True),
                        wait=[(erow, er_t)], inc=em)
                    pb.op("s", lambda E, t=t, ch=ch, pi=pi: E.activation(
                        d.vE[:, t, ch * 512:(ch + 1) * 512], ps_mm[pi][:, :], AF.Copy),
                        wait=[(em, em.n)], inc=ee)
                    nmm += 1
    pbs.append(pb)

    if "C" not in stages:
        return
    pb = PB(p)
    eo = p.edge("p1c_o")
    eob = p.edge("p1c_ob")
    e_sq = p.edge("p1c_sq")
    e_mm = p.edge("p1c_mm")
    e_bi = p.edge("p1c_bi")
    ps_row = p.PSF[2]
    e_mA = p.edge("p1c_mA")
    pb.op("g", lambda E: E.dma_start(d.maskA[:, :], d.d_maskA[:, :]), inc=e_mA, inc_k=16)
    mA_t = e_mA.n
    pb.op("g", lambda E: E.dma_start(d.om_f[0:64, :], d.d_omega[:, :]), inc=eo, inc_k=16)
    pb.op("g", lambda E: E.dma_start(d.om_f[64:128, :], d.d_omega[:, :]), inc=eo, inc_k=16)
    pb.op("v", lambda E: E.tensor_scalar(d.om[:, :], d.om_f[:, :], 0.125, None,
                                         op0=ALU.mult),
          wait=[(eo, eo.n)], inc=eob)
    u = 0
    for j, src_ar in ((0, d.qC), (1, d.kD)):
        for t in range(NT):
            for dt in range(ND):
                pb.op("s", lambda E, dt=dt, t=t, src_ar=src_ar, u=u: E.activation(
                    d.ln_sq[:, (u % 8) * 128:(u % 8) * 128 + 128],
                    src_ar[:, dt, t * 128:(t + 1) * 128], AF.Square),
                    wait=[(e_mm, u - 7)] if u >= 8 else (), inc=e_sq)
                pb.op("t", lambda E, dt=dt, u=u: E.matmul(
                    ps_row[:, dt * 2:dt * 2 + 2], d.ln_sq[:, (u % 8) * 128:(u % 8) * 128 + 128],
                    d.e2[:, :]),
                    wait=[(e_sq, e_sq.n), (e_bi, e_bi.n)], inc=e_mm)
                u += 1
            pb.op("s", lambda E, t=t, j=j: E.activation(
                d.nssqT[:, t, j * 16:j * 16 + 16], ps_row[:, 0:16], AF.Identity,
                bias=d.cst[:, 0:1]),
                wait=[(e_mm, e_mm.n)], inc=e_bi)
    pbs.append(pb)

    if "D" not in stages:
        return
    pb = PB(p)
    ph_t = p.edge("p1d_t")
    ph_s = p.edge("p1d_s")
    ph_tt = p.edge("p1d_tt")
    ph_ve = p.edge("p1d_ve")
    sc_tv = [p.edge("p1d_tv0"), p.edge("p1d_tv1")]
    sc_vt = [p.edge("p1d_vt0"), p.edge("p1d_vt1")]
    sc_tn = [p.edge("p1d_tn0"), p.edge("p1d_tn1")]
    sc_ts = [p.edge("p1d_ts0"), p.edge("p1d_ts1")]
    sc_st = [p.edge("p1d_st0"), p.edge("p1d_st1")]
    sc_yv = [p.edge("p1d_yv0"), p.edge("p1d_yv1")]
    sc_rc = [p.edge("p1d_rc0"), p.edge("p1d_rc1")]
    sc_ytp = p.edge("p1d_ytp")
    sc_yte = p.edge("p1d_yte")
    ps_S = [p.PSF[0], p.PSF[3]]
    ps_N = [p.PSF[1], p.PSF[4]]
    ps_P = [p.PSF[2], p.PSF[5]]
    e_biC = p.edge("p1c_bi")
    biC_t = e_biC.n
    e_mAC = p.edge("p1c_mA")
    mAC_t = e_mAC.n
    for g in range(8):
        dt = g
        for par in range(2):
            h = 2 * g + par
            ho = par * 64
            for j, (src_ar, phisel, nof) in enumerate(
                    ((d.qC, d.phiQ, 0), (d.kD, d.phiK, 16))):
                for t in range(NT):
                    u = (par * 2 + j) * NT + t
                    wv = [(eob, 1), (ph_s, ph_s.n - 1)]
                    if g == 0 and u == 0:
                        wv.append((e_biC, biC_t))
                    pb.op("t", lambda E, t=t, dt=dt, ho=ho, src_ar=src_ar, u=u:
                          E.matmul(
                        ps_P[u % 2][:, ((u // 2) % 2) * 128:
                                    ((u // 2) % 2) * 128 + 128],
                        src_ar[ho:ho + 64, dt, t * 128:(t + 1) * 128],
                        d.om[ho:ho + 64, :]),
                        wait=wv, inc=ph_t)
                    pb.op("s", lambda E, t=t, h=h, u=u, par=par, phisel=phisel,
                          nof=nof: E.activation(
                        phisel[:, par, t * 128:(t + 1) * 128],
                        ps_P[u % 2][:, ((u // 2) % 2) * 128:
                                    ((u // 2) % 2) * 128 + 128],
                        AF.Exp, bias=d.nssqT[:, t, nof + h:nof + h + 1]),
                        wait=[(ph_t, ph_t.n)], inc=ph_s)
        allexp = ph_s.n
        for pj, (phisel, phiTsel, par) in enumerate(
                ((d.phiQ, d.phiQT, 0), (d.phiK, d.phiKT, 0),
                 (d.phiQ, d.phiQT, 1), (d.phiK, d.phiKT, 1))):
            for t in range(NT):
                u = pj * NT + t
                grp = u // 4
                pb.op("t", lambda E, t=t, u=u, par=par, phisel=phisel, grp=grp:
                      E.transpose(
                    p.PSB[grp % 2][:, (u % 4) * 128:(u % 4) * 128 + 128],
                    phisel[:, par, t * 128:(t + 1) * 128], d.identb[:, :]),
                    wait=[(ph_s, allexp), (ph_ve, ph_ve.n - 1),
                          (sc_yte, sc_yte.n)] if t % 4 == 0 else (),
                    inc=ph_tt if t % 4 == 3 else None)
                if t % 4 == 3:
                    pb.op("v", lambda E, t=t, par=par, phiTsel=phiTsel, grp=grp:
                          E.tensor_copy(
                        phiTsel[:, par, (t - 3) * 128:(t + 1) * 128],
                        p.PSB[grp % 2][:, 0:512]),
                        wait=[(ph_tt, ph_tt.n)], inc=ph_ve)
        pair_rdy = ph_ve.n
        for c in range(NT):
            for par in range(2):
                h = 2 * g + par
                wv = [(ph_ve, pair_rdy)] if c == 0 else [(sc_vt[par],
                                                          16 * g + c)]
                pb.op("t", lambda E, c=c, par=par: E.matmul(
                    ps_N[par][:, 128:256], d.phiKT[:, par, c * 128:(c + 1) * 128],
                    d.phiQT[:, par, c * 128:(c + 1) * 128]),
                    wait=wv, inc=sc_tv[par])
                wv = [(sc_tv[par], 16 * g + c + 1), (sc_tn[par], 16 * g + c)]
                if g == 0 and c == 0 and par == 0:
                    wv.append((e_mAC, mAC_t))
                pb.op("v", lambda E, par=par: E.tensor_tensor(
                    d.Am[:, par, :], ps_N[par][:, 128:256], d.maskA[:, :],
                    op=ALU.mult),
                    wait=wv, inc=sc_vt[par])
                pb.op("t", lambda E, c=c, h=h, par=par: E.matmul(
                    ps_N[par][:, 0:64], d.Am[:, par, :],
                    d.vE[:, c, h * 64:h * 64 + 64],
                    start=True, stop=False),
                    wait=[(sc_vt[par], 16 * g + c + 1),
                          (sc_yv[par], 16 * g + c)])
                if c > 0:
                    pb.op("t", lambda E, par=par: E.matmul(
                        ps_N[par][:, 64:65], d.Am[:, par, :], d.onescol[:, :],
                        start=False, stop=False))
                    pb.op("t", lambda E, c=c, par=par: E.matmul(
                        ps_N[par][:, 0:65],
                        d.phiQT[:, par, c * 128:(c + 1) * 128],
                        d.S_sb[:, par, :],
                        start=False, stop=True),
                        wait=[(sc_st[par], 16 * g + c)], inc=sc_tn[par])
                else:
                    pb.op("t", lambda E, par=par: E.matmul(
                        ps_N[par][:, 64:65], d.Am[:, par, :], d.onescol[:, :],
                        start=False, stop=True), inc=sc_tn[par])
                pb.op("t", lambda E, c=c, h=h, par=par: E.matmul(
                    ps_S[par][:, 0:64], d.phiK[:, par, c * 128:(c + 1) * 128],
                    d.vE[:, c, h * 64:h * 64 + 64],
                    start=(c == 0), stop=False, skip_group_check=True),
                    wait=[(sc_st[par], 16 * g + c)] if c > 0 else ())
                pb.op("t", lambda E, c=c, par=par: E.matmul(
                    ps_S[par][:, 64:65], d.phiK[:, par, c * 128:(c + 1) * 128],
                    d.onescol[:, :],
                    start=False, stop=True, skip_group_check=True),
                    inc=sc_ts[par])
                pb.op("s", lambda E, par=par: E.activation(
                    d.S_sb[:, par, :], ps_S[par][:, 0:65], AF.Copy),
                    wait=[(sc_ts[par], 16 * g + c + 1)], inc=sc_st[par])
                def yev(E, c=c, par=par):
                    E.tensor_scalar(d.den[:, 4 * par:4 * par + 1],
                                    ps_N[par][:, 64:65], d.cst[:, 1:2], None,
                                    op0=ALU.add)
                    E.drain()
                    E.reciprocal(d.den[:, 4 * par + 1:4 * par + 2],
                                 d.den[:, 4 * par:4 * par + 1])
                    E.drain()
                    return E.tensor_scalar(
                        d.yF2[:, c, par * 64:par * 64 + 64],
                        ps_N[par][:, 0:64],
                        d.den[:, 4 * par + 1:4 * par + 2], None, op0=ALU.mult)
                pb.op("v", yev, wait=[(sc_tn[par], 16 * g + c + 1)],
                      inc=sc_yv[par])
        ps_Y = p.PSB[1]
        for r in range(2):
            for tt in range(8):
                t = r * 8 + tt
                pb.op("t", lambda E, t=t, tt=tt: E.transpose(
                    ps_Y[:, tt * 128:(tt + 1) * 128],
                    d.yF2[:, t, 0:128], d.identb[:, :]),
                    wait=[(sc_yv[0], 16 * (g + 1)), (sc_yv[1], 16 * (g + 1)),
                          (sc_yte, sc_yte.n)] if tt == 0 else (),
                    inc=sc_ytp if tt == 7 else None)
            pb.op("v", lambda E, r=r, dt=dt: E.tensor_copy(
                d.kD[:, dt, r * 1024:(r + 1) * 1024], ps_Y[:, 0:1024]),
                wait=[(sc_ytp, sc_ytp.n)], inc=sc_yte)
    pbs.append(pb)

    if "E" not in stages:
        return
    pb = PB(p)
    ew = p.edge("proj_w")
    em = p.edge("proj_m")
    ee = p.edge("proj_e")
    ea = p.edge("proj_a")
    ps_mm = [p.PSF[0], p.PSF[1]]
    yT = d.kD
    ee_b = ee.n
    ea_b = ea.n
    for kc in range(ND):
        pb.op("g", lambda E, kc=kc: E.dma_start(
            d.wW[:, kc, :], d.d_W_proj[kc * 128:(kc + 1) * 128, :]),
            inc=ew, inc_k=16)
    wt = ew.n
    nmm = 0
    for m in range(ND):
        for ch in range(4):
            pi = nmm % 2
            for kc in range(ND):
                w = [(ew, wt), (ee, ee_b + nmm - 1)] if kc == 0 else []
                pb.op("t", lambda E, m=m, ch=ch, kc=kc, pi=pi: E.matmul(
                    ps_mm[pi][:, :], d.wW[:, kc, m * 128:(m + 1) * 128],
                    yT[:, kc, ch * 512:(ch + 1) * 512],
                    start=(kc == 0), stop=(kc == ND - 1)),
                    wait=w, inc=em if kc == ND - 1 else None)
            bcol = POOL["b_proj"] + m
            pb.op("s", lambda E, pi=pi, bcol=bcol: E.activation(
                d.stmp[:, pi, :], ps_mm[pi][:, :], AF.Identity,
                bias=d.pool[:, bcol:bcol + 1]),
                wait=[(em, em.n), (ea, ea_b + nmm - 1)], inc=ee)
            pb.op("v", lambda E, m=m, ch=ch, pi=pi: E.tensor_tensor(
                d.arB[:, m, ch * 512:(ch + 1) * 512], d.stmp[:, pi, :],
                d.arA[:, m, ch * 512:(ch + 1) * 512], op=ALU.add),
                wait=[(ee, ee.n)], inc=ea)
            nmm += 1
    pbs.append(pb)

    if "F" not in stages:
        return
    pb = PB(p)
    emit_ln(p, pb,
            src=lambda dt: d.arB[:, dt, :],
            dst=lambda dt, ch: d.arA[:, dt, ch * 512:(ch + 1) * 512],
            name="p1ln")
    pbs.append(pb)



def alloc_p2(p, d):
    d.accum = p.sb("accum", [128, ND, T], F32)
    d.w1_sb = p.sb("w1_sb", [128, ND, 2048], BF16)
    d.w2_sb = p.sb("w2_sb", [128, 16, 1024], BF16)
    d.gateb = p.sb("gateb", [128, T], BF16)
    d.h_sb = p.sb("h_sb", [128, 16, 512], BF16)
    d.gbrow = p.sb("gbrow", [1, 4], BF16)
    d.gw_sb = p.sb("gw_sb", [128, ND, 4], BF16)
    d.wsel = p.sb("wsel", [128, NT, 4], F32)
    d.wselT = p.sb("wselT", [4, T], BF16)
    d.gtmp = p.sb("gtmp", [128, 8], F32)


def emit_p2(p, pbs, stages="GEL"):
    d = p.d
    pb = PB(p)
    eg = p.edge("gt_g")
    em = p.edge("gt_m")
    ev = p.edge("gt_v")
    es = p.edge("gt_s")
    etp = p.edge("gt_tp")
    ete = p.edge("gt_te")
    ps_l = p.PSF[5]
    ps_t = p.PSF[4]
    pb.op("g", lambda E: E.dma_start(
        d.gbrow[:, :], d.d_rows[0:1, ROWS["gate_b"]:ROWS["gate_b"] + 4]),
        inc=eg, inc_k=16)
    pb.op("g", lambda E: E.dma_start(
        d.gw_sb[:, :, :], d.d_gate_w.rearrange("(kc p) e -> p kc e", p=128)),
        inc=eg, inc_k=16)
    eg_t = eg.n
    ei = p.edge("fx_init")
    for dt in range(ND):
        pb.op("v", lambda E, dt=dt: E.tensor_copy(d.accum[:, dt, :], d.arA[:, dt, :]),
              inc=ei)
    for t in range(NT):
        for kc in range(ND):
            pb.op("t", lambda E, t=t, kc=kc: E.matmul(
                ps_l[:, (t % 2) * 4:(t % 2) * 4 + 4],
                d.arA[:, kc, t * 128:(t + 1) * 128], d.gw_sb[:, kc, :],
                start=(kc == 0), stop=False),
                wait=[(eg_t_dummy if False else eg, eg_t), (es, es.n)] if kc == 0 else ())
        pb.op("t", lambda E, t=t: E.matmul(
            ps_l[:, (t % 2) * 4:(t % 2) * 4 + 4], d.ones1[:, :],
            d.gbrow[0:1, 0:4],
            start=False, stop=True), inc=em)

        def soft(E, t=t):
            q = ps_l[:, (t % 2) * 4:(t % 2) * 4 + 4]
            E.reduce_max(d.gtmp[:, 0:1], q, axis=AX.X)
            E.drain()
            return E.tensor_scalar(d.gtmp[:, 1:2], d.gtmp[:, 0:1], -1.0, None,
                                   op0=ALU.mult)
        pb.op("v", soft, wait=[(em, em.n)], drain=True, inc=ev)
        pb.op("s", lambda E, t=t: E.activation(
            d.wsel[:, t, :], ps_l[:, (t % 2) * 4:(t % 2) * 4 + 4],
            AF.Exp, bias=d.gtmp[:, 1:2], accum_out=d.gtmp[:, 2:3]),
            wait=[(ev, ev.n)], drain=True, inc=es)

        def fin(E, t=t):
            E.reciprocal(d.gtmp[:, 3:4], d.gtmp[:, 2:3])
            E.drain()
            E.tensor_scalar(d.wsel[:, t, :], d.wsel[:, t, :], d.gtmp[:, 3:4], None,
                            op0=ALU.mult)
            E.drain()
            E.reduce_max(d.gtmp[:, 4:5], d.wsel[:, t, :], axis=AX.X)
            E.drain()
            E.tensor_scalar(d.ln_tmp[:, 0:4], d.wsel[:, t, :], d.gtmp[:, 4:5], None,
                            op0=ALU.is_ge)
            E.drain()
            E.scalar_tensor_tensor(d.ln_tmp[:, 4:8], d.ln_tmp[:, 0:4], -2.0,
                                   d.wsel[:, t, :], op0=ALU.mult, op1=ALU.add)
            E.drain()
            E.reduce_max(d.gtmp[:, 6:7], d.ln_tmp[:, 4:8], axis=AX.X)
            E.drain()
            E.tensor_scalar(d.ln_tmp[:, 0:4], d.wsel[:, t, :], d.gtmp[:, 6:7], None,
                            op0=ALU.is_ge)
            E.drain()
            return E.tensor_tensor(d.wsel[:, t, :], d.wsel[:, t, :], d.ln_tmp[:, 0:4],
                                   op=ALU.mult)
        pb.op("v", fin, wait=[(es, es.n)], drain=True, inc=ev)
    for t in range(NT):
        pb.op("t", lambda E, t=t: E.transpose(
            ps_t[0:4, (t % 4) * 128:(t % 4) * 128 + 128],
            d.wsel[:, t, :], d.identf[:, :]),
            wait=[(ev, ev.n), (ete, ete.n)] if t % 4 == 0 else (),
            inc=etp if t % 4 == 3 else None)
        if t % 4 == 3:
            pb.op("s", lambda E, t=t: E.activation(
                d.wselT[0:4, (t - 3) * 128:(t + 1) * 128], ps_t[0:4, 0:512], AF.Copy),
                wait=[(etp, etp.n)], inc=ete)
    pbs.append(pb)

    if hasattr(d, "d_dbg_wsel"):
        dpb = PB(p)
        eD = p.edge("p2dbg1")
        ev_g = p.edge("gt_v")
        dpb.op("y", lambda E: E.dma_start(d.d_dbg_wsel[:, :, :], d.wsel[:, :, :]),
               wait=[(ev_g, ev_g.n)], inc=eD, inc_k=16)
        dpb.op("y", lambda E: E.nop(), wait=[(eD, eD.n)])
        pbs.append(dpb)

    if "E" not in stages:
        return
    pb = PB(p)
    e_w1a = p.edge("fx_w1a")
    e_w1b = p.edge("fx_w1b")
    e_w2 = p.edge("fx_w2")
    e_gb = p.edge("fx_gb")
    e_gm = p.edge("fx_gm")
    e_hm = p.edge("fx_hm")
    e_he = p.edge("fx_he")
    e_ym = p.edge("fx_ym")
    e_ye = p.edge("fx_ye")
    ps_h = p.psh2
    ps_y = p.psy2
    experts = [
        ("s", 0, "d_sw1", "d_sw2", "sb1_0", "sb2_0"),
        ("s", 1, "d_sw1", "d_sw2", "sb1_1", "sb2_1"),
        ("r", 0, "d_rw1", "d_rw2", "rb1_0", "rb2_0"),
        ("r", 1, "d_rw1", "d_rw2", "rb1_1", "rb2_1"),
        ("r", 2, "d_rw1", "d_rw2", "rb1_2", "rb2_2"),
        ("r", 3, "d_rw1", "d_rw2", "rb1_3", "rb2_3"),
    ]
    gyu = 0
    ye_b = e_ye.n
    hm_a = 0
    ete_g = p.edge("gt_te")
    first_h = True
    for ei_, (kind, xi, w1n, w2n, b1n, b2n) in enumerate(experts):
        w1_d = getattr(d, w1n)
        w2_d = getattr(d, w2n)
        if kind == "s":
            pb.op("g", lambda E: E.memset(d.gateb[:, :], 0.5),
                  wait=[(e_ye, e_ye.n)], inc=e_gb)
        else:
            for c4 in range(4):
                pb.op("t", lambda E, c4=c4, xi=xi: E.matmul(
                    ps_y[c4][:, 0:512], d.bcsel8[0:4, xi * 128:xi * 128 + 128],
                    d.wselT[0:4, c4 * 512:(c4 + 1) * 512]),
                    wait=[(ete_g, ete_g.n), (e_ye, e_ye.n)] if c4 == 0 else (),
                    inc=e_gm)
            for c4 in range(4):
                pb.op("v", lambda E, c4=c4: E.tensor_copy(
                    d.gateb[:, c4 * 512:(c4 + 1) * 512], ps_y[c4][:, 0:512]),
                    wait=[(e_gm, e_gm.n)] if c4 == 0 else (), inc=e_gb)
        gb_t = e_gb.n
        for fh in range(2):
            for half, e_wh, hm_tgt in ((0, e_w1a, hm_a), (1, e_w1b, e_hm.n)):
                for kc in range(ND):
                    pb.op("g", lambda E, kc=kc, fh=fh, half=half, w1_d=w1_d, xi=xi:
                          E.dma_start(
                        d.w1_sb[:, kc, half * 1024:(half + 1) * 1024],
                        w1_d[xi, kc * 128:(kc + 1) * 128,
                             fh * 2048 + half * 1024:fh * 2048 + (half + 1) * 1024]),
                        wait=[(e_hm, hm_tgt)] if kc == 0 else (),
                        inc=e_wh, inc_k=16)
            for fc in range(16):
                pb.op("g", lambda E, fc=fc, fh=fh, w2_d=w2_d, xi=xi: E.dma_start(
                    d.w2_sb[:, fc, :],
                    w2_d[xi, fh * 2048 + fc * 128:fh * 2048 + (fc + 1) * 128, :]),
                    wait=[(e_ym, e_ym.n)] if fc == 0 else (),
                    inc=e_w2, inc_k=16)
            wt_a = e_w1a.n
            wt_b = e_w1b.n
            wt_2 = e_w2.n
            for ci, (cs, W) in enumerate(CHUNKS):
                for ft in range(16):
                    for kc in range(ND):
                        wv = ()
                        if kc == 0:
                            wv = [(e_w1a if ft < 8 else e_w1b,
                                   wt_a if ft < 8 else wt_b),
                                  (e_he, e_he.n - 3)]
                            if first_h:
                                wv.append((ete_g, 4))
                                first_h = False
                        pb.op("t", lambda E, ft=ft, kc=kc, cs=cs, W=W: E.matmul(
                            ps_h[ft % 4][:, 0:W],
                            d.w1_sb[:, kc, ft * 128:(ft + 1) * 128],
                            d.arA[:, kc, cs:cs + W],
                            start=(kc == 0), stop=(kc == ND - 1)),
                            wait=wv, inc=e_hm if kc == ND - 1 else None)
                    if ci == 3 and ft == 7:
                        hm_a = e_hm.n
                    bcol = POOL[b1n] + fh * 16 + ft
                    wv = [(e_hm, e_hm.n)]
                    if ft == 0:
                        wv.append((e_ym, e_ym.n))
                    pb.op("s", lambda E, ft=ft, bcol=bcol, W=W: E.activation(
                        d.h_sb[:, ft, 0:W], ps_h[ft % 4][:, 0:W], AF.Relu,
                        bias=d.pool[:, bcol:bcol + 1]),
                        wait=wv, inc=e_he)
                for ot in range(8):
                    wv = [(e_ye, ye_b + gyu - 3)]
                    if gyu == 0:
                        wv.append((ete_g, 4))
                    if ot == 0:
                        wv.append((e_he, e_he.n))
                        wv.append((e_w2, wt_2))
                        wv.append((e_gb, gb_t))
                    for ft in range(16):
                        last = (ft == 15)
                        pb.op("t", lambda E, ft=ft, ot=ot, last=last, W=W: E.matmul(
                            ps_y[ot % 4][:, 0:W],
                            d.w2_sb[:, ft, ot * 128:(ot + 1) * 128],
                            d.h_sb[:, ft, 0:W],
                            start=(ft == 0), stop=last),
                            wait=wv if ft == 0 else (),
                            inc=e_ym if last else None)
                    b2c = POOL[b2n] + ot

                    def yev(E, ot=ot, cs=cs, b2c=b2c, fh=fh, W=W):
                        r = None
                        for pc, w in (((0, 512), (512, W - 512)) if W > 512
                                      else ((0, 512),)):
                            c0 = cs + pc
                            if fh == 0:
                                E.scalar_tensor_tensor(
                                    d.ln_tmp[:, 0:w], ps_y[ot % 4][:, pc:pc + w],
                                    d.pool[:, b2c:b2c + 1], d.gateb[:, c0:c0 + w],
                                    op0=ALU.add, op1=ALU.mult)
                            else:
                                E.tensor_tensor(d.ln_tmp[:, 0:w],
                                                ps_y[ot % 4][:, pc:pc + w],
                                                d.gateb[:, c0:c0 + w], op=ALU.mult)
                            E.drain()
                            r = E.tensor_tensor(
                                d.accum[:, ot, c0:c0 + w],
                                d.accum[:, ot, c0:c0 + w],
                                d.ln_tmp[:, 0:w], op=ALU.add)
                            if pc == 0 and W > 512:
                                E.drain()
                        return r
                    pb.op("v", yev,
                          wait=[(e_ym, e_ym.n),
                                (p.edge("fx_init"), p.edge("fx_init").n)],
                          drain=True, inc=e_ye)
                    gyu += 1
    pbs.append(pb)

    if hasattr(d, "d_dbg_accum"):
        dpb = PB(p)
        eD = p.edge("p2dbg2")
        for dt in range(ND):
            dpb.op("y", lambda E, dt=dt: E.dma_start(
                d.d_dbg_accum[:, dt, :], d.accum[:, dt, :]),
                wait=[(e_ye, e_ye.n)] if dt == 0 else (), inc=eD, inc_k=16)
        dpb.op("y", lambda E: E.nop(), wait=[(eD, eD.n)])
        pbs.append(dpb)

    if "L" not in stages:
        return
    pb = PB(p)
    emit_ln(p, pb,
            src=lambda dt: d.accum[:, dt, :],
            dst=lambda dt, ch: d.arA[:, dt, ch * 512:(ch + 1) * 512],
            name="p2ln", src_f32=True,
            prewait=[(e_ye, e_ye.n)])
    pbs.append(pb)


def alloc_p3_outer(p, d):
    d.maskC = p.sb("maskC_sb", [128, 3584], BF16)
    d.aT = p.sb("aT", [128, ND, T], BF16)
    d.p_sb = p.sb("p_sb", [128, 3, 768], BF16)
    d.pt_sb = p.sb("pt_sb", [128, 2, 768], BF16)
    d.mx = p.sb("mx", [128, 8], F32)
    d.rows_bv3 = p.sb("rows_bv3", [1, 1024], BF16)


def alloc_p3_attn(p, d):
    d.wQ = p.sb("wQ", [128, ND, 1024], BF16)
    d.qT3 = p.sb("qT3", [128, ND, T], BF16)
    d.kT3 = p.sb("kT3", [128, ND, T], BF16)
    d.vE3 = p.sb("vE3", [128, NT, D], BF16)


def alloc_p3_s1(p, d):
    d.wO = p.sb("wO", [128, ND, 1024], BF16)
    d.s1 = p.sb("s1", [128, ND, T], BF16)
    d.stmp3 = p.sb("stmp3", [128, 2, 512], F32)


def alloc_p3_ffn(p, d):
    d.accum = p.sb("accum3", [128, ND, T], F32)
    d.chalf = p.sb("chalf", [128, 32], F32)
    d.w1_sb = p.sb("w1_sb3", [128, ND, 1024], BF16)
    d.w2_sb = p.sb("w2_sb3", [128, 8, 1024], BF16)
    d.h_sb = p.sb("h_sb3", [128, 8, 512], BF16)
    d.z_sb = p.sb("z_sb3", [128, 2, 512], F32)
    d.u_sb = p.sb("u_sb3", [128, 512], F32)


def emit_p3_attn(p, pbs):
    d = p.d
    pb = PB(p)
    ew = p.edge("p3b_w")
    em = p.edge("p3b_m")
    ee = p.edge("p3b_e")
    erow = p.edge("p3b_row")
    eM = p.edge("p3b_mask")
    ps_mm = [p.ps3_sc[:, 0:512], p.ps3_pt[:, 0:512]]
    pb.op("g", lambda E: E.dma_start(d.maskC[:, :], d.d_maskC[:, :]), inc=eM, inc_k=16)
    pb.op("g", lambda E: E.dma_start(
        d.rows_bv3[:, :], d.d_rows[0:1, ROWS["bv"]:ROWS["bv"] + 1024]),
        inc=erow, inc_k=16)
    nmm = 0
    specs = [("d_Wq", d.qT3, POOL["bq"]), ("d_Wk", d.kT3, POOL["bk"]),
             ("d_Wv", None, None)]
    for part, (wname, dst_ar, bpool) in enumerate(specs):
        w_d = getattr(d, wname)
        for kc in range(ND):
            pb.op("g", lambda E, kc=kc, w_d=w_d: E.dma_start(
                d.wQ[:, kc, :], w_d[kc * 128:(kc + 1) * 128, :]),
                wait=[(em, part * 32)] if kc == 0 else (), inc=ew, inc_k=16)
        if part < 2:
            for m in range(ND):
                for ch in range(4):
                    pi = nmm % 2
                    for kc in range(ND):
                        w = [(ew, (part + 1) * ND * 16), (ee, nmm - 1)] if kc == 0 else []
                        pb.op("t", lambda E, m=m, ch=ch, kc=kc, pi=pi: E.matmul(
                            ps_mm[pi], d.wQ[:, kc, m * 128:(m + 1) * 128],
                            d.arA[:, kc, ch * 512:(ch + 1) * 512],
                            start=(kc == 0), stop=(kc == ND - 1)),
                            wait=w, inc=em if kc == ND - 1 else None)
                    pb.op("s", lambda E, pi=pi, m=m, ch=ch, dst_ar=dst_ar, bpool=bpool:
                          E.activation(dst_ar[:, m, ch * 512:(ch + 1) * 512],
                                       ps_mm[pi], AF.Identity,
                                       bias=d.pool[:, bpool + m:bpool + m + 1]),
                          wait=[(em, em.n)], inc=ee)
                    nmm += 1
        else:
            for t in range(NT):
                for ch in range(2):
                    pi = nmm % 2
                    for kc in range(ND):
                        w = [(ew, 3 * ND * 16), (ee, nmm - 1)] if kc == 0 else []
                        pb.op("t", lambda E, t=t, ch=ch, kc=kc, pi=pi: E.matmul(
                            ps_mm[pi], d.arA[:, kc, t * 128:(t + 1) * 128],
                            d.wQ[:, kc, ch * 512:(ch + 1) * 512],
                            start=(kc == 0), stop=False),
                            wait=w)
                    pb.op("t", lambda E, ch=ch, pi=pi: E.matmul(
                        ps_mm[pi], d.ones1[:, :],
                        d.rows_bv3[0:1, ch * 512:ch * 512 + 512],
                        start=False, stop=True),
                        wait=[(erow, 16)], inc=em)
                    pb.op("s", lambda E, t=t, ch=ch, pi=pi: E.activation(
                        d.vE3[:, t, ch * 512:(ch + 1) * 512], ps_mm[pi], AF.Copy),
                        wait=[(em, em.n)], inc=ee)
                    nmm += 1
    pbs.append(pb)

    pb = PB(p)
    e_sc = p.edge("p3c_sc")
    e_ex = p.edge("p3c_ex")
    e_mk = p.edge("p3c_mk")
    e_rc = p.edge("p3c_rc")
    e_ptm = p.edge("p3c_ptm")
    e_pte = p.edge("p3c_pte")
    e_pv = p.edge("p3c_pv")
    e_ae = p.edge("p3c_ae")
    e_qkv3 = p.edge("p3b_e")
    qkv_done = e_qkv3.n
    NB = 8
    units = []
    for h in range(NHEAD):
        for n in range(NB):
            for qs in range(2):
                units.append((h, n, qs))
    NHU = len(units)

    def u_geom(k):
        h, n, qs = units[k]
        dt = h // 2
        ho = (h % 2) * 64
        kbs = [kb for kb in (n - 1, n, n + 1) if 0 <= kb < NB]
        if n == 0:
            moff = 1536 + qs * 512
        elif n == NB - 1:
            moff = 2560 + qs * 512
        else:
            moff = qs * 768
        return h, n, qs, dt, ho, kbs, moff

    def st_scores(k):
        h, n, qs, dt, ho, kbs, moff = u_geom(k)
        base = kbs[0] * 256
        W = 256 * len(kbs)
        segs = [(0, 512)] + ([(512, 256)] if W == 768 else [])
        for si, (c0, w) in enumerate(segs):
            wv = []
            if si == 0:
                wv = [(e_ex, k - 1), (e_qkv3, qkv_done if k == 0 else 0)]
            pb.op("t", lambda E, k=k, c0=c0, w=w, dt=dt, ho=ho, n=n, qs=qs,
                  base=base: E.matmul(
                p.ps3_scb[k % 2][:, c0:c0 + w],
                d.qT3[ho:ho + 64, dt, (n * 2 + qs) * 128:(n * 2 + qs) * 128 + 128],
                d.kT3[ho:ho + 64, dt, base + c0:base + c0 + w]),
                wait=wv, inc=e_sc if si == len(segs) - 1 else None)

    def st_exp(k):
        W = 256 * len(u_geom(k)[5])
        pb.op("s", lambda E, k=k, W=W: E.activation(
            d.p_sb[:, k % 3, 0:W], p.ps3_scb[k % 2][:, 0:W], AF.Exp, scale=0.125),
            wait=[(e_sc, k + 1), (e_ptm, k - 2)], inc=e_ex)

    def st_masknorm(k):
        _, _, _, _, _, kbs, moff = u_geom(k)
        W = 256 * len(kbs)
        sl = k % 3
        pb.op("v", lambda E, W=W, moff=moff, sl=sl: E.scalar_tensor_tensor(
            d.p_sb[:, sl, 0:W], d.p_sb[:, sl, 0:W], 1.0,
            d.maskC[:, moff:moff + W], op0=ALU.mult, op1=ALU.mult,
            accum_out=d.mx[:, sl:sl + 1]),
            wait=[(e_ex, k + 1)] + ([(p.edge("p3b_mask"), p.edge("p3b_mask").n)]
                                    if k == 0 else []),
            inc=e_mk)

        def vnorm(E, W=W, sl=sl):
            E.reciprocal(d.mx[:, 4 + sl:5 + sl], d.mx[:, sl:sl + 1])
            E.drain()
            return E.tensor_scalar(d.p_sb[:, sl, 0:W], d.p_sb[:, sl, 0:W],
                                   d.mx[:, 4 + sl:5 + sl], None, op0=ALU.mult)
        pb.op("v", vnorm, wait=[(e_mk, k + 1)], drain=True, inc=e_rc)

    def st_transp(k):
        nk2 = 2 * len(u_geom(k)[5])
        for j2 in range(nk2):
            pb.op("t", lambda E, k=k, j2=j2: E.transpose(
                p.ps3_ptb[k % 2][:, j2 * 128:j2 * 128 + 128],
                d.p_sb[:, k % 3, j2 * 128:j2 * 128 + 128], d.identb[:, :]),
                wait=[(e_rc, k + 1), (e_pte, k - 1)] if j2 == 0 else (),
                inc=e_ptm if j2 == nk2 - 1 else None)

    def st_evacpt(k):
        nk2 = 2 * len(u_geom(k)[5])
        pb.op("s", lambda E, k=k, nk2=nk2: E.activation(
            d.pt_sb[:, k % 2, 0:nk2 * 128], p.ps3_ptb[k % 2][:, 0:nk2 * 128],
            AF.Copy),
            wait=[(e_ptm, k + 1), (e_pv, k - 1)], inc=e_pte)

    def st_pv(k):
        h, n, qs, dt, ho, kbs, _ = u_geom(k)
        nk2 = 2 * len(kbs)
        for j2 in range(nk2):
            kt = kbs[j2 // 2] * 2 + (j2 % 2)
            pb.op("t", lambda E, k=k, j2=j2, kt=kt, h=h, ho=ho, nk2=nk2: E.matmul(
                p.ps3_pvb[k % 2][ho:ho + 64, 0:128],
                d.vE3[:, kt, h * 64:h * 64 + 64],
                d.pt_sb[:, k % 2, j2 * 128:j2 * 128 + 128],
                start=(j2 == 0), stop=(j2 == nk2 - 1)),
                wait=[(e_pte, k + 1), (e_ae, k - 1)] if j2 == 0 else (),
                inc=e_pv if j2 == nk2 - 1 else None)

    def st_evaca(k):
        h, n, qs, dt, ho, _, _ = u_geom(k)
        pb.op("s", lambda E, k=k, h=h, n=n, qs=qs, dt=dt, ho=ho: E.activation(
            d.aT[ho:ho + 64, dt, (n * 2 + qs) * 128:(n * 2 + qs) * 128 + 128],
            p.ps3_pvb[k % 2][ho:ho + 64, 0:128], AF.Copy),
            wait=[(e_pv, k + 1)], inc=e_ae)

    stages = [st_scores, st_exp, st_masknorm, st_transp, st_evacpt, st_pv,
              st_evaca]
    for it in range(NHU + len(stages) - 1):
        for si, st in enumerate(stages):
            k = it - si
            if 0 <= k < NHU:
                st(k)
    pbs.append(pb)


def emit_p3_s1(p, pbs):
    d = p.d
    pb = PB(p)
    ew = p.edge("p3d_w")
    em = p.edge("p3d_m")
    ee = p.edge("p3d_e")
    ea = p.edge("p3d_a")
    ps_mm = [p.PSF[0], p.PSF[1]]
    for kc in range(ND):
        pb.op("g", lambda E, kc=kc: E.dma_start(
            d.wO[:, kc, :], d.d_Wo[kc * 128:(kc + 1) * 128, :]),
            inc=ew, inc_k=16)
    nmm = 0
    eaT = p.edge("p3c_ae")
    for m in range(ND):
        for ch in range(4):
            pi = nmm % 2
            for kc in range(ND):
                w = [(ew, ND * 16), (ee, nmm - 1)] if kc == 0 else []
                if nmm == 0 and kc == 0:
                    w.append((eaT, eaT.n))
                pb.op("t", lambda E, m=m, ch=ch, kc=kc, pi=pi: E.matmul(
                    ps_mm[pi][:, :], d.wO[:, kc, m * 128:(m + 1) * 128],
                    d.aT[:, kc, ch * 512:(ch + 1) * 512],
                    start=(kc == 0), stop=(kc == ND - 1)),
                    wait=w, inc=em if kc == ND - 1 else None)
            bcol = POOL["bo"] + m
            pb.op("s", lambda E, pi=pi, bcol=bcol: E.activation(
                d.stmp3[:, pi, :], ps_mm[pi][:, :], AF.Identity,
                bias=d.pool[:, bcol:bcol + 1]),
                wait=[(em, em.n), (ea, nmm - 1)], inc=ee)
            pb.op("v", lambda E, m=m, ch=ch, pi=pi: E.tensor_tensor(
                d.s1[:, m, ch * 512:(ch + 1) * 512], d.stmp3[:, pi, :],
                d.arA[:, m, ch * 512:(ch + 1) * 512], op=ALU.add),
                wait=[(ee, ee.n)], drain=True, inc=ea)
            nmm += 1
    pbs.append(pb)
    pb = PB(p)
    emit_ln(p, pb,
            src=lambda dt: d.s1[:, dt, :],
            dst=lambda dt, ch: d.aT[:, dt, ch * 512:(ch + 1) * 512],
            gain=POOL["g1"], bias=POOL["b1"], name="p3ln1",
            prewait=[(p.edge("p3d_a"), p.edge("p3d_a").n)])
    pbs.append(pb)


def emit_p3_ffn(p, pbs):
    d = p.d
    pb = PB(p)
    e_w = p.edge("p3f_w")
    e_hm = p.edge("p3f_hm")
    e_ze = p.edge("p3f_ze")
    e_gl = p.edge("p3f_gl")
    e_th = p.edge("p3f_th")
    e_ym = p.edge("p3f_ym")
    e_ye = p.edge("p3f_ye")
    ei = p.edge("p3f_init")
    ps_h = p.psh2
    ps_y = p.psy2
    ech = p.edge("p3f_ch")
    pb.op("v", lambda E: E.tensor_scalar(
        d.chalf[:, :], d.pool[:, POOL["c1"]:POOL["c1"] + 32], 0.5, None, op0=ALU.mult),
        inc=ech)
    for dt in range(ND):
        pb.op("v", lambda E, dt=dt: E.tensor_copy(d.accum[:, dt, :], d.aT[:, dt, :]),
              wait=[(p.edge("ln_ap"), p.edge("ln_ap").n)] if dt == 0 else (),
              inc=ei)
    gyu = 0
    yb4 = e_ye.n
    for q4 in range(4):
        for kc in range(ND):
            pb.op("g", lambda E, kc=kc, q4=q4: E.dma_start(
                d.w1_sb[:, kc, :],
                d.d_W1[kc * 128:(kc + 1) * 128, q4 * 1024:(q4 + 1) * 1024]),
                wait=[(e_ym, e_ym.n)] if kc == 0 else (), inc=e_w, inc_k=16)
        for fc in range(8):
            pb.op("g", lambda E, fc=fc, q4=q4: E.dma_start(
                d.w2_sb[:, fc, :],
                d.d_W2[q4 * 1024 + fc * 128:q4 * 1024 + (fc + 1) * 128, :]),
                inc=e_w, inc_k=16)
        wtarget = e_w.n
        for tp in range(4):
            cs = tp * 512
            for ft in range(8):
                for kc in range(ND):
                    wv2 = []
                    if kc == 0:
                        wv2 = [(e_w, wtarget), (e_ze, e_ze.n - 3)]
                        if q4 == 0 and tp == 0 and ft == 0:
                            wv2.append((p.edge("ln_ap"),
                                        p.edge("ln_ap").n))
                    pb.op("t", lambda E, ft=ft, kc=kc, cs=cs: E.matmul(
                        ps_h[ft % 4][:, 0:512],
                        d.w1_sb[:, kc, ft * 128:(ft + 1) * 128],
                        d.aT[:, kc, cs:cs + 512],
                        start=(kc == 0), stop=(kc == ND - 1)),
                        wait=wv2,
                        inc=e_hm if kc == ND - 1 else None)
                bcol = POOL["c1"] + q4 * 8 + ft
                wv = [(e_hm, e_hm.n), (e_gl, e_gl.n - 2)]
                if ft == 0:
                    wv.append((e_ym, e_ym.n))
                pb.op("s", lambda E, ft=ft, bcol=bcol: E.activation(
                    d.z_sb[:, ft % 2, :], ps_h[ft % 4][:, 0:512], AF.Identity,
                    scale=0.5, bias=d.chalf[:, bcol - POOL["c1"]:bcol - POOL["c1"] + 1]),
                    wait=wv + [(ech, 1)], inc=e_ze)

                def gel1(E, ft=ft):
                    E.drain()
                    E.tensor_tensor(d.u_sb[:, :], d.z_sb[:, ft % 2, :],
                                    d.z_sb[:, ft % 2, :], op=ALU.mult)
                    E.drain()
                    E.tensor_scalar(d.u_sb[:, :], d.u_sb[:, :], 0.2854192435,
                                    1.5957691216, op0=ALU.mult, op1=ALU.add)
                    E.drain()
                    return E.tensor_tensor(d.u_sb[:, :], d.u_sb[:, :],
                                           d.z_sb[:, ft % 2, :], op=ALU.mult)
                pb.op("v", gel1, wait=[(e_ze, e_ze.n)], drain=True, inc=e_gl)
                pb.op("s", lambda E: E.activation(d.u_sb[:, :], d.u_sb[:, :], AF.Tanh),
                      wait=[(e_gl, e_gl.n)], drain=True, inc=e_th)
                pb.op("v", lambda E, ft=ft: E.scalar_tensor_tensor(
                    d.h_sb[:, ft, :], d.u_sb[:, :], 1.0, d.z_sb[:, ft % 2, :],
                    op0=ALU.add, op1=ALU.mult),
                    wait=[(e_th, e_th.n)], drain=True, inc=e_gl)
            for ot in range(8):
                wv = [(e_ye, yb4 + gyu - 3)]
                if ot == 0:
                    wv.append((e_gl, e_gl.n))
                for ft in range(8):
                    last = (ft == 7)
                    pb.op("t", lambda E, ft=ft, ot=ot, last=last: E.matmul(
                        ps_y[ot % 4][:, 0:512],
                        d.w2_sb[:, ft, ot * 128:(ot + 1) * 128],
                        d.h_sb[:, ft, :],
                        start=(ft == 0), stop=last),
                        wait=wv if ft == 0 else (),
                        inc=e_ym if last else None)

                def yev(E, ot=ot, cs=cs, q4=q4):
                    E.drain()
                    if q4 == 0:
                        c2c = POOL["c2"] + ot
                        return E.scalar_tensor_tensor(
                            d.accum[:, ot, cs:cs + 512], ps_y[ot % 4][:, 0:512],
                            d.pool[:, c2c:c2c + 1], d.accum[:, ot, cs:cs + 512],
                            op0=ALU.add, op1=ALU.add)
                    return E.tensor_tensor(
                        d.accum[:, ot, cs:cs + 512], d.accum[:, ot, cs:cs + 512],
                        ps_y[ot % 4][:, 0:512], op=ALU.add)
                pb.op("v", yev, wait=[(e_ym, e_ym.n), (ei, ND)],
                      drain=True, inc=e_ye)
                gyu += 1
    pbs.append(pb)
    pb = PB(p)
    emit_ln(p, pb,
            src=lambda dt: d.accum[:, dt, :],
            dst=lambda dt, ch: d.aT[:, dt, ch * 512:(ch + 1) * 512],
            gain=POOL["g2"], bias=POOL["b2"], name="p3ln2", src_f32=True,
            prewait=[(p.edge("p3f_ye"), p.edge("p3f_ye").n)])
    pbs.append(pb)
    pb = PB(p)
    e2 = p.edge("p3_s2")
    for dt in range(ND):
        pb.op("v", lambda E, dt=dt: E.tensor_tensor(
            d.accum[:, dt, :], d.arA[:, dt, :], d.aT[:, dt, :], op=ALU.add),
            wait=[(p.edge("ln_ap"), p.edge("ln_ap").n)] if dt == 0 else (),
            drain=True, inc=e2)
    emit_ln(p, pb,
            src=lambda dt: d.accum[:, dt, :],
            dst=lambda dt, ch: d.arA[:, dt, ch * 512:(ch + 1) * 512],
            name="p3ln3", src_f32=True,
            prewait=[(e2, ND)])
    pbs.append(pb)


def alloc_p4(p, d):
    d.accum = p.sb("accum4", [128, ND, T], F32)
    d.w1_sb = p.sb("w1_sb4", [128, ND, 2048], BF16)
    d.w2_sb = p.sb("w2_sb4", [128, 16, 1024], BF16)
    d.gateb = p.sb("gateb4", [128, 512], BF16)
    d.h_sb = p.sb("h_sb4", [128, 16, 512], BF16)
    d.cw_sb = p.sb("cw_sb", [128, ND, 8], BF16)
    d.wsel4 = p.sb("wsel4", [128, NT, 8], F32)
    d.probsT = p.sb("probsT", [8, T], BF16)
    d.wT8 = p.sb("wT8", [8, T], BF16)
    d.bs = p.sb("bs", [8, 8], F32)
    d.gtmp4 = p.sb("gtmp4", [128, 8], F32)


def emit_p4(p, pbs):
    d = p.d
    pb = PB(p)
    eg = p.edge("gt_g")
    em = p.edge("gt_m")
    ev = p.edge("gt_v")
    es = p.edge("gt_s")
    etp = p.edge("gt_tp")
    ete = p.edge("gt_te")
    ebi = p.edge("gt_bi")
    ps_l = p.PSF[5]
    ps_t = p.PSF[4]
    pb.op("g", lambda E: E.dma_start(
        d.cw_sb[:, :, :], d.d_choice_wT.rearrange("(kc p) e -> p kc e", p=128)),
        inc=eg, inc_k=16)
    eg_t = eg.n
    ebsi = p.edge("p4_bsi")
    pb.op("g", lambda E: E.memset(d.bs[:, 0:1], 0.0))
    pb.op("g", lambda E: E.memset(d.bs[:, 1:2], 1.0), inc=ebsi)
    ei = p.edge("fx_init")
    for dt in range(ND):
        pb.op("v", lambda E, dt=dt: E.tensor_copy(d.accum[:, dt, :], d.arA[:, dt, :]),
              inc=ei)
    for t in range(NT):
        for kc in range(ND):
            pb.op("t", lambda E, t=t, kc=kc: E.matmul(
                ps_l[:, (t % 2) * 8:(t % 2) * 8 + 8],
                d.arA[:, kc, t * 128:(t + 1) * 128], d.cw_sb[:, kc, :],
                start=(kc == 0), stop=(kc == ND - 1)),
                wait=[(eg, eg_t), (es, es.n)] if kc == 0 else (),
                inc=em if kc == ND - 1 else None)

        def soft(E, t=t):
            q = ps_l[:, (t % 2) * 8:(t % 2) * 8 + 8]
            E.reduce_max(d.gtmp4[:, 0:1], q, axis=AX.X)
            E.drain()
            return E.tensor_scalar(d.gtmp4[:, 1:2], d.gtmp4[:, 0:1], -1.0, None,
                                   op0=ALU.mult)
        pb.op("v", soft, wait=[(em, em.n)], drain=True, inc=ev)
        pb.op("s", lambda E, t=t: E.activation(
            d.wsel4[:, t, :], ps_l[:, (t % 2) * 8:(t % 2) * 8 + 8],
            AF.Exp, bias=d.gtmp4[:, 1:2], accum_out=d.gtmp4[:, 2:3]),
            wait=[(ev, ev.n)], drain=True, inc=es)

        def fin(E, t=t):
            E.reciprocal(d.gtmp4[:, 3:4], d.gtmp4[:, 2:3])
            E.drain()
            return E.tensor_scalar(d.wsel4[:, t, :], d.wsel4[:, t, :],
                                   d.gtmp4[:, 3:4], None, op0=ALU.mult)
        pb.op("v", fin, wait=[(es, es.n)], drain=True, inc=ev)
    for t in range(NT):
        pb.op("t", lambda E, t=t: E.transpose(
            ps_t[0:8, (t % 4) * 128:(t % 4) * 128 + 128],
            d.wsel4[:, t, :], d.identf[:, :]),
            wait=[(ev, ev.n), (ete, ete.n)] if t % 4 == 0 else (),
            inc=etp if t % 4 == 3 else None)
        if t % 4 == 3:
            pb.op("s", lambda E, t=t: E.activation(
                d.probsT[0:8, (t - 3) * 128:(t + 1) * 128], ps_t[0:8, 0:512], AF.Copy),
                wait=[(etp, etp.n)], inc=ete)
    def bisect(E):
        for it in range(20):
            E.drain()
            E.tensor_tensor(d.bs[:, 2:3], d.bs[:, 0:1], d.bs[:, 1:2], op=ALU.add)
            E.drain()
            E.tensor_scalar(d.bs[:, 3:4], d.bs[:, 2:3], 0.5, None, op0=ALU.mult)
            E.drain()
            E.tensor_scalar(d.wT8[0:8, :], d.probsT[0:8, :], d.bs[:, 3:4][0:8, :],
                            0.0, op0=ALU.is_ge, op1=ALU.add,
                            accum_out=d.bs[:, 4:5])
            E.drain()
            E.tensor_scalar(d.bs[:, 5:6], d.bs[:, 4:5], 512.0, None, op0=ALU.is_ge)
            E.drain()
            E.tensor_tensor(d.bs[:, 6:7], d.bs[:, 3:4], d.bs[:, 0:1], op=ALU.subtract)
            E.drain()
            E.tensor_tensor(d.bs[:, 6:7], d.bs[:, 6:7], d.bs[:, 5:6], op=ALU.mult)
            E.drain()
            E.tensor_tensor(d.bs[:, 0:1], d.bs[:, 0:1], d.bs[:, 6:7], op=ALU.add)
            E.drain()
            E.tensor_scalar(d.bs[:, 7:8], d.bs[:, 5:6], -1.0, 1.0, op0=ALU.mult,
                            op1=ALU.add)
            E.drain()
            E.tensor_tensor(d.bs[:, 6:7], d.bs[:, 3:4], d.bs[:, 1:2], op=ALU.subtract)
            E.drain()
            E.tensor_tensor(d.bs[:, 6:7], d.bs[:, 6:7], d.bs[:, 7:8], op=ALU.mult)
            E.drain()
            E.tensor_tensor(d.bs[:, 1:2], d.bs[:, 1:2], d.bs[:, 6:7], op=ALU.add)
        E.drain()
        E.tensor_scalar(d.wT8[0:8, :], d.probsT[0:8, :], d.bs[:, 0:1][0:8, :],
                        None, op0=ALU.is_ge)
        E.drain()
        return E.tensor_tensor(d.wT8[0:8, :], d.probsT[0:8, :], d.wT8[0:8, :],
                               op=ALU.mult)
    pb.op("v", bisect, wait=[(ete, ete.n), (eg, eg.n), (ebsi, 1)], drain=True,
          inc=ebi)
    pbs.append(pb)

    pb = PB(p)
    e_w1a = p.edge("fx_w1a")
    e_w1b = p.edge("fx_w1b")
    e_w2 = p.edge("fx_w2")
    e_gb = p.edge("fx_gb")
    e_gm = p.edge("fx_gm")
    e_hm = p.edge("fx_hm")
    e_he = p.edge("fx_he")
    e_ym = p.edge("fx_ym")
    e_ye = p.edge("fx_ye")
    ps_h = p.psh2
    ps_y = p.psy2
    gyu = 0
    ye_b = e_ye.n
    hm_a = e_hm.n
    for ex in range(8):
        for half, e_wh, hm_tgt in ((0, e_w1a, hm_a), (1, e_w1b, e_hm.n)):
            for kc in range(ND):
                pb.op("g", lambda E, kc=kc, half=half, ex=ex: E.dma_start(
                    d.w1_sb[:, kc, half * 1024:(half + 1) * 1024],
                    d.d_fw1[ex, kc * 128:(kc + 1) * 128,
                            half * 1024:(half + 1) * 1024]),
                    wait=[(e_hm, hm_tgt)] if kc == 0 else (),
                    inc=e_wh, inc_k=16)
        for fc in range(16):
            pb.op("g", lambda E, fc=fc, ex=ex: E.dma_start(
                d.w2_sb[:, fc, :], d.d_fw2[ex, fc * 128:(fc + 1) * 128, :]),
                wait=[(e_ym, e_ym.n)] if fc == 0 else (),
                inc=e_w2, inc_k=16)
        wt_a = e_w1a.n
        wt_b = e_w1b.n
        wt_2 = e_w2.n
        for ci, (cs, W) in enumerate(CHUNKS4):
            pb.op("t", lambda E, ex=ex, cs=cs, W=W: E.matmul(
                ps_y[0][:, 0:W], d.bcsel8[0:8, ex * 128:ex * 128 + 128],
                d.wT8[0:8, cs:cs + W]),
                wait=[(ebi, 1), (e_ye, ye_b + gyu - 3)], inc=e_gm)
            pb.op("v", lambda E, W=W: E.tensor_copy(
                d.gateb[:, 0:W], ps_y[0][:, 0:W]),
                wait=[(e_gm, e_gm.n)], inc=e_gb)
            gb_t = e_gb.n
            for ft in range(16):
                for kc in range(ND):
                    pb.op("t", lambda E, ft=ft, kc=kc, cs=cs, W=W: E.matmul(
                        ps_h[ft % 4][:, 0:W],
                        d.w1_sb[:, kc, ft * 128:(ft + 1) * 128],
                        d.arA[:, kc, cs:cs + W],
                        start=(kc == 0), stop=(kc == ND - 1)),
                        wait=([(e_w1a if ft < 8 else e_w1b,
                                wt_a if ft < 8 else wt_b),
                               (e_he, e_he.n - 3)] if kc == 0 else ()),
                        inc=e_hm if kc == ND - 1 else None)
                if ci == 3 and ft == 7:
                    hm_a = e_hm.n
                wv = [(e_hm, e_hm.n)]
                if ft == 0:
                    wv.append((e_ym, e_ym.n))
                pb.op("s", lambda E, ft=ft, W=W: E.activation(
                    d.h_sb[:, ft, 0:W], ps_h[ft % 4][:, 0:W], AF.Silu),
                    wait=wv, inc=e_he)
            for ot in range(8):
                wv = [(e_ye, ye_b + gyu - 3)]
                if ot == 0:
                    wv.append((e_he, e_he.n))
                    wv.append((e_w2, wt_2))
                    wv.append((e_gb, gb_t))
                for ft in range(16):
                    last = (ft == 15)
                    pb.op("t", lambda E, ft=ft, ot=ot, last=last, W=W: E.matmul(
                        ps_y[ot % 4][:, 0:W],
                        d.w2_sb[:, ft, ot * 128:(ot + 1) * 128],
                        d.h_sb[:, ft, 0:W],
                        start=(ft == 0), stop=last),
                        wait=wv if ft == 0 else (),
                        inc=e_ym if last else None)

                def yev(E, ot=ot, cs=cs, W=W):
                    r = None
                    for pc, w in (((0, 512), (512, W - 512)) if W > 512
                                  else ((0, 512),)):
                        E.tensor_tensor(d.ln_tmp[:, 0:w],
                                        ps_y[ot % 4][:, pc:pc + w],
                                        d.gateb[:, pc:pc + w], op=ALU.mult)
                        E.drain()
                        r = E.tensor_tensor(
                            d.accum[:, ot, cs + pc:cs + pc + w],
                            d.accum[:, ot, cs + pc:cs + pc + w],
                            d.ln_tmp[:, 0:w], op=ALU.add)
                        if pc == 0 and W > 512:
                            E.drain()
                    return r
                pb.op("v", yev, wait=[(e_ym, e_ym.n),
                                      (p.edge("fx_init"), p.edge("fx_init").n)],
                      drain=True, inc=e_ye)
                gyu += 1
    pbs.append(pb)

    pb = PB(p)
    emit_ln(p, pb,
            src=lambda dt: d.accum[:, dt, :],
            dst=lambda dt, ch: d.accum[:, dt, ch * 512:(ch + 1) * 512],
            name="p4ln", src_f32=True,
            prewait=[(p.edge("fx_ye"), p.edge("fx_ye").n)])
    pbs.append(pb)
    pb = PB(p)
    e_tp = p.edge("p4o_tp")
    e_te = p.edge("p4o_te")
    e_st = [p.edge("p4o_st0"), p.edge("p4o_st1")]
    eap = p.edge("ln_ap")
    ps_o = [p.PSF[0], p.PSF[1]]
    g = 0
    for t in range(NT):
        for half in range(2):
            for dj in range(4):
                dt = half * 4 + dj
                pb.op("t", lambda E, t=t, dt=dt, dj=dj, g=g: E.transpose(
                    ps_o[g % 2][:, dj * 128:dj * 128 + 128],
                    d.accum[:, dt, t * 128:(t + 1) * 128], d.identf[:, :]),
                    wait=([(eap, eap.n)] if g == 0 else []) +
                         ([(e_te, e_te.n - 1)] if dj == 0 else []),
                    inc=e_tp if dj == 3 else None)
            pb.op("s", lambda E, t=t, half=half, g=g: E.activation(
                d.accum[:, half * 4:half * 4 + 4, t * 128:(t + 1) * 128],
                ps_o[g % 2][:, 0:512].rearrange("p (a b) -> p a b", b=128), AF.Copy),
                wait=[(e_tp, e_tp.n)], inc=e_te)
            g += 1
        pb.op("y", lambda E, t=t: E.dma_start(
            d.d_out[t * 128:(t + 1) * 128, :],
            d.accum[:, :, t * 128:(t + 1) * 128]),
            wait=[(e_te, e_te.n)], inc=e_st[t % 2], inc_k=16)
    pb.op("y", lambda E: E.nop(), wait=[(e_st[0], e_st[0].n), (e_st[1], e_st[1].n)])
    pbs.append(pb)

def alloc_common(p, io):
    d = NS()
    d.__dict__.update(io)
    d.arA = p.sb("arA", [128, ND, T], BF16)
    d.pool = p.sb("pool", [128, NPOOL], F32)
    d.identb = p.sb("identb", [128, 128], BF16)
    d.identf = p.sb("identf", [128, 128], F32)
    d.e2 = p.sb("e2", [128, 2], BF16)
    d.ones1 = p.sb("ones1", [1, 128], BF16)
    d.onescol = p.sb("onescol", [128, 1], BF16)
    d.onescol_f = p.sb("onescol_f", [128, 1], F32)
    d.cst = p.sb("cst", [128, 4], F32)
    d.bcsel8 = p.sb("bcsel8_sb", [8, 8 * 128], BF16)
    d.ln_rowAb = p.sb("ln_rowAb", [1, 512], BF16)
    d.ln_rowBb = p.sb("ln_rowBb", [1, 512], BF16)
    d.ln_r1 = p.sb("ln_r1", [1, 512], F32)
    d.ln_r2 = p.sb("ln_r2", [1, 512], F32)
    d.ln_sq = p.sb("ln_sq", [128, 1024], BF16)
    d.ln_tmp = p.sb("ln_tmp", [128, 512], F32)
    p.d = d
    return d


def alloc_p1(p, d):
    d.maskA = p.sb("maskA", [128, 128], BF16)
    d.qC = p.sb("qC", [128, ND, T], BF16)
    d.kD = p.sb("kD", [128, ND, T], BF16)
    d.arB = d.qC
    d.vE = p.sb("vE", [128, NT, D], BF16)
    d.yF2 = p.sb("yF2", [128, NT, 128], BF16)
    d.xld = p.sb("xld", [128, 2, D], BF16)
    d.rows_bv = p.sb("rows_bv", [1, 1024], BF16)
    d.wW = p.sb("wW0", [128, ND, 1024], BF16)
    d.om_f = p.sb("om_f", [128, 128], F32)
    d.om = p.sb("om", [128, 128], BF16)
    d.nssqT = p.sb("nssqT", [128, NT, 32], F32)
    d.phiQ = p.sb("phiQ", [128, 2, T], BF16)
    d.phiK = p.sb("phiK", [128, 2, T], BF16)
    d.phiQT = p.sb("phiQT", [128, 2, T], BF16)
    d.phiKT = p.sb("phiKT", [128, 2, T], BF16)
    d.S_sb = p.sb("S_sb", [128, 2, 65], BF16)
    d.S_acc = p.sb("S_acc", [128, 2, 65], F32)
    d.Am = p.sb("Am", [128, 2, 128], BF16)
    d.den = p.sb("den", [128, 8], F32)
    d.stmp = p.sb("stmp", [128, 2, 512], BF16)


def host_constants():
    identb = np.eye(128, dtype=BF)
    identf = np.eye(128, dtype=np.float32)
    maskA = (np.arange(128)[None, :] >= np.arange(128)[:, None]).astype(BF)
    i0 = np.arange(128)[:, None]
    j = np.arange(256)[None, :]
    mL0 = (j >= i0).astype(BF)
    mL1 = (j >= i0 + 128).astype(BF)
    mR0 = (j <= i0).astype(BF)
    mR1 = (j <= i0 + 128).astype(BF)
    maskP = np.ascontiguousarray(np.concatenate([mL0, mL1, mR0, mR1], axis=1))
    e2 = np.zeros((128, 2), dtype=BF)
    e2[0:64, 0] = BF(-0.0078125)
    e2[64:128, 1] = BF(-0.0078125)
    bcsel = np.zeros((4, 512), dtype=BF)
    for e in range(4):
        bcsel[e, e * 128:(e + 1) * 128] = BF(1.0)
    i = np.arange(128)[:, None]
    j = np.arange(256)[None, :]
    ones = np.ones((128, 256), dtype=np.float32)
    mC = np.zeros((128, 3584), dtype=np.float32)
    for qs in range(2):
        L = (j >= i + qs * 128).astype(np.float32)
        R = (j <= i + qs * 128).astype(np.float32)
        mC[:, qs * 768:qs * 768 + 768] = np.concatenate([L, ones, R], axis=1)
        mC[:, 1536 + qs * 512:1536 + qs * 512 + 512] = np.concatenate([ones, R], axis=1)
        mC[:, 2560 + qs * 512:2560 + qs * 512 + 512] = np.concatenate([L, ones], axis=1)
    bcsel8 = np.zeros((8, 1024), dtype=BF)
    for e in range(8):
        bcsel8[e, e * 128:(e + 1) * 128] = BF(1.0)
    return dict(identb=identb, identf=identf, maskA=maskA, maskP=maskP, e2=e2,
                bcsel=bcsel, maskC=mC.astype(BF), bcsel8=bcsel8)


def host_pool(inp):
    pool = np.zeros((NPOOL, 128), dtype=np.float32)

    def put(name, arr):
        arr = np.asarray(arr).reshape(-1)
        k = arr.shape[0] // 128
        pool[POOL[name]:POOL[name] + k, :] = arr.reshape(k, 128)

    put("b_attn", inp["L0_attn_b_attn"])
    put("b_proj", inp["L0_attn_b_proj"])
    put("sb1_0", inp["L0_ffn_sb1"][0]); put("sb1_1", inp["L0_ffn_sb1"][1])
    for e in range(4):
        put("rb1_%d" % e, inp["L0_ffn_rb1"][e])
    put("bq", inp["L1_attn_bq"]); put("bk", inp["L1_attn_bk"])
    put("bo", inp["L1_attn_bo"])
    put("g1", inp["L1_attn_g1"]); put("b1", inp["L1_attn_b1"])
    put("c1", inp["L1_attn_c1"]); put("c2", inp["L1_attn_c2"])
    put("g2", inp["L1_attn_g2"]); put("b2", inp["L1_attn_b2"])
    put("sb2_0", inp["L0_ffn_sb2"][0]); put("sb2_1", inp["L0_ffn_sb2"][1])
    for e in range(4):
        put("rb2_%d" % e, inp["L0_ffn_rb2"][e])
    pool_dev = np.ascontiguousarray(pool.T)

    rows = np.zeros((1, NROW), dtype=np.float32)

    def putr(name, arr):
        arr = np.asarray(arr).reshape(-1)
        rows[0, ROWS[name]:ROWS[name] + arr.shape[0]] = arr

    putr("bv", inp["L1_attn_bv"])
    putr("bav", inp["L0_attn_b_attn"][2048:3072])
    putr("gate_b", inp["L0_ffn_gate_b"])
    putr("sb2_0", inp["L0_ffn_sb2"][0]); putr("sb2_1", inp["L0_ffn_sb2"][1])
    for e in range(4):
        putr("rb2_%d" % e, inp["L0_ffn_rb2"][e])
    return pool_dev, rows


def const_inmap(inp):
    cst = host_constants()
    pool_dev, rows = host_pool(inp)
    return {
        "pool_in": pool_dev, "rows_in": rows,
        "identb_in": cst["identb"], "identf_in": cst["identf"],
        "maskA_in": cst["maskA"], "maskP_in": cst["maskP"], "e2_in": cst["e2"],
        "bcsel": cst["bcsel"], "maskC": cst["maskC"], "bcsel8": cst["bcsel8"],
    }


def build_program(phases=("p1", "p2", "p3", "p4"), stream_in=False, stream_out=False):
    nc = bass.Bass("TRN2", target_bir_lowering=False)
    io = {}
    io["d_pool"] = nc.dram_tensor("pool_in", [128, NPOOL], F32, kind="ExternalInput")
    io["d_rows"] = nc.dram_tensor("rows_in", [1, NROW], F32, kind="ExternalInput")
    io["d_identb"] = nc.dram_tensor("identb_in", [128, 128], BF16, kind="ExternalInput")
    io["d_identf"] = nc.dram_tensor("identf_in", [128, 128], F32, kind="ExternalInput")
    io["d_maskA"] = nc.dram_tensor("maskA_in", [128, 128], BF16, kind="ExternalInput")
    io["d_maskP"] = nc.dram_tensor("maskP_in", [128, 1024], BF16, kind="ExternalInput")
    io["d_e2"] = nc.dram_tensor("e2_in", [128, 2], BF16, kind="ExternalInput")
    if "p1" in phases:
        io["d_x"] = nc.dram_tensor("x_in", [T, D], F32, kind="ExternalInput")
        io["d_W_attn"] = nc.dram_tensor("W_attn", [D, 3 * D], F32, kind="ExternalInput")
        io["d_W_proj"] = nc.dram_tensor("W_proj", [D, D], F32, kind="ExternalInput")
        io["d_omega"] = nc.dram_tensor("omega", [64, 128], F32, kind="ExternalInput")
    if "p2" in phases:
        import os as _os2
        if _os2.environ.get("P2_DBG"):
            io["d_dbg_wsel"] = nc.dram_tensor("dbg_wsel", [128, NT, 4], F32,
                                              kind="ExternalOutput")
            io["d_dbg_accum"] = nc.dram_tensor("dbg_accum", [128, ND, T], F32,
                                               kind="ExternalOutput")
            io["d_dbg_gateb"] = nc.dram_tensor("dbg_gateb", [6, 128, T], BF16,
                                               kind="ExternalOutput")
        io["d_gate_w"] = nc.dram_tensor("gate_w", [D, 4], F32, kind="ExternalInput")
        io["d_sw1"] = nc.dram_tensor("sw1", [2, D, 4096], F32, kind="ExternalInput")
        io["d_sw2"] = nc.dram_tensor("sw2", [2, 4096, D], F32, kind="ExternalInput")
        io["d_rw1"] = nc.dram_tensor("rw1", [4, D, 4096], F32, kind="ExternalInput")
        io["d_rw2"] = nc.dram_tensor("rw2", [4, 4096, D], F32, kind="ExternalInput")
        io["d_bcsel"] = nc.dram_tensor("bcsel", [4, 512], BF16, kind="ExternalInput")
    io["d_bcsel8"] = nc.dram_tensor("bcsel8", [8, 1024], BF16,
                                    kind="ExternalInput")
    if "p4" in phases:
        io["d_out"] = nc.dram_tensor("out", [T, D], F32, kind="ExternalOutput")
        io["d_choice_wT"] = nc.dram_tensor("choice_wT", [D, 8], F32,
                                           kind="ExternalInput")
        io["d_fw1"] = nc.dram_tensor("fw1", [8, D, 2048], F32, kind="ExternalInput")
        io["d_fw2"] = nc.dram_tensor("fw2", [8, 2048, D], F32, kind="ExternalInput")
    if "p3" in phases:
        io["d_Wq"] = nc.dram_tensor("Wq", [D, D], F32, kind="ExternalInput")
        io["d_Wk"] = nc.dram_tensor("Wk", [D, D], F32, kind="ExternalInput")
        io["d_Wv"] = nc.dram_tensor("Wv", [D, D], F32, kind="ExternalInput")
        io["d_Wo"] = nc.dram_tensor("Wo", [D, D], F32, kind="ExternalInput")
        io["d_W1"] = nc.dram_tensor("W1", [D, 4096], F32, kind="ExternalInput")
        io["d_W2"] = nc.dram_tensor("W2", [4096, D], F32, kind="ExternalInput")
        io["d_maskC"] = nc.dram_tensor("maskC", [128, 3584], BF16, kind="ExternalInput")
    if stream_in:
        io["d_sin"] = nc.dram_tensor("stream_in", [128, ND, T], BF16, kind="ExternalInput")
    if stream_out:
        io["d_sout"] = nc.dram_tensor("stream_out", [128, ND, T], BF16,
                                      kind="ExternalOutput")
    import os as _os
    dbg = _os.environ.get("P1_DBG", "")
    if dbg:
        for nm in dbg.split(","):
            io["d_dbg_" + nm] = nc.dram_tensor("dbg_" + nm, [128, 16, 2048], BF16,
                                               kind="ExternalOutput")

    with ExitStack() as ctx:
        p = Prog(nc, ctx)
        d = alloc_common(p, io)
        pb0 = PB(p)
        emit_load_consts(p, pb0, d)
        if stream_in:
            eS = p.edge("sin")
            for dt in range(ND):
                pb0.op("g", lambda E, dt=dt: E.dma_start(d.arA[:, dt, :], d.d_sin[:, dt, :]),
                       inc=eS, inc_k=16)
        fw = [(p.edge("cst"), p.edge("cst").n)]
        if stream_in:
            fw.append((p.edge("sin"), p.edge("sin").n))
        for engx in ("t", "v", "s", "y"):
            pb0.op(engx, lambda E: E.nop(), wait=fw)
        pb0.emit()

        if "p1" in phases:
            with ExitStack() as c1:
                pp = sub_prog(p, c1)
                pp.alloc_std_psum("p1", nf=6, nb=2)
                alloc_p1(pp, d)
                ppbs = []
                import os as _os
                stages = _os.environ.get("P1_STAGES", "ABCDEF")
                dumps = {"B": [("q", d.qC), ("k", d.kD), ("v", d.vE)],
                         "C": [("nssqt", None)],
                         "D": [("yt", d.kD), ("phiq", None)],
                         "E": [("s", d.arB)],
                         "F": []}
                ppbs2 = []
                emit_p1(pp, ppbs2, stages=stages)
                per_stage = {"A": ppbs2[0:1]}
                idx = 1
                for st in "BCDEF":
                    if st in stages and idx < len(ppbs2) + 1:
                        per_stage[st] = ppbs2[idx:idx + 1]
                        idx += 1
                for st in "ABCDEF":
                    if st not in stages:
                        continue
                    for b in per_stage.get(st, []):
                        b.emit()
                    for nm, src_t in dumps.get(st, []):
                        if ("d_dbg_" + nm) in io and src_t is not None:
                            dpb = PB(pp)
                            eD = pp.edge("dbg_" + nm)
                            dst = io["d_dbg_" + nm]
                            if src_t.shape[1] == ND:
                                for dt2 in range(ND):
                                    dpb.op("y", lambda E, dt2=dt2, src_t=src_t, dst=dst:
                                           E.dma_start(dst[:, dt2, :], src_t[:, dt2, :]),
                                           inc=eD, inc_k=16)
                            else:
                                for t2 in range(NT):
                                    dpb.op("y", lambda E, t2=t2, src_t=src_t, dst=dst:
                                           E.dma_start(dst[:, t2, 0:src_t.shape[2]],
                                                       src_t[:, t2, :]),
                                           inc=eD, inc_k=16)
                            dpb.op("y", lambda E: E.nop(), wait=[(eD, eD.n)])
                            dpb.emit()

        if "p1" in phases:
            fpb = PB(p)
            e = p.edge("ln_ap")
            for engx in ("t", "s", "g", "y"):
                fpb.op(engx, lambda E: E.nop(), wait=[(e, e.n)])
            fpb.emit()

        if "p2" in phases:
            with ExitStack() as c2:
                pp = sub_prog(p, c2)
                pp.alloc_wide_psum("p2")
                alloc_p2(pp, d)
                ppbs = []
                emit_p2(pp, ppbs, stages=_os.environ.get("P2_STAGES", "GEL"))
                for b in ppbs:
                    b.emit()

        if "p2" in phases:
            fpb = PB(p)
            e = p.edge("ln_ap")
            for engx in ("t", "s", "g", "y"):
                fpb.op(engx, lambda E: E.nop(), wait=[(e, e.n)])
            fpb.emit()

        if "p3" in phases:
            with ExitStack() as c3:
                pp = sub_prog(p, c3)
                alloc_p3_outer(pp, d)
                with ExitStack() as c3a:
                    pa = sub_prog(pp, c3a)
                    alloc_p3_attn(pa, d)
                    pa.ps3_scb = [pa.psum("ps3_sca", [128, 1024], F32),
                                  pa.psum("ps3_scc", [128, 1024], F32)]
                    pa.ps3_ptb = [pa.psum("ps3_pta", [128, 1024], BF16),
                                  pa.psum("ps3_ptc", [128, 1024], BF16)]
                    pa.ps3_pvb = [pa.psum("ps3_pva", [128, 512], F32),
                                  pa.psum("ps3_pvc", [128, 512], F32)]
                    pa.ps3_sc = pa.ps3_scb[0]
                    pa.ps3_pt = pa.ps3_scb[1]
                    ppbs = []
                    emit_p3_attn(pa, ppbs)
                    for b in ppbs:
                        b.emit()
                with ExitStack() as c3b:
                    pbp = sub_prog(pp, c3b)
                    pbp.alloc_std_psum("p3s1", nf=6, nb=0)
                    alloc_p3_s1(pbp, d)
                    ppbs = []
                    emit_p3_s1(pbp, ppbs)
                    for b in ppbs:
                        b.emit()
                with ExitStack() as c3c:
                    pc = sub_prog(pp, c3c)
                    pc.alloc_wide_psum("p3f")
                    alloc_p3_ffn(pc, d)
                    ppbs = []
                    emit_p3_ffn(pc, ppbs)
                    for b in ppbs:
                        b.emit()
            fpb = PB(p)
            e = p.edge("ln_ap")
            for engx in ("t", "s", "g", "y"):
                fpb.op(engx, lambda E: E.nop(), wait=[(e, e.n)])
            fpb.emit()

        if "p4" in phases:
            with ExitStack() as c4:
                pp = sub_prog(p, c4)
                pp.alloc_wide_psum("p4")
                alloc_p4(pp, d)
                ppbs = []
                emit_p4(pp, ppbs)
                for b in ppbs:
                    b.emit()

        outp = PB(p)
        if stream_out:
            eO = p.edge("sout")
            for dt in range(ND):
                outp.op("y", lambda E, dt=dt: E.dma_start(d.d_sout[:, dt, :], d.arA[:, dt, :]),
                        inc=eO, inc_k=16)
            outp.op("y", lambda E: E.nop(), wait=[(eO, eO.n)])
        outp.emit()
    return nc


FUSED = True


def kernel(**inputs):
    inp = inputs
    ncores = 8
    base = const_inmap(inp)
    wmap_p1 = {
        "W_attn": np.ascontiguousarray(inp["L0_attn_W_attn"]),
        "W_proj": np.ascontiguousarray(inp["L0_attn_W_proj"]),
        "omega": np.ascontiguousarray(inp["L0_attn_omega"]),
    }
    wmap_p2 = {
        "gate_w": np.ascontiguousarray(inp["L0_ffn_gate_w"]),
        "sw1": np.ascontiguousarray(inp["L0_ffn_sw1"]),
        "sw2": np.ascontiguousarray(inp["L0_ffn_sw2"]),
        "rw1": np.ascontiguousarray(inp["L0_ffn_rw1"]),
        "rw2": np.ascontiguousarray(inp["L0_ffn_rw2"]),
    }
    wmap_p3 = {
        "Wq": np.ascontiguousarray(inp["L1_attn_Wq"]),
        "Wk": np.ascontiguousarray(inp["L1_attn_Wk"]),
        "Wv": np.ascontiguousarray(inp["L1_attn_Wv"]),
        "Wo": np.ascontiguousarray(inp["L1_attn_Wo"]),
        "W1": np.ascontiguousarray(inp["L1_attn_W1"]),
        "W2": np.ascontiguousarray(inp["L1_attn_W2"]),
    }
    wmap_p4 = {
        "choice_wT": np.ascontiguousarray(np.asarray(inp["L1_ffn_choice_w"]).T),
        "fw1": np.ascontiguousarray(inp["L1_ffn_w1"]),
        "fw2": np.ascontiguousarray(inp["L1_ffn_w2"]),
    }
    xs = [np.ascontiguousarray(inp["L0_attn_x"][b]) for b in range(ncores)]

    if FUSED:
        nc = build_program(phases=("p1", "p2", "p3", "p4"))
        im = dict(base)
        im.update(wmap_p1); im.update(wmap_p2); im.update(wmap_p3); im.update(wmap_p4)
        in_maps = [dict(im, x_in=xs[b]) for b in range(ncores)]
        res = run_bass_kernel_spmd(nc, in_maps, core_ids=list(range(ncores)))
        return np.stack([res.results[b]["out"] for b in range(ncores)], axis=0)

    nc1 = build_program(phases=("p1",), stream_out=True)
    im = dict(base); im.update(wmap_p1)
    res = run_bass_kernel_spmd(nc1, [dict(im, x_in=xs[b]) for b in range(ncores)],
                               core_ids=list(range(ncores)))
    streams = [res.results[b]["stream_out"] for b in range(ncores)]

    nc2 = build_program(phases=("p2",), stream_in=True, stream_out=True)
    im = dict(base); im.update(wmap_p2)
    res = run_bass_kernel_spmd(nc2, [dict(im, stream_in=streams[b])
                                     for b in range(ncores)],
                               core_ids=list(range(ncores)))
    streams = [res.results[b]["stream_out"] for b in range(ncores)]

    nc3 = build_program(phases=("p3",), stream_in=True, stream_out=True)
    im = dict(base); im.update(wmap_p3)
    res = run_bass_kernel_spmd(nc3, [dict(im, stream_in=streams[b])
                                     for b in range(ncores)],
                               core_ids=list(range(ncores)))
    streams = [res.results[b]["stream_out"] for b in range(ncores)]

    nc4 = build_program(phases=("p4",), stream_in=True)
    im = dict(base); im.update(wmap_p4)
    res = run_bass_kernel_spmd(nc4, [dict(im, stream_in=streams[b])
                                     for b in range(ncores)],
                               core_ids=list(range(ncores)))
    return np.stack([res.results[b]["out"] for b in range(ncores)], axis=0)
